# version 05
# baseline (speedup 1.0000x reference)
import numpy as np
from contextlib import ExitStack

import concourse.bass as bass
import concourse.mybir as mybir
from concourse.bass_utils import run_bass_kernel_spmd

F32 = mybir.dt.float32
BF16 = mybir.dt.bfloat16
AF = mybir.ActivationFunctionType
ALU = mybir.AluOpType
AX = mybir.AxisListType

M32 = float(np.float32(2.0 ** 23 + 2.0 ** 22))
M64 = float(2.0 ** 52 + 2.0 ** 51)
INV_SQRT_DH = 1.0 / float(np.sqrt(np.float32(128.0)))
DEBUG_BUFS = False

D = 2048
S = 8192
OWN = 1024
NC_D = D // 128
NH = 16
HID = 5632
NHC = HID // 128
TT = 512
NTT = S // TT
EPS = 1e-5

ENGS = ("sync", "gpsimd", "tensor", "vector", "scalar")
SEMS = ("sx", "sw", "sp", "sv", "sc", "sg", "so", "sb")


class Ctx:

    def __init__(self, eng, E, sems):
        self.eng, self.E, self.sems = eng, E, sems
        self.n = {k: 0 for k in SEMS}

    def w(self, owner, sem, val):
        if val > 0 and self.eng == owner:
            self.E.wait_ge(self.sems[sem], val)

    def op(self, owner, f, sem=None, inc=1):
        if sem is not None:
            self.n[sem] += inc
        if self.eng == owner:
            ins = f(self.E)
            if sem is not None:
                ins.then_inc(self.sems[sem], inc)
        return self.n[sem] if sem else None

    def dma(self, owner, f, sem):
        self.n[sem] += 16
        if self.eng == owner:
            f(self.E).then_inc(self.sems[sem], 16)
        return self.n[sem]

    def barrier(self):
        self.n["sb"] += 5
        tgt = self.n["sb"]
        if self.eng == "sync":
            self.E.wait_ge(self.sems["sx"], self.n["sx"])
            self.E.wait_ge(self.sems["so"], self.n["so"])
            self.E.nop().then_inc(self.sems["sb"], 1)
        elif self.eng == "gpsimd":
            self.E.wait_ge(self.sems["sw"], self.n["sw"])
            self.E.drain()
            self.E.nop().then_inc(self.sems["sb"], 1)
        else:
            self.E.drain()
            self.E.nop().then_inc(self.sems["sb"], 1)
        self.E.wait_ge(self.sems["sb"], tgt)


def build_nc():
    nc = bass.Bass("TRN2", target_bir_lowering=False,
                   detect_race_conditions=False)

    xT = nc.dram_tensor("xT", [D, S], F32, kind="ExternalInput")
    x_own = nc.dram_tensor("x_own", [OWN, D], F32, kind="ExternalInput")
    wq_c = nc.dram_tensor("wq_c", [D, D], F32, kind="ExternalInput")
    wk_c = nc.dram_tensor("wk_c", [D, D], F32, kind="ExternalInput")
    wv_c = nc.dram_tensor("wv_c", [D, D], F32, kind="ExternalInput")
    wo_c = nc.dram_tensor("wo_c", [D, D], F32, kind="ExternalInput")
    wq_s = nc.dram_tensor("wq_s", [D, D], F32, kind="ExternalInput")
    wk_s = nc.dram_tensor("wk_s", [D, D], F32, kind="ExternalInput")
    wv_s = nc.dram_tensor("wv_s", [D, D], F32, kind="ExternalInput")
    wo_s = nc.dram_tensor("wo_s", [D, D], F32, kind="ExternalInput")
    bq = nc.dram_tensor("bq", [D], F32, kind="ExternalInput")
    bk = nc.dram_tensor("bk", [D], F32, kind="ExternalInput")
    bv = nc.dram_tensor("bv", [D], F32, kind="ExternalInput")
    bo = nc.dram_tensor("bo", [D], F32, kind="ExternalInput")
    w13T = nc.dram_tensor("w13T", [D, 2 * HID], F32, kind="ExternalInput")
    w2T = nc.dram_tensor("w2T", [HID, D], F32, kind="ExternalInput")
    normw = nc.dram_tensor("normw", [HID], F32, kind="ExternalInput")
    cvec = nc.dram_tensor("cvec", [1, D], F32, kind="ExternalInput")
    wmod = nc.dram_tensor("wmod", [D, 6 * D], F32, kind="ExternalInput")

    dbg = "ExternalOutput" if DEBUG_BUFS else "Internal"
    out = nc.dram_tensor("out", [OWN, D], F32, kind="ExternalOutput")
    hbuf = nc.dram_tensor("hbuf", [D, S], BF16, kind=dbg)
    ktbuf = nc.dram_tensor("ktbuf", [D, S], BF16, kind=dbg)
    vbuf = nc.dram_tensor("vbuf", [NH, S, 128], BF16, kind=dbg)
    qtbuf = nc.dram_tensor("qtbuf", [D, OWN], BF16, kind=dbg)
    modbuf = nc.dram_tensor("modbuf", [1, 9 * D], F32, kind=dbg)
    xpbuf = nc.dram_tensor("xpbuf", [OWN, D], F32, kind=dbg)
    gbuf = nc.dram_tensor("gbuf", [OWN, HID], BF16, kind=dbg)
    if DEBUG_BUFS:
        h2dbg = nc.dram_tensor("h2dbg", [OWN, D], F32, kind="ExternalOutput")
        xqdbg = nc.dram_tensor("xqdbg", [OWN, D], BF16, kind="ExternalOutput")
    iqbuf = nc.dram_tensor("iqbuf", [HID, OWN], BF16, kind=dbg)

    with ExitStack() as ctx:
        E_ = ctx.enter_context
        idn = E_(nc.sbuf_tensor("idn", [128, 128], BF16))
        ones_cb = E_(nc.sbuf_tensor("ones_cb", [128, 1], BF16))
        ones_rf = E_(nc.sbuf_tensor("ones_rf", [1, 128], F32))
        ones_rb = E_(nc.sbuf_tensor("ones_rb", [1, 128], BF16))
        w_col = E_(nc.sbuf_tensor("w_col", [128, 1], F32))
        cols = E_(nc.sbuf_tensor("cols", [128, 160], F32))
        silu_cb = E_(nc.sbuf_tensor("silu_cb", [128, 16], BF16))
        rows = E_(nc.sbuf_tensor("rows", [1, 4096], F32))
        brow = E_(nc.sbuf_tensor("brow", [1, 2 * D], BF16))
        AB = E_(nc.sbuf_tensor("AB", [128, 57344], BF16))
        FA = E_(nc.sbuf_tensor("FA", [128, 17920], F32))
        p0 = E_(nc.psum_tensor("p0", [128, 512], F32))
        p1 = E_(nc.psum_tensor("p1", [128, 512], F32))
        p2 = E_(nc.psum_tensor("p2", [128, 512], F32))
        p3 = E_(nc.psum_tensor("p3", [128, 512], F32))
        p4 = E_(nc.psum_tensor("p4", [128, 512], F32))
        pt0 = E_(nc.psum_tensor("pt0", [128, 512], BF16))
        pt1 = E_(nc.psum_tensor("pt1", [128, 512], BF16))

        sems = {k: E_(nc.semaphore(k)) for k in SEMS}
        block = E_(nc.Block())

        PF = [p0, p1]
        PS = [p2, p3]
        PO = [p2, p3, p4]
        PTT = [pt0, pt1]
        OSL = 132

        def emit(c: Ctx):
            n = c.n

            def vd():
                c.op("vector", lambda E: E.drain())

            c.op("gpsimd", lambda E: E.memset(idn[:], 0.0), "sg")
            c.w("gpsimd", "sg", n["sg"])
            c.op("gpsimd", lambda E: E.affine_select(
                out=idn[:], in_=idn[:], compare_op=ALU.not_equal, fill=1.0,
                base=0, pattern=[[-1, 128]], channel_multiplier=1), "sg")
            c.op("gpsimd", lambda E: E.memset(ones_cb[:], 1.0), "sg")
            c.op("gpsimd", lambda E: E.memset(ones_rf[:], 1.0), "sg")
            c.op("gpsimd", lambda E: E.memset(ones_rb[:], 1.0), "sg")
            c.op("gpsimd", lambda E: E.memset(w_col[:], 1.0), "sg")
            g_consts = n["sg"]
            c.w("vector", "sg", g_consts)
            c.op("vector", lambda E: E.tensor_scalar(
                w_col[:], w_col[:], M64, -M64, op0=ALU.add, op1=ALU.add), "sv")
            vd()

            c.dma("gpsimd", lambda E: E.dma_start(
                out=brow[0:1, 0:D], in_=bass.AP(bv, 0, [[D, 1], [1, D]])), "sw")
            c.dma("gpsimd", lambda E: E.dma_start(
                out=brow[0:1, D:2 * D], in_=bass.AP(bo, 0, [[D, 1], [1, D]])), "sw")
            c.dma("sync", lambda E: E.dma_start(
                out=bass.AP(modbuf, 7 * D, [[9 * D, 1], [1, D]]),
                in_=bass.AP(bq, 0, [[D, 1], [1, D]])), "sx")
            c.dma("sync", lambda E: E.dma_start(
                out=bass.AP(modbuf, 8 * D, [[9 * D, 1], [1, D]]),
                in_=bass.AP(bk, 0, [[D, 1], [1, D]])), "sx")
            c.w("sync", "sx", n["sx"])
            c.dma("sync", lambda E: E.dma_start(
                out=cols[:, 32:48],
                in_=bass.AP(modbuf, 7 * D, [[1, 128], [128, 16]]),
                allow_slow_non_contiguous=True), "sx")
            c.dma("sync", lambda E: E.dma_start(
                out=cols[:, 48:64],
                in_=bass.AP(modbuf, 8 * D, [[1, 128], [128, 16]]),
                allow_slow_non_contiguous=True), "sx")

            c.dma("sync", lambda E: E.dma_start(
                out=rows[0:1, 0:D], in_=cvec.ap()), "sx")
            x_c = n["sx"]
            c.w("scalar", "sx", x_c)
            c.op("scalar", lambda E: E.activation(
                out=rows[0:1, D:2 * D], in_=rows[0:1, 0:D], func=AF.Silu), "sc")
            sc_silu = n["sc"]
            c.w("sync", "sc", sc_silu)
            c.dma("sync", lambda E: E.dma_start(
                out=bass.AP(modbuf, 6 * D, [[9 * D, 1], [1, D]]),
                in_=rows[0:1, D:2 * D]), "sx")
            x_silu_st = n["sx"]
            c.w("sync", "sx", x_silu_st)
            c.dma("sync", lambda E: E.dma_start(
                out=cols[:, 80:96], in_=bass.AP(modbuf, 6 * D, [[1, 128], [128, 16]]),
                allow_slow_non_contiguous=True), "sx")
            x_silu_col = n["sx"]


            W_F = 12288
            mod_row = FA[0:1, 0:6 * D]
            mod_evac = {}
            wm_loads = {}
            for nb in range(96):
                par = nb % 2
                c.w("sync", "sp", wm_loads.get(nb - 2, 0))
                wl = c.dma("sync", lambda E, nb=nb, par=par: E.dma_start(
                    out=FA[:, W_F + par * 2048:W_F + par * 2048 + 2048],
                    in_=bass.AP(wmod, nb * 128,
                                [[6 * D, 128], [128 * 6 * D, 16], [1, 128]])),
                    "sx")
                c.w("tensor", "sx", wl)
                c.w("tensor", "sx", x_silu_col)
                c.w("tensor", "sv", mod_evac.get(nb - 2, 0))
                for dc in range(NC_D):
                    c.op("tensor", lambda E, dc=dc, par=par: E.matmul(
                        PF[par][0:1, 0:128], cols[:, 80 + dc:81 + dc],
                        FA[:, W_F + par * 2048 + dc * 128:
                           W_F + par * 2048 + (dc + 1) * 128],
                        start=(dc == 0), stop=(dc == NC_D - 1),
                        skip_group_check=True),
                        "sp" if dc == NC_D - 1 else None)
                wm_loads[nb] = n["sp"]
                mm = n["sp"]
                c.w("vector", "sp", mm)
                c.op("vector", lambda E, nb=nb, par=par: E.tensor_copy(
                    mod_row[0:1, nb * 128:(nb + 1) * 128],
                    PF[par][0:1, 0:128]), "sv")
                mod_evac[nb] = n["sv"]
            v_mod = n["sv"]

            sa_o, ca_o, ga_o, sb_o, cb_o, gb_o = [i * D for i in range(6)]
            tr = FA[0:1, 6 * D:7 * D]
            tr2 = FA[0:1, 7 * D:8 * D]
            c.w("vector", "sv", v_mod)
            vd()
            for g_o in (ga_o, gb_o):
                c.op("vector", lambda E, g_o=g_o: E.tensor_scalar(
                    tr, mod_row[0:1, g_o:g_o + D], 0.0, None, op0=ALU.is_ge))
                vd()
                c.op("vector", lambda E: E.tensor_scalar(
                    tr, tr, 2.0, -1.0, op0=ALU.mult, op1=ALU.add))
                vd()
                c.op("vector", lambda E, g_o=g_o: E.tensor_scalar(
                    tr2, mod_row[0:1, g_o:g_o + D], -1.0, None, op0=ALU.mult))
                vd()
                c.op("vector", lambda E, g_o=g_o: E.tensor_tensor(
                    mod_row[0:1, g_o:g_o + D], mod_row[0:1, g_o:g_o + D],
                    tr2, op=ALU.max))
                vd()
                c.op("vector", lambda E, g_o=g_o: E.tensor_scalar(
                    mod_row[0:1, g_o:g_o + D], mod_row[0:1, g_o:g_o + D],
                    0.25, None, op0=ALU.max))
                vd()
                c.op("vector", lambda E, g_o=g_o: E.tensor_tensor(
                    mod_row[0:1, g_o:g_o + D], mod_row[0:1, g_o:g_o + D],
                    tr, op=ALU.mult))
                vd()
            c.op("vector", lambda E: E.tensor_scalar(
                mod_row[0:1, cb_o:cb_o + D], mod_row[0:1, cb_o:cb_o + D],
                1.0, None, op0=ALU.add))
            vd()
            c.op("vector", lambda E: E.tensor_scalar(
                mod_row[0:1, ca_o:ca_o + D], mod_row[0:1, ca_o:ca_o + D],
                1.0, None, op0=ALU.add), "sv")
            vd()
            v_modt = n["sv"]
            c.w("sync", "sv", v_modt)
            for i, g_o in enumerate((ga_o, cb_o, sb_o, gb_o, ca_o, sa_o)):
                c.dma("sync", lambda E, i=i, g_o=g_o: E.dma_start(
                    out=bass.AP(modbuf, i * D, [[9 * D, 1], [1, D]]),
                    in_=mod_row[0:1, g_o:g_o + D]), "sx")
            x_modst = n["sx"]
            c.w("sync", "sx", x_modst)
            c.dma("sync", lambda E: E.dma_start(
                out=cols[:, 0:16], in_=bass.AP(modbuf, 4 * D, [[1, 128], [128, 16]]),
                allow_slow_non_contiguous=True), "sx")
            c.dma("sync", lambda E: E.dma_start(
                out=cols[:, 16:32], in_=bass.AP(modbuf, 5 * D, [[1, 128], [128, 16]]),
                allow_slow_non_contiguous=True), "sx")
            c.op("vector", lambda E: E.memset(rows[0:1, 3072:3104], 0.0), "sv")
            c.barrier()

            def dequant(codes, scales, stage=32768, dbl=True):
                for dc in range(NC_D):
                    so_ = stage + ((dc % 2) * 4096 if dbl else 0)
                    c.w("gpsimd", "sv", n["sv"] if dc >= (2 if dbl else 1) else 0)
                    c.dma("gpsimd", lambda E, dc=dc, so_=so_: E.dma_start(
                        out=AB[:, so_:so_ + 2048],
                        in_=bass.AP(codes, dc * 128 * D, [[D, 128], [1, D]])),
                        "sw")
                    sw2 = c.dma("gpsimd", lambda E, dc=dc, so_=so_: E.dma_start(
                        out=AB[:, so_ + 2048:so_ + 4096],
                        in_=bass.AP(scales, dc * 128 * D, [[D, 128], [1, D]])),
                        "sw")
                    c.w("vector", "sw", sw2)
                    c.op("vector", lambda E, dc=dc, so_=so_: E.tensor_tensor(
                        AB[:, dc * D:(dc + 1) * D],
                        AB[:, so_:so_ + 2048],
                        AB[:, so_ + 2048:so_ + 4096],
                        op=ALU.mult), "sv")
                return n["sv"]

            A_xbf = 32768
            A_h = 40960
            A_kt = 49152
            F_xT = 0
            F_h1 = 16896
            R_mean, R_ss, R_A, R_B = 0, 512, 1024, 1536

            def h_and_proj(w_codes, w_scales, tiles, kt_is_q, norm_row_off):
                v_deq = dequant(w_codes, w_scales)
                h_dma_done = 0
                kt_dma_done = 0
                prev_proj_done = 0
                hfree = {}
                for idx, t in enumerate(tiles):
                    if idx == 0:
                        c.w("vector", "sv", v_deq)
                        c.w("scalar", "sv", v_deq)
                    c.w("sync", "sv", hfree.get(idx - 2, 0))
                    xl = c.dma("sync", lambda E, t=t, idx=idx: E.dma_start(
                        out=FA[:, F_xT + (idx % 2) * 8192:
                               F_xT + (idx % 2) * 8192 + 8192],
                        in_=bass.AP(xT, t * TT,
                                    [[S, 128], [128 * S, 16], [1, TT]])), "sx")
                    xb = F_xT + (idx % 2) * 8192
                    stats_base = n["sp"]
                    for dc in range(NC_D):
                        par = dc % 2
                        c.w("vector", "sx", xl)
                        c.w("vector", "sp",
                            stats_base + dc - 1 if dc >= 2 else prev_proj_done)
                        c.op("vector", lambda E, dc=dc, par=par, xb=xb: E.tensor_copy(
                            AB[:, A_xbf + par * 1024:A_xbf + par * 1024 + 512],
                            FA[:, xb + dc * 512:xb + (dc + 1) * 512]), "sv")
                        cast_v = n["sv"]
                        c.w("scalar", "sx", xl)
                        c.w("scalar", "sp",
                            stats_base + dc - 1 if dc >= 2 else prev_proj_done)
                        c.op("scalar", lambda E, dc=dc, par=par, xb=xb: E.activation(
                            out=AB[:, A_xbf + par * 1024 + 512:
                                   A_xbf + par * 1024 + 1024],
                            in_=FA[:, xb + dc * 512:xb + (dc + 1) * 512],
                            func=AF.Square), "sc")
                        sq_v = n["sc"]
                        c.w("tensor", "sv", cast_v)
                        c.w("tensor", "sc", sq_v)
                        c.op("tensor", lambda E, dc=dc, par=par: E.matmul(
                            p2[0:1, 0:512], ones_cb[:],
                            AB[:, A_xbf + par * 1024:A_xbf + par * 1024 + 512],
                            start=(dc == 0), stop=(dc == NC_D - 1),
                            skip_group_check=True))
                        c.op("tensor", lambda E, dc=dc, par=par: E.matmul(
                            p3[0:1, 0:512], ones_cb[:],
                            AB[:, A_xbf + par * 1024 + 512:
                               A_xbf + par * 1024 + 1024],
                            start=(dc == 0), stop=(dc == NC_D - 1),
                            skip_group_check=True), "sp")
                    stats_mm = n["sp"]
                    c.w("vector", "sp", stats_mm)
                    c.op("vector", lambda E: E.tensor_scalar(
                        rows[0:1, R_mean:R_mean + 512], p2[0:1, 0:512],
                        1.0 / D, None, op0=ALU.mult))
                    c.op("vector", lambda E: E.tensor_scalar(
                        rows[0:1, R_ss:R_ss + 512], p3[0:1, 0:512],
                        1.0 / D, None, op0=ALU.mult))
                    vd()
                    c.op("vector", lambda E: E.tensor_tensor(
                        rows[0:1, R_A:R_A + 512], rows[0:1, R_mean:R_mean + 512],
                        rows[0:1, R_mean:R_mean + 512], op=ALU.mult))
                    vd()
                    c.op("vector", lambda E: E.tensor_tensor(
                        rows[0:1, R_A:R_A + 512], rows[0:1, R_ss:R_ss + 512],
                        rows[0:1, R_A:R_A + 512], op=ALU.subtract))
                    vd()
                    c.op("vector", lambda E: E.tensor_scalar(
                        rows[0:1, R_A:R_A + 512], rows[0:1, R_A:R_A + 512],
                        EPS, None, op0=ALU.add), "sv")
                    vd()
                    v_var = n["sv"]
                    c.w("scalar", "sv", v_var)
                    c.op("scalar", lambda E: E.activation(
                        out=rows[0:1, R_A:R_A + 512],
                        in_=rows[0:1, R_A:R_A + 512], func=AF.Sqrt), "sc")
                    sc_sq = n["sc"]
                    c.w("vector", "sc", sc_sq)
                    c.op("vector", lambda E: E.reciprocal(
                        rows[0:1, R_A:R_A + 512], rows[0:1, R_A:R_A + 512]))
                    vd()
                    c.op("vector", lambda E: E.scalar_tensor_tensor(
                        rows[0:1, R_B:R_B + 512], rows[0:1, R_mean:R_mean + 512],
                        -1.0, rows[0:1, R_A:R_A + 512],
                        op0=ALU.mult, op1=ALU.mult), "sv")
                    vd()
                    v_ab = n["sv"]
                    c.w("tensor", "sv", v_ab)
                    c.op("tensor", lambda E: E.matmul(
                        p2[:, 0:512], ones_rf[:], rows[0:1, R_A:R_A + 512],
                        start=True, stop=True, skip_group_check=True))
                    c.op("tensor", lambda E: E.matmul(
                        p3[:, 0:512], ones_rf[:], rows[0:1, R_B:R_B + 512],
                        start=True, stop=True, skip_group_check=True), "sp")
                    bc_mm = n["sp"]
                    c.w("vector", "sp", bc_mm)
                    c.w("vector", "sx", h_dma_done)
                    c.w("vector", "sp", prev_proj_done)
                    for dc in range(NC_D):
                        c.op("vector", lambda E, dc=dc, xb=xb: E.tensor_tensor(
                            FA[:, F_h1:F_h1 + 512],
                            FA[:, xb + dc * 512:xb + (dc + 1) * 512],
                            p2[:, 0:512], op=ALU.mult))
                        c.op("vector", lambda E: E.tensor_tensor(
                            FA[:, F_h1:F_h1 + 512], FA[:, F_h1:F_h1 + 512],
                            p3[:, 0:512], op=ALU.add))
                        c.op("vector", lambda E, dc=dc: E.tensor_scalar(
                            AB[:, A_h + dc * 512:A_h + (dc + 1) * 512],
                            FA[:, F_h1:F_h1 + 512],
                            cols[:, dc:dc + 1], cols[:, 16 + dc:17 + dc],
                            op0=ALU.mult, op1=ALU.add),
                            "sv" if dc == NC_D - 1 else None)
                    h_done = n["sv"]
                    hfree[idx] = h_done
                    c.w("sync", "sv", h_done)
                    h_dma_done = c.dma("sync", lambda E, t=t: E.dma_start(
                        out=bass.AP(hbuf, t * TT,
                                    [[S, 128], [128 * S, 16], [1, TT]]),
                        in_=AB[:, A_h:A_h + 8192]), "sx")
                    evac_base = n["sc"]
                    for fc in range(NC_D):
                        par = fc % 2
                        c.w("tensor", "sv", h_done if fc == 0 else 0)
                        c.w("tensor", "sv", v_deq if fc == 0 else 0)
                        c.w("tensor", "sc", evac_base + fc - 1 if fc >= 2 else 0)
                        for dc in range(NC_D):
                            c.op("tensor", lambda E, fc=fc, dc=dc, par=par: E.matmul(
                                PF[par][:, 0:512],
                                AB[:, dc * D + fc * 128:dc * D + fc * 128 + 128],
                                AB[:, A_h + dc * 512:A_h + (dc + 1) * 512],
                                start=(dc == 0), stop=(dc == NC_D - 1),
                                skip_group_check=True),
                                "sp" if dc == NC_D - 1 else None)
                        pj = n["sp"]
                        bias = (cols[:, 32 + fc:33 + fc] if kt_is_q
                                else cols[:, 48 + fc:49 + fc])
                        c.w("scalar", "sp", pj)
                        c.op("scalar", lambda E, fc=fc, par=par, bias=bias: E.activation(
                            out=AB[:, A_kt + fc * 512:A_kt + (fc + 1) * 512],
                            in_=PF[par][:, 0:512], func=AF.Identity,
                            bias=bias), "sc")
                        ev = n["sc"]
                        c.w("vector", "sc", ev)
                        c.op("vector", lambda E, fc=fc: E.tensor_tensor(
                            AB[:, A_xbf:A_xbf + 512],
                            AB[:, A_kt + fc * 512:A_kt + (fc + 1) * 512],
                            AB[:, A_kt + fc * 512:A_kt + (fc + 1) * 512],
                            op=ALU.mult), "sv")
                        v_k2 = n["sv"]
                        c.w("tensor", "sv", v_k2)
                        c.op("tensor", lambda E: E.matmul(
                            p4[0:1, 0:512], ones_cb[:], AB[:, A_xbf:A_xbf + 512],
                            start=True, stop=True, skip_group_check=True), "sp")
                        nm = n["sp"]
                        c.w("vector", "sp", nm)
                        c.op("vector", lambda E: E.reduce_max(
                            rows[0:1, 3120:3121], p4[0:1, 0:512], axis=AX.X))
                        vd()
                        c.op("vector", lambda E, fc=fc: E.tensor_tensor(
                            rows[0:1, norm_row_off + fc:norm_row_off + fc + 1],
                            rows[0:1, norm_row_off + fc:norm_row_off + fc + 1],
                            rows[0:1, 3120:3121], op=ALU.max), "sv")
                    prev_proj_done = n["sp"]
                    c.op("scalar", lambda E: E.drain(), "sc")
                    c.w("sync", "sc", n["sc"])
                    c.w("sync", "sv", n["sv"])
                    if kt_is_q:
                        dst = bass.AP(qtbuf, t * TT,
                                      [[OWN, 128], [128 * OWN, 16], [1, TT]])
                    else:
                        dst = bass.AP(ktbuf, t * TT,
                                      [[S, 128], [128 * S, 16], [1, TT]])
                    kt_dma_done = c.dma("sync", lambda E, dst=dst: E.dma_start(
                        out=dst, in_=AB[:, A_kt:A_kt + 8192]), "sx")
                    c.w("scalar", "sx", kt_dma_done)

            h_and_proj(wq_c, wq_s, [0, 1], True, 3088)
            c.barrier()
            h_and_proj(wk_c, wk_s, list(range(NTT)), False, 3072)
            c.barrier()

            v_deq = dequant(wv_c, wv_s)
            A_vsb = 32768
            bvr = brow[0:1, 0:D]
            prev_mm = 0
            vdma = {}
            for t in range(NTT):
                c.w("sync", "sp", prev_mm)
                xl = c.dma("sync", lambda E, t=t: E.dma_start(
                    out=AB[:, A_h:A_h + 8192],
                    in_=bass.AP(hbuf, t * TT,
                                [[S, 128], [128 * S, 16], [1, TT]])), "sx")
                for ts4 in range(4):
                    vpar = ts4 % 2
                    for fg in range(4):
                        par = (ts4 * 4 + fg) % 2
                        c.w("tensor", "sx", xl)
                        c.w("tensor", "sv", v_deq)
                        c.w("tensor", "sc",
                            n["sc"] if (ts4 * 4 + fg) >= 2 else 0)
                        for dc in range(NC_D):
                            c.op("tensor", lambda E, ts4=ts4, fg=fg, dc=dc, par=par: E.matmul(
                                PF[par][:, 0:512],
                                AB[:, A_h + dc * 512 + ts4 * 128:
                                   A_h + dc * 512 + ts4 * 128 + 128],
                                AB[:, dc * D + fg * 512:dc * D + (fg + 1) * 512],
                                start=(dc == 0), stop=False,
                                skip_group_check=True))
                        c.op("tensor", lambda E, fg=fg, par=par: E.matmul(
                            PF[par][:, 0:512], ones_rb[:],
                            bvr[0:1, fg * 512:(fg + 1) * 512],
                            start=False, stop=True, skip_group_check=True), "sp")
                        vmm = n["sp"]
                        c.w("scalar", "sp", vmm)
                        c.w("scalar", "sx",
                            vdma.get((t, ts4 - 2), vdma.get((t - 1, ts4 + 2), 0))
                            if fg == 0 else 0)
                        c.op("scalar", lambda E, fg=fg, par=par, vpar=vpar: E.activation(
                            out=AB[:, A_vsb + vpar * 2048 + fg * 512:
                                   A_vsb + vpar * 2048 + (fg + 1) * 512],
                            in_=PF[par][:, 0:512], func=AF.Identity), "sc")
                    prev_mm = n["sp"]
                    c.op("scalar", lambda E: E.drain(), "sc")
                    ev = n["sc"]
                    c.w("sync", "sc", ev)
                    vdma[(t, ts4)] = c.dma(
                        "sync", lambda E, t=t, ts4=ts4, vpar=vpar: E.dma_start(
                            out=bass.AP(vbuf, (t * TT + ts4 * 128) * 128,
                                        [[128, 128], [S * 128, 16], [1, 128]]),
                            in_=bass.AP(AB, A_vsb + vpar * 2048,
                                        [[57344, 128], [512, 4], [1, 512]])),
                        "sx")
            c.barrier()

            vd()
            c.op("vector", lambda E: E.tensor_tensor(
                rows[0:1, 3104:3120], rows[0:1, 3088:3104],
                rows[0:1, 3072:3088], op=ALU.mult), "sv")
            vd()
            v_c1 = n["sv"]
            c.w("scalar", "sv", v_c1)
            c.op("scalar", lambda E: E.activation(
                out=rows[0:1, 3104:3120], in_=rows[0:1, 3104:3120],
                func=AF.Sqrt), "sc")
            sc_c = n["sc"]
            c.w("vector", "sc", sc_c)
            c.op("vector", lambda E: E.tensor_scalar(
                rows[0:1, 3104:3120], rows[0:1, 3104:3120],
                -INV_SQRT_DH, None, op0=ALU.mult), "sv")
            vd()
            v_c2 = n["sv"]
            c.w("tensor", "sv", v_c2)
            c.op("tensor", lambda E: E.matmul(
                p4[:, 0:16], ones_rf[:], rows[0:1, 3104:3120],
                start=True, stop=True, skip_group_check=True), "sp")
            pc = n["sp"]
            c.w("vector", "sp", pc)
            c.op("vector", lambda E: E.tensor_copy(cols[:, 64:80], p4[:, 0:16]),
                 "sv")
            v_negc = n["sv"]

            A_vaug = 0
            A_qt = 16512
            A_pt = 18560
            A_kth = 20608
            A_oT = 36992
            A_obf = 53376

            c.w("scalar", "sv", v_negc)
            c.w("vector", "sv", v_negc)
            c.op("vector", lambda E: E.memset(
                bass.AP(AB, A_vaug + 128, [[57344, 128], [129, 128]]), 1.0),
                "sv")
            v_ones = n["sv"]
            c.w("sync", "sv", v_ones)

            head_mm_done = {}
            head_evac_done = 0
            for h in range(NH):
                par = h % 2
                c.w("sync", "sp", head_mm_done.get(h - 2, 0))
                c.dma("sync", lambda E, h=h, par=par: E.dma_start(
                    out=AB[:, A_kth + par * 8192:A_kth + par * 8192 + 8192],
                    in_=bass.AP(ktbuf, h * 128 * S, [[S, 128], [1, S]])), "sx")
                c.dma("sync", lambda E, h=h, par=par: E.dma_start(
                    out=bass.AP(AB, A_vaug + par * 8256,
                                [[57344, 128], [129, 64], [1, 128]]),
                    in_=bass.AP(vbuf, h * S * 128,
                                [[128, 128], [128 * 128, 64], [1, 128]])), "sx")
                lx = c.dma("sync", lambda E, h=h, par=par: E.dma_start(
                    out=AB[:, A_qt + par * 1024:A_qt + par * 1024 + 1024],
                    in_=bass.AP(qtbuf, h * 128 * OWN, [[OWN, 128], [1, OWN]])),
                    "sx")
                exp_base = n["sc"]
                h2k = {}
                for kt in range(64):
                    for qh in range(2):
                        stp = kt * 2 + qh
                        spar = stp % 2
                        c.w("tensor", "sx", lx)
                        c.w("tensor", "sc", exp_base + stp - 1)
                        if stp == 0:
                            c.w("tensor", "sv", head_evac_done)
                        c.op("tensor", lambda E, kt=kt, qh=qh, par=par, spar=spar: E.matmul(
                            PF[spar][:, 0:512],
                            AB[:, A_kth + par * 8192 + kt * 128:
                               A_kth + par * 8192 + (kt + 1) * 128],
                            AB[:, A_qt + par * 1024 + qh * 512:
                               A_qt + par * 1024 + (qh + 1) * 512],
                            start=True, stop=True, skip_group_check=True), "sp")
                        mm1 = n["sp"]
                        c.w("scalar", "sp", mm1)
                        if kt >= 2 and qh == 0:
                            c.w("scalar", "sp", h2k[kt - 2])
                        c.op("scalar", lambda E, kt=kt, qh=qh, h=h, spar=spar: E.activation(
                            out=AB[:, A_pt + (kt % 2) * 1024 + qh * 512:
                                   A_pt + (kt % 2) * 1024 + (qh + 1) * 512],
                            in_=PF[spar][:, 0:512], func=AF.Exp,
                            bias=cols[:, 64 + h:65 + h], scale=INV_SQRT_DH),
                            "sc")
                        ex = n["sc"]
                        c.w("tensor", "sc", ex)
                        for qs in range(4):
                            qs_g = qh * 4 + qs
                            ob, osl = PO[qs_g // 3], (qs_g % 3) * OSL
                            c.op("tensor", lambda E, kt=kt, qs_g=qs_g, ob=ob, osl=osl, h=h: E.matmul(
                                ob[:, osl:osl + 129],
                                AB[:, A_pt + (kt % 2) * 1024 + qs_g * 128:
                                   A_pt + (kt % 2) * 1024 + (qs_g + 1) * 128],
                                bass.AP(AB, A_vaug + (h % 2) * 8256 + kt * 129,
                                        [[57344, 128], [1, 129]]),
                                start=(kt == 0 and qs_g in (0, 3, 6)),
                                stop=(kt == 63),
                                skip_group_check=True),
                                "sp" if qs == 3 else None)
                    h2k[kt] = n["sp"]
                head_mm_done[h] = n["sp"]
                mmall = n["sp"]
                for qs_g in range(8):
                    ob, osl = PO[qs_g // 3], (qs_g % 3) * OSL
                    c.w("vector", "sp", mmall)
                    c.op("vector", lambda E, ob=ob, osl=osl: E.reciprocal(
                        cols[:, 144:145], ob[:, osl + 128:osl + 129]))
                    vd()
                    c.op("vector", lambda E, ob=ob, osl=osl, qs_g=qs_g: E.tensor_scalar(
                        AB[:, A_obf + (qs_g % 2) * 128:
                           A_obf + (qs_g % 2) * 128 + 128],
                        ob[:, osl:osl + 128], cols[:, 144:145], None,
                        op0=ALU.mult), "sv")
                    v_ev = n["sv"]
                    c.w("tensor", "sv", v_ev)
                    c.op("tensor", lambda E, qs_g=qs_g: E.transpose(
                        PTT[qs_g % 2][:, 0:128],
                        AB[:, A_obf + (qs_g % 2) * 128:
                           A_obf + (qs_g % 2) * 128 + 128], idn[:]), "sp")
                    tp = n["sp"]
                    c.w("vector", "sp", tp)
                    c.op("vector", lambda E, h=h, qs_g=qs_g: E.tensor_copy(
                        AB[:, A_oT + h * 1024 + qs_g * 128:
                           A_oT + h * 1024 + (qs_g + 1) * 128],
                        PTT[qs_g % 2][:, 0:128]), "sv")
                head_evac_done = n["sv"]
            c.barrier()

            v_deq = dequant(wo_c, wo_s, stage=32768, dbl=False)
            F_ga = 0
            F_xo = 2048
            F_t1 = 6144
            F_xp = 6656
            c.dma("sync", lambda E: E.dma_start(
                out=rows[0:1, 0:D], in_=bass.AP(modbuf, 0, [[9 * D, 1], [1, D]])),
                "sx")
            x_ga = n["sx"]
            for i in range(4):
                c.w("tensor", "sx", x_ga)
                c.w("tensor", "sv", n["sv"] if i >= 1 else 0)
                c.op("tensor", lambda E, i=i: E.matmul(
                    p4[:, 0:512], ones_rf[:], rows[0:1, i * 512:(i + 1) * 512],
                    start=True, stop=True, skip_group_check=True), "sp")
                bm = n["sp"]
                c.w("vector", "sp", bm)
                c.op("vector", lambda E, i=i: E.tensor_copy(
                    FA[:, F_ga + i * 512:F_ga + (i + 1) * 512], p4[:, 0:512]),
                    "sv")
            vd()
            v_ga = n["sv"]

            bor = brow[0:1, D:2 * D]
            xp_dmas = {}
            evdone = {}
            for qs in range(8):
                c.w("sync", "sv", evdone.get(qs - 2, v_ga))
                xo = c.dma("sync", lambda E, qs=qs: E.dma_start(
                    out=FA[:, F_xo + (qs % 2) * 2048:
                           F_xo + (qs % 2) * 2048 + 2048],
                    in_=bass.AP(x_own, qs * 128 * D, [[D, 128], [1, D]])), "sx")
                for dg in range(4):
                    par = (qs * 4 + dg) % 2
                    c.w("tensor", "sv", v_deq if qs == 0 and dg == 0 else 0)
                    c.w("tensor", "sv",
                        n["sv"] if (qs * 4 + dg) >= 2 else 0)
                    for fc in range(NC_D):
                        c.op("tensor", lambda E, qs=qs, dg=dg, fc=fc, par=par: E.matmul(
                            PF[par][:, 0:512],
                            AB[:, A_oT + fc * 1024 + qs * 128:
                               A_oT + fc * 1024 + (qs + 1) * 128],
                            AB[:, fc * D + dg * 512:fc * D + (dg + 1) * 512],
                            start=(fc == 0), stop=False,
                            skip_group_check=True))
                    c.op("tensor", lambda E, dg=dg, par=par: E.matmul(
                        PF[par][:, 0:512], ones_rb[:],
                        bor[0:1, dg * 512:(dg + 1) * 512],
                        start=False, stop=True, skip_group_check=True), "sp")
                    omm = n["sp"]
                    c.w("vector", "sp", omm)
                    c.w("vector", "sx", xo)
                    c.w("vector", "sx", xp_dmas.get(qs - 2, 0))
                    c.op("vector", lambda E, dg=dg, par=par: E.tensor_tensor(
                        FA[:, F_t1:F_t1 + 512], PF[par][:, 0:512],
                        FA[:, F_ga + dg * 512:F_ga + (dg + 1) * 512],
                        op=ALU.mult))
                    c.op("vector", lambda E, qs=qs, dg=dg: E.tensor_tensor(
                        FA[:, F_xp + (qs % 2) * 2048 + dg * 512:
                           F_xp + (qs % 2) * 2048 + (dg + 1) * 512],
                        FA[:, F_t1:F_t1 + 512],
                        FA[:, F_xo + (qs % 2) * 2048 + dg * 512:
                           F_xo + (qs % 2) * 2048 + (dg + 1) * 512],
                        op=ALU.add), "sv")
                evdone[qs] = n["sv"]
                c.w("sync", "sv", evdone[qs])
                xp_dmas[qs] = c.dma("sync", lambda E, qs=qs: E.dma_start(
                    out=bass.AP(xpbuf, qs * 128 * D, [[D, 128], [1, D]]),
                    in_=FA[:, F_xp + (qs % 2) * 2048:
                           F_xp + (qs % 2) * 2048 + 2048]), "sx")
            c.barrier()

            F_C2 = 0
            F_sb = 2048
            F_xps = 4096
            F_xn = 8192
            F_q1 = 10240
            F_q2 = 12288
            A_xq = 32768
            A_xqT = 40960
            for slot, foff in ((1, F_C2), (2, F_sb)):
                c.dma("sync", lambda E, slot=slot: E.dma_start(
                    out=rows[0:1, 0:D],
                    in_=bass.AP(modbuf, slot * D, [[9 * D, 1], [1, D]])), "sx")
                xg = n["sx"]
                for j in range(4):
                    c.w("tensor", "sx", xg)
                    c.w("tensor", "sv", n["sv"] if j >= 1 or slot == 2 else 0)
                    c.op("tensor", lambda E, j=j: E.matmul(
                        p4[:, 0:512], ones_rf[:],
                        rows[0:1, j * 512:(j + 1) * 512],
                        start=True, stop=True, skip_group_check=True), "sp")
                    bm = n["sp"]
                    c.w("vector", "sp", bm)
                    c.op("vector", lambda E, j=j, foff=foff: E.tensor_copy(
                        FA[:, foff + j * 512:foff + (j + 1) * 512],
                        p4[:, 0:512]), "sv")
                c.w("sync", "sv", n["sv"])
            vd()
            v_bc = n["sv"]

            xn_dones = {}
            for qs in range(8):
                c.w("sync", "sc", xn_dones.get(qs - 2, 0))
                c.w("sync", "sv", v_bc if qs == 0 else 0)
                xl = c.dma("sync", lambda E, qs=qs: E.dma_start(
                    out=FA[:, F_xps + (qs % 2) * 2048:
                           F_xps + (qs % 2) * 2048 + 2048],
                    in_=bass.AP(xpbuf, qs * 128 * D, [[D, 128], [1, D]])), "sx")
                xb = F_xps + (qs % 2) * 2048
                c.w("scalar", "sx", xl)
                c.w("scalar", "sp", n["sp"])
                c.op("scalar", lambda E, xb=xb: E.activation(
                    out=AB[:, A_xq:A_xq + 2048], in_=FA[:, xb:xb + 2048],
                    func=AF.Identity, accum_out=cols[:, 96:97]))
                c.op("scalar", lambda E, xb=xb: E.activation(
                    out=AB[:, A_xq:A_xq + 2048], in_=FA[:, xb:xb + 2048],
                    func=AF.Square, accum_out=cols[:, 104:105]))
                c.op("scalar", lambda E: E.drain(), "sc")
                st = n["sc"]
                c.w("vector", "sc", st)
                c.op("vector", lambda E: E.tensor_scalar(
                    cols[:, 96:97], cols[:, 96:97], 1.0 / D, None,
                    op0=ALU.mult))
                c.op("vector", lambda E: E.tensor_scalar(
                    cols[:, 104:105], cols[:, 104:105], 1.0 / D, None,
                    op0=ALU.mult))
                vd()
                c.op("vector", lambda E: E.tensor_tensor(
                    cols[:, 145:146], cols[:, 96:97], cols[:, 96:97],
                    op=ALU.mult))
                vd()
                c.op("vector", lambda E: E.tensor_tensor(
                    cols[:, 104:105], cols[:, 104:105], cols[:, 145:146],
                    op=ALU.subtract))
                vd()
                c.op("vector", lambda E: E.tensor_scalar(
                    cols[:, 104:105], cols[:, 104:105], EPS, None,
                    op0=ALU.add), "sv")
                vd()
                vv = n["sv"]
                c.w("scalar", "sv", vv)
                c.op("scalar", lambda E: E.activation(
                    out=cols[:, 104:105], in_=cols[:, 104:105],
                    func=AF.Sqrt), "sc")
                sq = n["sc"]
                c.w("vector", "sc", sq)
                c.op("vector", lambda E: E.reciprocal(
                    cols[:, 104:105], cols[:, 104:105]))
                vd()
                c.op("vector", lambda E: E.scalar_tensor_tensor(
                    cols[:, 96:97], cols[:, 96:97], -1.0, cols[:, 104:105],
                    op0=ALU.mult, op1=ALU.mult), "sv")
                vd()
                ab2 = n["sv"]
                c.w("scalar", "sv", ab2)
                c.op("scalar", lambda E, xb=xb: E.activation(
                    out=FA[:, F_xn:F_xn + 2048], in_=FA[:, xb:xb + 2048],
                    func=AF.Identity, bias=cols[:, 96:97],
                    scale=cols[:, 104:105]), "sc")
                xn_done = n["sc"]
                xn_dones[qs] = xn_done
                c.w("vector", "sc", xn_done)
                c.op("vector", lambda E: E.tensor_tensor(
                    FA[:, F_xn:F_xn + 2048], FA[:, F_xn:F_xn + 2048],
                    FA[:, F_C2:F_C2 + 2048], op=ALU.mult))
                c.op("vector", lambda E: E.tensor_tensor(
                    FA[:, F_xn:F_xn + 2048], FA[:, F_xn:F_xn + 2048],
                    FA[:, F_sb:F_sb + 2048], op=ALU.add),
                    "sv" if DEBUG_BUFS else None)
                if DEBUG_BUFS:
                    c.w("sync", "sv", n["sv"])
                    hd_dma = c.dma("sync", lambda E, qs=qs: E.dma_start(
                        out=bass.AP(h2dbg, qs * 128 * D, [[D, 128], [1, D]]),
                        in_=FA[:, F_xn:F_xn + 2048]), "sx")
                    c.w("vector", "sx", hd_dma)
                c.op("vector", lambda E: E.tensor_reduce(
                    cols[:, 145:146], FA[:, F_xn:F_xn + 2048], axis=AX.X,
                    op=ALU.max, apply_absolute_value=True))
                vd()
                c.op("vector", lambda E: E.tensor_scalar(
                    cols[:, 145:146], cols[:, 145:146], 1e-5, None,
                    op0=ALU.max))
                vd()
                c.op("vector", lambda E: E.tensor_scalar(
                    cols[:, 146:147], cols[:, 145:146], 1.0 / 127.0, None,
                    op0=ALU.mult))
                c.op("vector", lambda E: E.reciprocal(
                    cols[:, 145:146], cols[:, 145:146]))
                vd()
                c.op("vector", lambda E: E.tensor_scalar(
                    cols[:, 145:146], cols[:, 145:146], 127.0, None,
                    op0=ALU.mult))
                vd()
                c.op("vector", lambda E: E.tensor_scalar(
                    FA[:, F_xn:F_xn + 2048], FA[:, F_xn:F_xn + 2048],
                    cols[:, 145:146], None, op0=ALU.mult))
                c.op("vector", lambda E: E.tensor_scalar(
                    FA[:, F_q1:F_q1 + 2048], FA[:, F_xn:F_xn + 2048],
                    M32, -M32, op0=ALU.add, op1=ALU.add))
                c.op("vector", lambda E: E.tensor_scalar(
                    FA[:, F_q2:F_q2 + 2048], FA[:, F_xn:F_xn + 2048],
                    M64, -M64, op0=ALU.add, op1=ALU.add))
                c.op("vector", lambda E: E.tensor_tensor(
                    FA[:, F_q2:F_q2 + 2048], FA[:, F_q2:F_q2 + 2048],
                    FA[:, F_q1:F_q1 + 2048], op=ALU.subtract))
                c.op("vector", lambda E: E.scalar_tensor_tensor(
                    FA[:, F_q1:F_q1 + 2048], FA[:, F_q2:F_q2 + 2048],
                    w_col[:], FA[:, F_q1:F_q1 + 2048],
                    op0=ALU.mult, op1=ALU.add))
                c.op("vector", lambda E: E.tensor_scalar(
                    FA[:, F_q1:F_q1 + 2048], FA[:, F_q1:F_q1 + 2048],
                    -128.0, 127.0, op0=ALU.max, op1=ALU.min))
                c.op("vector", lambda E: E.tensor_scalar(
                    AB[:, A_xq:A_xq + 2048], FA[:, F_q1:F_q1 + 2048],
                    cols[:, 146:147], None, op0=ALU.mult), "sv")
                xq_done = n["sv"]
                if DEBUG_BUFS:
                    c.w("sync", "sv", xq_done)
                    xq_dma = c.dma("sync", lambda E, qs=qs: E.dma_start(
                        out=bass.AP(xqdbg, qs * 128 * D, [[D, 128], [1, D]]),
                        in_=AB[:, A_xq:A_xq + 2048]), "sx")
                for dc in range(NC_D):
                    c.w("tensor", "sv", xq_done if dc == 0 else 0)
                    c.w("tensor", "sv",
                        n["sv"] if (qs * 16 + dc) >= 2 else 0)
                    c.op("tensor", lambda E, dc=dc: E.transpose(
                        PTT[dc % 2][:, 0:128],
                        AB[:, A_xq + dc * 128:A_xq + (dc + 1) * 128],
                        idn[:]), "sp")
                    tp = n["sp"]
                    c.w("vector", "sp", tp)
                    c.op("vector", lambda E, dc=dc, qs=qs: E.tensor_copy(
                        AB[:, A_xqT + dc * 1024 + qs * 128:
                           A_xqT + dc * 1024 + (qs + 1) * 128],
                        PTT[dc % 2][:, 0:128]), "sv")
            c.barrier()

            F_nw = 0
            F_gb = 5632
            F_g = 7680
            F_nwrow = 8192
            F_gbrow = 13824
            A_w13 = 0
            A_r = 32768
            A_t = 34816
            A_gsq = 35840

            c.dma("sync", lambda E: E.dma_start(
                out=FA[0:1, F_gbrow:F_gbrow + D],
                in_=bass.AP(modbuf, 3 * D, [[9 * D, 1], [1, D]])), "sx")
            xg1 = n["sx"]
            c.dma("sync", lambda E: E.dma_start(
                out=FA[0:1, F_nwrow:F_nwrow + HID],
                in_=bass.AP(normw, 0, [[HID, 1], [1, HID]])), "sx")
            xg2 = n["sx"]
            for j in range(4):
                c.w("tensor", "sx", xg1)
                c.w("tensor", "sv", n["sv"] if j >= 1 else 0)
                c.op("tensor", lambda E, j=j: E.matmul(
                    p4[:, 0:512], ones_rf[:],
                    FA[0:1, F_gbrow + j * 512:F_gbrow + (j + 1) * 512],
                    start=True, stop=True, skip_group_check=True), "sp")
                bm = n["sp"]
                c.w("vector", "sp", bm)
                c.op("vector", lambda E, j=j: E.tensor_copy(
                    FA[:, F_gb + j * 512:F_gb + (j + 1) * 512], p4[:, 0:512]),
                    "sv")
            for j in range(11):
                c.w("tensor", "sx", xg2)
                c.w("tensor", "sv", n["sv"])
                c.op("tensor", lambda E, j=j: E.matmul(
                    p4[:, 0:512], ones_rf[:],
                    FA[0:1, F_nwrow + j * 512:F_nwrow + (j + 1) * 512],
                    start=True, stop=True, skip_group_check=True), "sp")
                bm = n["sp"]
                c.w("vector", "sp", bm)
                c.op("vector", lambda E, j=j: E.tensor_copy(
                    FA[:, F_nw + j * 512:F_nw + (j + 1) * 512], p4[:, 0:512]),
                    "sv")
            vd()
            c.op("vector", lambda E: E.memset(cols[:, 112:128], 0.0), "sv")
            vd()
            v_z2 = n["sv"]

            w13_loads = {}
            g_dones = {}
            t_dmas = {}
            for hp in range(11):
                wpar = hp % 2
                c.w("gpsimd", "sp", w13_loads.get(hp - 2, (0, 0))[1])
                c.dma("gpsimd", lambda E, hp=hp, wpar=wpar: E.dma_start(
                    out=AB[:, A_w13 + wpar * 16384:A_w13 + wpar * 16384 + 8192],
                    in_=bass.AP(w13T, hp * 512,
                                [[2 * HID, 128], [128 * 2 * HID, 16], [1, 512]])),
                    "sw")
                wl = c.dma("gpsimd", lambda E, hp=hp, wpar=wpar: E.dma_start(
                    out=AB[:, A_w13 + wpar * 16384 + 8192:
                           A_w13 + wpar * 16384 + 16384],
                    in_=bass.AP(w13T, HID + hp * 512,
                                [[2 * HID, 128], [128 * 2 * HID, 16], [1, 512]])),
                    "sw")
                for qs in range(8):
                    it = hp * 8 + qs
                    c.w("tensor", "sw", wl)
                    c.w("tensor", "sv", v_z2 if it == 0 else 0)
                    c.w("tensor", "sv", n["sv"] if it >= 2 else 0)
                    c.w("tensor", "sc", n["sc"] if it >= 2 else 0)
                    for dc in range(NC_D):
                        c.op("tensor", lambda E, qs=qs, dc=dc, wpar=wpar: E.matmul(
                            PF[qs % 2][:, 0:512],
                            AB[:, A_xqT + dc * 1024 + qs * 128:
                               A_xqT + dc * 1024 + (qs + 1) * 128],
                            AB[:, A_w13 + wpar * 16384 + dc * 512:
                               A_w13 + wpar * 16384 + (dc + 1) * 512],
                            start=(dc == 0), stop=(dc == NC_D - 1),
                            skip_group_check=True))
                        c.op("tensor", lambda E, qs=qs, dc=dc, wpar=wpar: E.matmul(
                            PS[qs % 2][:, 0:512],
                            AB[:, A_xqT + dc * 1024 + qs * 128:
                               A_xqT + dc * 1024 + (qs + 1) * 128],
                            AB[:, A_w13 + wpar * 16384 + 8192 + dc * 512:
                               A_w13 + wpar * 16384 + 8192 + (dc + 1) * 512],
                            start=(dc == 0), stop=(dc == NC_D - 1),
                            skip_group_check=True),
                            "sp" if dc == NC_D - 1 else None)
                    x13 = n["sp"]
                    c.w("scalar", "sp", x13)
                    c.w("scalar", "sv", g_dones.get(it - 2, 0))
                    c.op("scalar", lambda E, qs=qs: E.activation(
                        out=AB[:, A_r + (qs % 2) * 1024:
                               A_r + (qs % 2) * 1024 + 512],
                        in_=PF[qs % 2][:, 0:512], func=AF.Relu))
                    c.op("scalar", lambda E: E.drain())
                    c.op("scalar", lambda E, qs=qs: E.activation(
                        out=AB[:, A_r + (qs % 2) * 1024 + 512:
                               A_r + (qs % 2) * 1024 + 1024],
                        in_=AB[:, A_r + (qs % 2) * 1024:
                               A_r + (qs % 2) * 1024 + 512],
                        func=AF.Square), "sc")
                    r2 = n["sc"]
                    c.w("vector", "sc", r2)
                    c.op("vector", lambda E, qs=qs: E.tensor_tensor(
                        FA[:, F_g:F_g + 512],
                        AB[:, A_r + (qs % 2) * 1024 + 512:
                           A_r + (qs % 2) * 1024 + 1024],
                        PS[qs % 2][:, 0:512], op=ALU.mult), "sv")
                    g_done = n["sv"]
                    g_dones[it] = g_done
                    c.w("scalar", "sv", g_done)
                    c.op("scalar", lambda E: E.activation(
                        out=AB[:, A_gsq:A_gsq + 512], in_=FA[:, F_g:F_g + 512],
                        func=AF.Square, accum_out=cols[:, 147:148]))
                    c.op("scalar", lambda E: E.drain(), "sc")
                    gss = n["sc"]
                    c.w("vector", "sc", gss)
                    c.op("vector", lambda E, qs=qs: E.tensor_tensor(
                        cols[:, 112 + qs:113 + qs], cols[:, 112 + qs:113 + qs],
                        cols[:, 147:148], op=ALU.add))
                    c.w("vector", "sx", t_dmas.get(it - 2, 0))
                    c.op("vector", lambda E, hp=hp, qs=qs: E.tensor_tensor(
                        AB[:, A_t + (qs % 2) * 512:A_t + (qs % 2) * 512 + 512],
                        FA[:, F_g:F_g + 512],
                        FA[:, F_nw + hp * 512:F_nw + (hp + 1) * 512],
                        op=ALU.mult), "sv")
                    t_done = n["sv"]
                    c.op("vector", lambda E, qs=qs: E.tensor_reduce(
                        cols[:, 148:149],
                        AB[:, A_t + (qs % 2) * 512:A_t + (qs % 2) * 512 + 512],
                        axis=AX.X, op=ALU.max, apply_absolute_value=True))
                    vd()
                    c.op("vector", lambda E, qs=qs: E.tensor_tensor(
                        cols[:, 120 + qs:121 + qs], cols[:, 120 + qs:121 + qs],
                        cols[:, 148:149], op=ALU.max), "sv")
                    c.w("sync", "sv", t_done)
                    t_dmas[it] = c.dma("sync", lambda E, hp=hp, qs=qs: E.dma_start(
                        out=bass.AP(gbuf, qs * 128 * HID + hp * 512,
                                    [[HID, 128], [1, 512]]),
                        in_=AB[:, A_t + (qs % 2) * 512:
                               A_t + (qs % 2) * 512 + 512]), "sx")
                w13_loads[hp] = (wl, n["sp"])
            c.barrier()

            vd()
            c.op("vector", lambda E: E.tensor_scalar(
                cols[:, 112:120], cols[:, 112:120], 1.0 / HID, EPS,
                op0=ALU.mult, op1=ALU.add), "sv")
            vd()
            vq = n["sv"]
            c.w("scalar", "sv", vq)
            c.op("scalar", lambda E: E.activation(
                out=cols[:, 112:120], in_=cols[:, 112:120], func=AF.Sqrt), "sc")
            sq2 = n["sc"]
            c.w("vector", "sc", sq2)
            c.op("vector", lambda E: E.reciprocal(
                cols[:, 112:120], cols[:, 112:120]))
            vd()
            c.op("vector", lambda E: E.tensor_tensor(
                cols[:, 120:128], cols[:, 112:120], cols[:, 120:128],
                op=ALU.mult))
            vd()
            c.op("vector", lambda E: E.tensor_scalar(
                cols[:, 120:128], cols[:, 120:128], 1e-5, None, op0=ALU.max))
            vd()
            c.op("vector", lambda E: E.tensor_scalar(
                cols[:, 136:144], cols[:, 120:128], 1.0 / 127.0, None,
                op0=ALU.mult))
            c.op("vector", lambda E: E.reciprocal(
                cols[:, 120:128], cols[:, 120:128]))
            vd()
            c.op("vector", lambda E: E.tensor_scalar(
                cols[:, 120:128], cols[:, 120:128], 127.0, None,
                op0=ALU.mult))
            vd()
            c.op("vector", lambda E: E.tensor_tensor(
                cols[:, 128:136], cols[:, 112:120], cols[:, 120:128],
                op=ALU.mult), "sv")
            vd()
            v_q2c = n["sv"]

            A_iqT = 16384
            A_tin = 22528
            A_iq = 23552
            A_w2 = 24576
            F_y = 0
            F_xps2 = 2048
            F_o1 = 4096

            iq_store = {"v": 0}

            def quant2_and_iqT(qs):
                c.w("vector", "sx", iq_store["v"])
                for hp in range(11):
                    c.w("sync", "sv", n["sv"] if hp >= 2 else v_q2c)
                    tl = c.dma("sync", lambda E, qs=qs, hp=hp: E.dma_start(
                        out=AB[:, A_tin + (hp % 2) * 512:
                               A_tin + (hp % 2) * 512 + 512],
                        in_=bass.AP(gbuf, qs * 128 * HID + hp * 512,
                                    [[HID, 128], [1, 512]])), "sx")
                    c.w("vector", "sx", tl)
                    c.op("vector", lambda E, qs=qs, hp=hp: E.tensor_scalar(
                        FA[:, F_y:F_y + 512],
                        AB[:, A_tin + (hp % 2) * 512:
                           A_tin + (hp % 2) * 512 + 512],
                        cols[:, 128 + qs:129 + qs], None, op0=ALU.mult))
                    c.op("vector", lambda E: E.tensor_scalar(
                        FA[:, F_y + 512:F_y + 1024], FA[:, F_y:F_y + 512],
                        M32, -M32, op0=ALU.add, op1=ALU.add))
                    c.op("vector", lambda E: E.tensor_scalar(
                        FA[:, F_y + 1024:F_y + 1536], FA[:, F_y:F_y + 512],
                        M64, -M64, op0=ALU.add, op1=ALU.add))
                    c.op("vector", lambda E: E.tensor_tensor(
                        FA[:, F_y + 1024:F_y + 1536],
                        FA[:, F_y + 1024:F_y + 1536],
                        FA[:, F_y + 512:F_y + 1024], op=ALU.subtract))
                    c.op("vector", lambda E: E.scalar_tensor_tensor(
                        FA[:, F_y + 512:F_y + 1024],
                        FA[:, F_y + 1024:F_y + 1536], w_col[:],
                        FA[:, F_y + 512:F_y + 1024],
                        op0=ALU.mult, op1=ALU.add))
                    c.op("vector", lambda E: E.tensor_scalar(
                        FA[:, F_y + 512:F_y + 1024], FA[:, F_y + 512:F_y + 1024],
                        -128.0, 127.0, op0=ALU.max, op1=ALU.min))
                    c.op("vector", lambda E, qs=qs, hp=hp: E.tensor_scalar(
                        AB[:, A_iq + (hp % 2) * 512:A_iq + (hp % 2) * 512 + 512],
                        FA[:, F_y + 512:F_y + 1024],
                        cols[:, 136 + qs:137 + qs], None, op0=ALU.mult), "sv")
                    iq_done = n["sv"]
                    for j in range(4):
                        hc = hp * 4 + j
                        c.w("tensor", "sv", iq_done if j == 0 else 0)
                        c.w("tensor", "sv", n["sv"] if hc >= 2 else 0)
                        c.op("tensor", lambda E, hp=hp, j=j: E.transpose(
                            PTT[j % 2][:, 0:128],
                            AB[:, A_iq + (hp % 2) * 512 + j * 128:
                               A_iq + (hp % 2) * 512 + (j + 1) * 128],
                            idn[:]), "sp")
                        tp = n["sp"]
                        c.w("vector", "sp", tp)
                        c.op("vector", lambda E, hc=hc: E.tensor_copy(
                            AB[:, A_iqT + hc * 128:A_iqT + (hc + 1) * 128],
                            PTT[hc % 2][:, 0:128]), "sv")
                return n["sv"]

            for dg in range(4):
                c.w("gpsimd", "sp", n["sp"] if dg > 0 else 0)
                wl = c.dma("gpsimd", lambda E, dg=dg: E.dma_start(
                    out=AB[:, A_w2:A_w2 + 22528],
                    in_=bass.AP(w2T, dg * 512,
                                [[D, 128], [128 * D, 44], [1, 512]])), "sw")
                for qs in range(8):
                    par = qs % 2
                    if dg == 0:
                        iqT_done = quant2_and_iqT(qs)
                        c.w("sync", "sv", iqT_done)
                        iq_store["v"] = c.dma("sync", lambda E, qs=qs: E.dma_start(
                            out=bass.AP(iqbuf, qs * 128,
                                        [[OWN, 128], [128 * OWN, 44], [1, 128]]),
                            in_=AB[:, A_iqT:A_iqT + 5632]), "sx")
                        c.w("tensor", "sv", iqT_done)
                    else:
                        c.w("sync", "sp", n["sp"])
                        c.dma("sync", lambda E, qs=qs: E.dma_start(
                            out=AB[:, A_iqT:A_iqT + 5632],
                            in_=bass.AP(iqbuf, qs * 128,
                                        [[OWN, 128], [128 * OWN, 44], [1, 128]])),
                            "sx")
                        c.w("tensor", "sx", n["sx"])
                    c.w("tensor", "sw", wl)
                    c.w("tensor", "sv",
                        n["sv"] if (dg * 8 + qs) >= 2 or dg > 0 else 0)
                    for hc in range(NHC):
                        c.op("tensor", lambda E, hc=hc, par=par: E.matmul(
                            PF[par][:, 0:512],
                            AB[:, A_iqT + hc * 128:A_iqT + (hc + 1) * 128],
                            AB[:, A_w2 + hc * 512:A_w2 + (hc + 1) * 512],
                            start=(hc == 0), stop=(hc == NHC - 1),
                            skip_group_check=True),
                            "sp" if hc == NHC - 1 else None)
                    w2mm = n["sp"]
                    xl = c.dma("sync", lambda E, qs=qs, dg=dg: E.dma_start(
                        out=FA[:, F_xps2 + (qs % 2) * 512:
                               F_xps2 + (qs % 2) * 512 + 512],
                        in_=bass.AP(xpbuf, qs * 128 * D + dg * 512,
                                    [[D, 128], [1, 512]])), "sx")
                    c.w("vector", "sp", w2mm)
                    c.w("vector", "sx", xl)
                    c.w("vector", "so", max(0, n["so"] - 32))
                    c.op("vector", lambda E, dg=dg, par=par, qs=qs: E.tensor_tensor(
                        FA[:, F_o1 + (qs % 2) * 512:F_o1 + (qs % 2) * 512 + 512],
                        PF[par][:, 0:512],
                        FA[:, F_gb + dg * 512:F_gb + (dg + 1) * 512],
                        op=ALU.mult))
                    c.op("vector", lambda E, qs=qs: E.tensor_tensor(
                        FA[:, F_o1 + (qs % 2) * 512:F_o1 + (qs % 2) * 512 + 512],
                        FA[:, F_o1 + (qs % 2) * 512:F_o1 + (qs % 2) * 512 + 512],
                        FA[:, F_xps2 + (qs % 2) * 512:
                           F_xps2 + (qs % 2) * 512 + 512],
                        op=ALU.add), "sv")
                    o_done = n["sv"]
                    c.w("sync", "sv", o_done)
                    c.dma("sync", lambda E, qs=qs, dg=dg: E.dma_start(
                        out=bass.AP(out, qs * 128 * D + dg * 512,
                                    [[D, 128], [1, 512]]),
                        in_=FA[:, F_o1 + (qs % 2) * 512:
                               F_o1 + (qs % 2) * 512 + 512]), "so")

            for e in ENGS:
                c.w(e, "so", n["so"])

        @block.sync
        def _(E):
            emit(Ctx("sync", E, sems))

        @block.gpsimd
        def _(E):
            emit(Ctx("gpsimd", E, sems))

        @block.tensor
        def _(E):
            emit(Ctx("tensor", E, sems))

        @block.vector
        def _(E):
            emit(Ctx("vector", E, sems))

        @block.scalar
        def _(E):
            emit(Ctx("scalar", E, sems))

    return nc


_NC_CACHE = None
_LAST_RES = None


def kernel(L0_attn_x, L0_attn_wq_codes, L0_attn_wq_scales, L0_attn_bq,
           L0_attn_wk_codes, L0_attn_wk_scales, L0_attn_bk,
           L0_attn_wv_codes, L0_attn_wv_scales, L0_attn_bv,
           L0_attn_wo_codes, L0_attn_wo_scales, L0_attn_bo,
           L0_ffn_w13, L0_ffn_w2, L0_ffn_norm_w, L0_c, L0_wmod):
    global _NC_CACHE
    f = np.float32
    x = np.ascontiguousarray(np.asarray(L0_attn_x, f).reshape(S, D))
    rep = lambda s: np.ascontiguousarray(np.repeat(np.asarray(s, f), 32, axis=0))
    shared = {
        "wq_c": np.ascontiguousarray(np.asarray(L0_attn_wq_codes, f)),
        "wk_c": np.ascontiguousarray(np.asarray(L0_attn_wk_codes, f)),
        "wv_c": np.ascontiguousarray(np.asarray(L0_attn_wv_codes, f)),
        "wo_c": np.ascontiguousarray(np.asarray(L0_attn_wo_codes, f)),
        "wq_s": rep(L0_attn_wq_scales),
        "wk_s": rep(L0_attn_wk_scales),
        "wv_s": rep(L0_attn_wv_scales),
        "wo_s": rep(L0_attn_wo_scales),
        "bq": np.ascontiguousarray(np.asarray(L0_attn_bq, f)),
        "bk": np.ascontiguousarray(np.asarray(L0_attn_bk, f)),
        "bv": np.ascontiguousarray(np.asarray(L0_attn_bv, f)),
        "bo": np.ascontiguousarray(np.asarray(L0_attn_bo, f)),
        "w13T": np.ascontiguousarray(np.asarray(L0_ffn_w13, f).T),
        "w2T": np.ascontiguousarray(np.asarray(L0_ffn_w2, f).T),
        "normw": np.ascontiguousarray(np.asarray(L0_ffn_norm_w, f)),
        "cvec": np.ascontiguousarray(np.asarray(L0_c, f).reshape(1, D)),
        "wmod": np.ascontiguousarray(np.asarray(L0_wmod, f)),
    }
    in_maps = []
    for cix in range(8):
        order = np.r_[cix * OWN:S, 0:cix * OWN]
        m = dict(shared)
        m["xT"] = np.ascontiguousarray(x[order].T)
        m["x_own"] = np.ascontiguousarray(x[cix * OWN:(cix + 1) * OWN])
        in_maps.append(m)

    if _NC_CACHE is None:
        _NC_CACHE = build_nc()
    res = run_bass_kernel_spmd(_NC_CACHE, in_maps, core_ids=list(range(8)))
    global _LAST_RES
    _LAST_RES = res
    o = np.empty((S, D), f)
    for cix in range(8):
        o[cix * OWN:(cix + 1) * OWN] = res.results[cix]["out"]
    return o.reshape(1, S, D)


# version 08
# speedup vs baseline: 1.3970x; 1.3970x over previous
import numpy as np
from contextlib import ExitStack

import concourse.bass as bass
import concourse.mybir as mybir
from concourse.bass_utils import run_bass_kernel_spmd

F32 = mybir.dt.float32
BF16 = mybir.dt.bfloat16
AF = mybir.ActivationFunctionType
ALU = mybir.AluOpType
AX = mybir.AxisListType

M32 = float(np.float32(2.0 ** 23 + 2.0 ** 22))
M64 = float(2.0 ** 52 + 2.0 ** 51)
INV_SQRT_DH = 1.0 / float(np.sqrt(np.float32(128.0)))
DEBUG_BUFS = False

D = 2048
S = 8192
OWN = 1024
NC_D = D // 128
NH = 16
HID = 5632
NHC = HID // 128
TT = 512
NTT = S // TT
EPS = 1e-5

ENGS = ("sync", "gpsimd", "tensor", "vector", "scalar")
SEMS = ("sx", "sw", "sp", "sv", "sc", "sg", "so", "sb")


class Ctx:

    def __init__(self, eng, E, sems):
        self.eng, self.E, self.sems = eng, E, sems
        self.n = {k: 0 for k in SEMS}

    def w(self, owner, sem, val):
        if val > 0 and self.eng == owner:
            self.E.wait_ge(self.sems[sem], val)

    def op(self, owner, f, sem=None, inc=1):
        if sem is not None:
            self.n[sem] += inc
        if self.eng == owner:
            ins = f(self.E)
            if sem is not None:
                ins.then_inc(self.sems[sem], inc)
        return self.n[sem] if sem else None

    def dma(self, owner, f, sem):
        self.n[sem] += 16
        if self.eng == owner:
            f(self.E).then_inc(self.sems[sem], 16)
        return self.n[sem]

    def barrier(self):
        self.n["sb"] += 5
        tgt = self.n["sb"]
        if self.eng == "sync":
            self.E.wait_ge(self.sems["sx"], self.n["sx"])
            self.E.wait_ge(self.sems["so"], self.n["so"])
            self.E.nop().then_inc(self.sems["sb"], 1)
        elif self.eng == "gpsimd":
            self.E.wait_ge(self.sems["sw"], self.n["sw"])
            self.E.drain()
            self.E.nop().then_inc(self.sems["sb"], 1)
        else:
            self.E.drain()
            self.E.nop().then_inc(self.sems["sb"], 1)
        self.E.wait_ge(self.sems["sb"], tgt)


def build_nc():
    nc = bass.Bass("TRN2", target_bir_lowering=False,
                   detect_race_conditions=False)

    xT = nc.dram_tensor("xT", [D, S], F32, kind="ExternalInput")
    x_own = nc.dram_tensor("x_own", [OWN, D], F32, kind="ExternalInput")
    wq_c = nc.dram_tensor("wq_c", [D, D], F32, kind="ExternalInput")
    wk_c = nc.dram_tensor("wk_c", [D, D], F32, kind="ExternalInput")
    wv_c = nc.dram_tensor("wv_c", [D, D], F32, kind="ExternalInput")
    wo_c = nc.dram_tensor("wo_c", [D, D], F32, kind="ExternalInput")
    wq_s = nc.dram_tensor("wq_s", [D, D], F32, kind="ExternalInput")
    wk_s = nc.dram_tensor("wk_s", [D, D], F32, kind="ExternalInput")
    wv_s = nc.dram_tensor("wv_s", [D, D], F32, kind="ExternalInput")
    wo_s = nc.dram_tensor("wo_s", [D, D], F32, kind="ExternalInput")
    bq = nc.dram_tensor("bq", [D], F32, kind="ExternalInput")
    bk = nc.dram_tensor("bk", [D], F32, kind="ExternalInput")
    bv = nc.dram_tensor("bv", [D], F32, kind="ExternalInput")
    bo = nc.dram_tensor("bo", [D], F32, kind="ExternalInput")
    w13T = nc.dram_tensor("w13T", [D, 2 * HID], F32, kind="ExternalInput")
    w2T = nc.dram_tensor("w2T", [HID, D], F32, kind="ExternalInput")
    normw = nc.dram_tensor("normw", [HID], F32, kind="ExternalInput")
    cvec = nc.dram_tensor("cvec", [1, D], F32, kind="ExternalInput")
    wmod = nc.dram_tensor("wmod", [D, 6 * D], F32, kind="ExternalInput")

    dbg = "ExternalOutput" if DEBUG_BUFS else "Internal"
    out = nc.dram_tensor("out", [OWN, D], F32, kind="ExternalOutput")
    hbuf = nc.dram_tensor("hbuf", [D, S], BF16, kind=dbg)
    ktbuf = nc.dram_tensor("ktbuf", [D, S], BF16, kind=dbg)
    vbuf = nc.dram_tensor("vbuf", [NH, S, 128], BF16, kind=dbg)
    qtbuf = nc.dram_tensor("qtbuf", [D, OWN], BF16, kind=dbg)
    modbuf = nc.dram_tensor("modbuf", [1, 9 * D], F32, kind=dbg)
    xpbuf = nc.dram_tensor("xpbuf", [OWN, D], F32, kind=dbg)
    gbuf = nc.dram_tensor("gbuf", [OWN, HID], BF16, kind=dbg)
    if DEBUG_BUFS:
        h2dbg = nc.dram_tensor("h2dbg", [OWN, D], F32, kind="ExternalOutput")
        xqdbg = nc.dram_tensor("xqdbg", [OWN, D], BF16, kind="ExternalOutput")
    iqbuf = nc.dram_tensor("iqbuf", [HID, OWN], BF16, kind=dbg)

    with ExitStack() as ctx:
        E_ = ctx.enter_context
        idn = E_(nc.sbuf_tensor("idn", [128, 128], BF16))
        ones_cb = E_(nc.sbuf_tensor("ones_cb", [128, 1], BF16))
        ones_rf = E_(nc.sbuf_tensor("ones_rf", [1, 128], F32))
        ones_rb = E_(nc.sbuf_tensor("ones_rb", [1, 128], BF16))
        w_col = E_(nc.sbuf_tensor("w_col", [128, 1], F32))
        cols = E_(nc.sbuf_tensor("cols", [128, 160], F32))
        silu_cb = E_(nc.sbuf_tensor("silu_cb", [128, 16], BF16))
        rows = E_(nc.sbuf_tensor("rows", [1, 4096], F32))
        brow = E_(nc.sbuf_tensor("brow", [1, 2 * D], BF16))
        AB = E_(nc.sbuf_tensor("AB", [128, 57344], BF16))
        FA = E_(nc.sbuf_tensor("FA", [128, 17920], F32))
        p01 = E_(nc.psum_tensor("p01", [128, 1024], F32))
        p23 = E_(nc.psum_tensor("p23", [128, 1024], F32))
        pOb = E_(nc.psum_tensor("pOb", [128, 1536], F32))
        pXb = E_(nc.psum_tensor("pXb", [128, 512], BF16))
        p0 = p01[:, 0:512]
        p1 = p01[:, 512:1024]
        p2 = p23[:, 0:512]
        p3 = p23[:, 512:1024]
        p4 = pOb[:, 0:512]
        pt0 = pXb[:, 0:128]
        pt1 = pXb[:, 128:256]

        sems = {k: E_(nc.semaphore(k)) for k in SEMS}
        block = E_(nc.Block())

        PF = [p0, p1]
        PS = [p2, p3]
        PSC = [p01, p23]
        PTT = [pt0, pt1]

        def emit(c: Ctx):
            n = c.n

            def vd():
                c.op("vector", lambda E: E.drain())

            c.op("gpsimd", lambda E: E.memset(idn[:], 0.0), "sg")
            c.w("gpsimd", "sg", n["sg"])
            c.op("gpsimd", lambda E: E.affine_select(
                out=idn[:], in_=idn[:], compare_op=ALU.not_equal, fill=1.0,
                base=0, pattern=[[-1, 128]], channel_multiplier=1), "sg")
            c.op("gpsimd", lambda E: E.memset(ones_cb[:], 1.0), "sg")
            c.op("gpsimd", lambda E: E.memset(ones_rf[:], 1.0), "sg")
            c.op("gpsimd", lambda E: E.memset(ones_rb[:], 1.0), "sg")
            c.op("gpsimd", lambda E: E.memset(w_col[:], 1.0), "sg")
            g_consts = n["sg"]
            c.w("vector", "sg", g_consts)
            c.op("vector", lambda E: E.tensor_scalar(
                w_col[:], w_col[:], M64, -M64, op0=ALU.add, op1=ALU.add), "sv")
            vd()

            c.dma("gpsimd", lambda E: E.dma_start(
                out=brow[0:1, 0:D], in_=bass.AP(bv, 0, [[D, 1], [1, D]])), "sw")
            c.dma("gpsimd", lambda E: E.dma_start(
                out=brow[0:1, D:2 * D], in_=bass.AP(bo, 0, [[D, 1], [1, D]])), "sw")
            c.dma("sync", lambda E: E.dma_start(
                out=bass.AP(modbuf, 7 * D, [[9 * D, 1], [1, D]]),
                in_=bass.AP(bq, 0, [[D, 1], [1, D]])), "sx")
            c.dma("sync", lambda E: E.dma_start(
                out=bass.AP(modbuf, 8 * D, [[9 * D, 1], [1, D]]),
                in_=bass.AP(bk, 0, [[D, 1], [1, D]])), "sx")
            c.w("sync", "sx", n["sx"])
            c.dma("sync", lambda E: E.dma_start(
                out=cols[:, 32:48],
                in_=bass.AP(modbuf, 7 * D, [[1, 128], [128, 16]]),
                allow_slow_non_contiguous=True), "sx")
            c.dma("sync", lambda E: E.dma_start(
                out=cols[:, 48:64],
                in_=bass.AP(modbuf, 8 * D, [[1, 128], [128, 16]]),
                allow_slow_non_contiguous=True), "sx")

            c.dma("sync", lambda E: E.dma_start(
                out=rows[0:1, 0:D], in_=cvec.ap()), "sx")
            x_c = n["sx"]
            c.w("scalar", "sx", x_c)
            c.op("scalar", lambda E: E.activation(
                out=rows[0:1, D:2 * D], in_=rows[0:1, 0:D], func=AF.Silu), "sc")
            sc_silu = n["sc"]
            c.w("sync", "sc", sc_silu)
            c.dma("sync", lambda E: E.dma_start(
                out=bass.AP(modbuf, 6 * D, [[9 * D, 1], [1, D]]),
                in_=rows[0:1, D:2 * D]), "sx")
            x_silu_st = n["sx"]
            c.w("sync", "sx", x_silu_st)
            c.dma("sync", lambda E: E.dma_start(
                out=cols[:, 80:96], in_=bass.AP(modbuf, 6 * D, [[1, 128], [128, 16]]),
                allow_slow_non_contiguous=True), "sx")
            x_silu_col = n["sx"]


            W_F = 12288
            mod_row = FA[0:1, 0:6 * D]
            mod_evac = {}
            wm_loads = {}
            for nb in range(96):
                par = nb % 2
                c.w("sync", "sp", wm_loads.get(nb - 2, 0))
                wl = c.dma("sync", lambda E, nb=nb, par=par: E.dma_start(
                    out=FA[:, W_F + par * 2048:W_F + par * 2048 + 2048],
                    in_=bass.AP(wmod, nb * 128,
                                [[6 * D, 128], [128 * 6 * D, 16], [1, 128]])),
                    "sx")
                c.w("tensor", "sx", wl)
                c.w("tensor", "sx", x_silu_col)
                c.w("tensor", "sv", mod_evac.get(nb - 2, 0))
                for dc in range(NC_D):
                    c.op("tensor", lambda E, dc=dc, par=par: E.matmul(
                        PF[par][0:1, 0:128], cols[:, 80 + dc:81 + dc],
                        FA[:, W_F + par * 2048 + dc * 128:
                           W_F + par * 2048 + (dc + 1) * 128],
                        start=(dc == 0), stop=(dc == NC_D - 1),
                        skip_group_check=True),
                        "sp" if dc == NC_D - 1 else None)
                wm_loads[nb] = n["sp"]
                mm = n["sp"]
                c.w("vector", "sp", mm)
                c.op("vector", lambda E, nb=nb, par=par: E.tensor_copy(
                    mod_row[0:1, nb * 128:(nb + 1) * 128],
                    PF[par][0:1, 0:128]), "sv")
                mod_evac[nb] = n["sv"]
            v_mod = n["sv"]

            sa_o, ca_o, ga_o, sb_o, cb_o, gb_o = [i * D for i in range(6)]
            tr = FA[0:1, 6 * D:7 * D]
            tr2 = FA[0:1, 7 * D:8 * D]
            c.w("vector", "sv", v_mod)
            vd()
            for g_o in (ga_o, gb_o):
                c.op("vector", lambda E, g_o=g_o: E.tensor_scalar(
                    tr, mod_row[0:1, g_o:g_o + D], 0.0, None, op0=ALU.is_ge))
                vd()
                c.op("vector", lambda E: E.tensor_scalar(
                    tr, tr, 2.0, -1.0, op0=ALU.mult, op1=ALU.add))
                vd()
                c.op("vector", lambda E, g_o=g_o: E.tensor_scalar(
                    tr2, mod_row[0:1, g_o:g_o + D], -1.0, None, op0=ALU.mult))
                vd()
                c.op("vector", lambda E, g_o=g_o: E.tensor_tensor(
                    mod_row[0:1, g_o:g_o + D], mod_row[0:1, g_o:g_o + D],
                    tr2, op=ALU.max))
                vd()
                c.op("vector", lambda E, g_o=g_o: E.tensor_scalar(
                    mod_row[0:1, g_o:g_o + D], mod_row[0:1, g_o:g_o + D],
                    0.25, None, op0=ALU.max))
                vd()
                c.op("vector", lambda E, g_o=g_o: E.tensor_tensor(
                    mod_row[0:1, g_o:g_o + D], mod_row[0:1, g_o:g_o + D],
                    tr, op=ALU.mult))
                vd()
            c.op("vector", lambda E: E.tensor_scalar(
                mod_row[0:1, cb_o:cb_o + D], mod_row[0:1, cb_o:cb_o + D],
                1.0, None, op0=ALU.add))
            vd()
            c.op("vector", lambda E: E.tensor_scalar(
                mod_row[0:1, ca_o:ca_o + D], mod_row[0:1, ca_o:ca_o + D],
                1.0, None, op0=ALU.add), "sv")
            vd()
            v_modt = n["sv"]
            c.w("sync", "sv", v_modt)
            for i, g_o in enumerate((ga_o, cb_o, sb_o, gb_o, ca_o, sa_o)):
                c.dma("sync", lambda E, i=i, g_o=g_o: E.dma_start(
                    out=bass.AP(modbuf, i * D, [[9 * D, 1], [1, D]]),
                    in_=mod_row[0:1, g_o:g_o + D]), "sx")
            x_modst = n["sx"]
            c.w("sync", "sx", x_modst)
            c.dma("sync", lambda E: E.dma_start(
                out=cols[:, 0:16], in_=bass.AP(modbuf, 4 * D, [[1, 128], [128, 16]]),
                allow_slow_non_contiguous=True), "sx")
            c.dma("sync", lambda E: E.dma_start(
                out=cols[:, 16:32], in_=bass.AP(modbuf, 5 * D, [[1, 128], [128, 16]]),
                allow_slow_non_contiguous=True), "sx")
            c.op("vector", lambda E: E.memset(rows[0:1, 3072:3104], 0.0), "sv")
            c.barrier()

            def dequant(codes, scales, stage=32768, dbl=True):
                for dc in range(NC_D):
                    so_ = stage + ((dc % 2) * 4096 if dbl else 0)
                    c.w("gpsimd", "sv", n["sv"] if dc >= (2 if dbl else 1) else 0)
                    c.dma("gpsimd", lambda E, dc=dc, so_=so_: E.dma_start(
                        out=AB[:, so_:so_ + 2048],
                        in_=bass.AP(codes, dc * 128 * D, [[D, 128], [1, D]])),
                        "sw")
                    sw2 = c.dma("gpsimd", lambda E, dc=dc, so_=so_: E.dma_start(
                        out=AB[:, so_ + 2048:so_ + 4096],
                        in_=bass.AP(scales, dc * 128 * D, [[D, 128], [1, D]])),
                        "sw")
                    c.w("vector", "sw", sw2)
                    c.op("vector", lambda E, dc=dc, so_=so_: E.tensor_tensor(
                        AB[:, dc * D:(dc + 1) * D],
                        AB[:, so_:so_ + 2048],
                        AB[:, so_ + 2048:so_ + 4096],
                        op=ALU.mult), "sv")
                return n["sv"]

            A_xbf = 32768
            A_k2 = 34816
            A_bb = 35840
            A_h0 = 36864
            A_kt = 53248
            F_xT = 0
            F_h1 = 16896
            R_mean, R_ss = 0, 512

            def h_and_proj(w_codes, w_scales, tiles, kt_is_q, norm_row_off):
                v_deq = dequant(w_codes, w_scales)
                nt = len(tiles)
                xl = {}
                stats_mm = {}
                comb_rd = {}
                vab = {}
                bc_mm = {}
                bcopy = {}
                h_done = {}
                h_dma = {}
                proj_mm = {}
                kt_dma = {}

                def pre(j):
                    t = tiles[j]
                    if j == 0:
                        c.w("vector", "sv", v_deq)
                        c.w("scalar", "sv", v_deq)
                        c.w("gpsimd", "sv", v_deq)
                    c.w("gpsimd", "sv", h_done.get(j - 2, 0))
                    xl[j] = c.dma("gpsimd", lambda E, t=t, j=j: E.dma_start(
                        out=FA[:, F_xT + (j % 2) * 8192:
                               F_xT + (j % 2) * 8192 + 8192],
                        in_=bass.AP(xT, t * TT,
                                    [[S, 128], [128 * S, 16], [1, TT]])), "sw")
                    xb = F_xT + (j % 2) * 8192
                    pgate = stats_mm.get(j - 1, 0)
                    sbase = n["sp"]
                    for dc in range(NC_D):
                        par = dc % 2
                        c.w("vector", "sw", xl[j])
                        c.w("vector", "sp",
                            sbase + dc - 1 if dc >= 2 else pgate)
                        c.op("vector", lambda E, dc=dc, par=par, xb=xb: E.tensor_copy(
                            AB[:, A_xbf + par * 1024:A_xbf + par * 1024 + 512],
                            FA[:, xb + dc * 512:xb + (dc + 1) * 512]), "sv")
                        cast_v = n["sv"]
                        c.w("scalar", "sw", xl[j])
                        c.w("scalar", "sp",
                            sbase + dc - 1 if dc >= 2 else pgate)
                        c.op("scalar", lambda E, dc=dc, par=par, xb=xb: E.activation(
                            out=AB[:, A_xbf + par * 1024 + 512:
                                   A_xbf + par * 1024 + 1024],
                            in_=FA[:, xb + dc * 512:xb + (dc + 1) * 512],
                            func=AF.Square), "sc")
                        sq_v = n["sc"]
                        c.w("tensor", "sv", cast_v)
                        c.w("tensor", "sc", sq_v)
                        if dc == 0:
                            c.w("tensor", "sv", comb_rd.get(j - 1, 0))
                        c.op("tensor", lambda E, dc=dc, par=par: E.matmul(
                            p2[0:1, 0:512], ones_cb[:],
                            AB[:, A_xbf + par * 1024:A_xbf + par * 1024 + 512],
                            start=(dc == 0), stop=(dc == NC_D - 1),
                            skip_group_check=True))
                        c.op("tensor", lambda E, dc=dc, par=par: E.matmul(
                            p3[0:1, 0:512], ones_cb[:],
                            AB[:, A_xbf + par * 1024 + 512:
                               A_xbf + par * 1024 + 1024],
                            start=(dc == 0), stop=(dc == NC_D - 1),
                            skip_group_check=True), "sp")
                    stats_mm[j] = n["sp"]
                    RA = 1024 + (j % 2) * 1024
                    RB = RA + 512
                    c.w("vector", "sp", stats_mm[j])
                    c.w("vector", "sp", bc_mm.get(j - 2, 0))
                    c.op("vector", lambda E: E.tensor_scalar(
                        rows[0:1, R_mean:R_mean + 512], p2[0:1, 0:512],
                        1.0 / D, None, op0=ALU.mult))
                    c.op("vector", lambda E: E.tensor_scalar(
                        rows[0:1, R_ss:R_ss + 512], p3[0:1, 0:512],
                        1.0 / D, None, op0=ALU.mult), "sv")
                    comb_rd[j] = n["sv"]
                    vd()
                    c.op("vector", lambda E, RA=RA: E.tensor_tensor(
                        rows[0:1, RA:RA + 512], rows[0:1, R_mean:R_mean + 512],
                        rows[0:1, R_mean:R_mean + 512], op=ALU.mult))
                    vd()
                    c.op("vector", lambda E, RA=RA: E.tensor_tensor(
                        rows[0:1, RA:RA + 512], rows[0:1, R_ss:R_ss + 512],
                        rows[0:1, RA:RA + 512], op=ALU.subtract))
                    vd()
                    c.op("vector", lambda E, RA=RA: E.tensor_scalar(
                        rows[0:1, RA:RA + 512], rows[0:1, RA:RA + 512],
                        EPS, None, op0=ALU.add), "sv")
                    vd()
                    v_var = n["sv"]
                    c.w("scalar", "sv", v_var)
                    c.op("scalar", lambda E, RA=RA: E.activation(
                        out=rows[0:1, RA:RA + 512],
                        in_=rows[0:1, RA:RA + 512], func=AF.Sqrt), "sc")
                    sc_sq = n["sc"]
                    c.w("vector", "sc", sc_sq)
                    c.op("vector", lambda E, RA=RA: E.reciprocal(
                        rows[0:1, RA:RA + 512], rows[0:1, RA:RA + 512]))
                    vd()
                    c.op("vector", lambda E, RA=RA, RB=RB: E.scalar_tensor_tensor(
                        rows[0:1, RB:RB + 512], rows[0:1, R_mean:R_mean + 512],
                        -1.0, rows[0:1, RA:RA + 512],
                        op0=ALU.mult, op1=ALU.mult), "sv")
                    vd()
                    vab[j] = n["sv"]

                def hstage(j):
                    xb = F_xT + (j % 2) * 8192
                    RA = 1024 + (j % 2) * 1024
                    RB = RA + 512
                    bAo = (j % 2) * 512
                    bbs = A_bb + (j % 2) * 512
                    hb = A_h0 + (j % 2) * 8192
                    c.w("tensor", "sv", vab[j])
                    c.w("tensor", "sv", h_done.get(j - 2, 0))
                    c.w("tensor", "sv", bcopy.get(j - 1, 0))
                    c.op("tensor", lambda E, RA=RA, bAo=bAo: E.matmul(
                        pOb[:, bAo:bAo + 512], ones_rf[:],
                        rows[0:1, RA:RA + 512],
                        start=True, stop=True, skip_group_check=True))
                    c.op("tensor", lambda E, RB=RB: E.matmul(
                        pOb[:, 1024:1536], ones_rf[:], rows[0:1, RB:RB + 512],
                        start=True, stop=True, skip_group_check=True), "sp")
                    bc_mm[j] = n["sp"]
                    c.w("vector", "sp", bc_mm[j])
                    c.op("vector", lambda E, bbs=bbs: E.tensor_copy(
                        AB[:, bbs:bbs + 512], pOb[:, 1024:1536]), "sv")
                    bcopy[j] = n["sv"]
                    c.w("vector", "sx", h_dma.get(j - 2, 0))
                    c.w("vector", "sp", proj_mm.get(j - 2, 0))
                    for dc in range(NC_D):
                        c.op("vector", lambda E, dc=dc, xb=xb, bAo=bAo: E.tensor_tensor(
                            FA[:, F_h1:F_h1 + 512],
                            FA[:, xb + dc * 512:xb + (dc + 1) * 512],
                            pOb[:, bAo:bAo + 512], op=ALU.mult))
                        c.op("vector", lambda E, bbs=bbs: E.tensor_tensor(
                            FA[:, F_h1:F_h1 + 512], FA[:, F_h1:F_h1 + 512],
                            AB[:, bbs:bbs + 512], op=ALU.add))
                        c.op("vector", lambda E, dc=dc, hb=hb: E.tensor_scalar(
                            AB[:, hb + dc * 512:hb + (dc + 1) * 512],
                            FA[:, F_h1:F_h1 + 512],
                            cols[:, dc:dc + 1], cols[:, 16 + dc:17 + dc],
                            op0=ALU.mult, op1=ALU.add),
                            "sv" if dc == NC_D - 1 else None)
                    h_done[j] = n["sv"]
                    if not kt_is_q:
                        c.w("sync", "sv", h_done[j])
                        h_dma[j] = c.dma("sync", lambda E, j=j, hb=hb: E.dma_start(
                            out=bass.AP(hbuf, tiles[j] * TT,
                                        [[S, 128], [128 * S, 16], [1, TT]]),
                            in_=AB[:, hb:hb + 8192]), "sx")

                ev_ct = {}
                k2v = {}
                nmm = {}
                rmax_done = {}

                def proj(idx):
                    t = tiles[idx]
                    hb = A_h0 + (idx % 2) * 8192

                    def norm_mm(fc):
                        c.w("tensor", "sv", k2v[(idx, fc)])
                        c.w("tensor", "sv",
                            rmax_done.get((idx, fc - 2),
                                          rmax_done.get((idx - 1, fc + 14), 0)))
                        po = 32 + (fc % 2) * 32
                        c.op("tensor", lambda E, fc=fc, po=po: E.matmul(
                            p3[po:po + 1, 0:512], ones_cb[:],
                            AB[:, A_k2 + (fc % 2) * 512:
                               A_k2 + (fc % 2) * 512 + 512],
                            start=True, stop=True, skip_group_check=True), "sp")
                        nmm[(idx, fc)] = n["sp"]
                        c.w("vector", "sp", nmm[(idx, fc)])
                        c.op("vector", lambda E, po=po: E.reduce_max(
                            rows[0:1, 3120:3121], p3[po:po + 1, 0:512],
                            axis=AX.X))
                        vd()
                        c.op("vector", lambda E, fc=fc: E.tensor_tensor(
                            rows[0:1, norm_row_off + fc:norm_row_off + fc + 1],
                            rows[0:1, norm_row_off + fc:norm_row_off + fc + 1],
                            rows[0:1, 3120:3121], op=ALU.max), "sv")
                        rmax_done[(idx, fc)] = n["sv"]

                    for fc in range(NC_D):
                        par = fc % 2
                        sub = fc // 4
                        subg = idx * 4 + sub
                        c.w("tensor", "sv", h_done[idx] if fc == 0 else 0)
                        c.w("tensor", "sv",
                            v_deq if fc == 0 and idx == 0 else 0)
                        c.w("tensor", "sc",
                            ev_ct.get((idx, fc - 2),
                                      ev_ct.get((idx - 1, fc + 14), 0)))
                        for dc in range(NC_D):
                            c.op("tensor", lambda E, fc=fc, dc=dc, par=par, hb=hb: E.matmul(
                                PF[par][:, 0:512],
                                AB[:, dc * D + fc * 128:dc * D + fc * 128 + 128],
                                AB[:, hb + dc * 512:hb + (dc + 1) * 512],
                                start=(dc == 0), stop=(dc == NC_D - 1),
                                skip_group_check=True),
                                "sp" if dc == NC_D - 1 else None)
                        pj = n["sp"]
                        if fc >= 1:
                            norm_mm(fc - 1)
                        bias = (cols[:, 32 + fc:33 + fc] if kt_is_q
                                else cols[:, 48 + fc:49 + fc])
                        ktslot = A_kt + (sub % 2) * 2048 + (fc % 4) * 512
                        c.w("scalar", "sp", pj)
                        c.w("scalar", "sv",
                            k2v.get((idx, fc - 8),
                                    k2v.get((idx - 1, fc + 8), 0)))
                        if fc % 4 == 0:
                            c.w("scalar", "sx", kt_dma.get(subg - 2, 0))
                        c.op("scalar", lambda E, par=par, bias=bias, ktslot=ktslot: E.activation(
                            out=AB[:, ktslot:ktslot + 512],
                            in_=PF[par][:, 0:512], func=AF.Identity,
                            bias=bias), "sc")
                        ev_ct[(idx, fc)] = n["sc"]
                        c.w("vector", "sc", ev_ct[(idx, fc)])
                        c.w("vector", "sp",
                            nmm.get((idx, fc - 2),
                                    nmm.get((idx - 1, fc + 14), 0)))
                        c.op("vector", lambda E, fc=fc, ktslot=ktslot: E.tensor_tensor(
                            AB[:, A_k2 + (fc % 2) * 512:
                               A_k2 + (fc % 2) * 512 + 512],
                            AB[:, ktslot:ktslot + 512],
                            AB[:, ktslot:ktslot + 512],
                            op=ALU.mult), "sv")
                        k2v[(idx, fc)] = n["sv"]
                        if fc % 4 == 3:
                            c.op("scalar", lambda E: E.drain(), "sc")
                            c.w("sync", "sc", n["sc"])
                            if kt_is_q:
                                dst = bass.AP(
                                    qtbuf, t * TT + sub * 4 * 128 * OWN,
                                    [[OWN, 128], [128 * OWN, 4], [1, TT]])
                            else:
                                dst = bass.AP(
                                    ktbuf, t * TT + sub * 4 * 128 * S,
                                    [[S, 128], [128 * S, 4], [1, TT]])
                            kt_dma[subg] = c.dma(
                                "sync", lambda E, dst=dst, sub=sub: E.dma_start(
                                    out=dst,
                                    in_=AB[:, A_kt + (sub % 2) * 2048:
                                           A_kt + (sub % 2) * 2048 + 2048]),
                                "sx")
                    norm_mm(NC_D - 1)
                    proj_mm[idx] = n["sp"]

                pre(0)
                if nt > 1:
                    pre(1)
                hstage(0)
                for idx in range(nt):
                    if idx + 1 < nt:
                        hstage(idx + 1)
                    proj(idx)
                    if idx + 2 < nt:
                        pre(idx + 2)

            h_and_proj(wq_c, wq_s, [0, 1], True, 3088)
            c.barrier()
            h_and_proj(wk_c, wk_s, list(range(NTT)), False, 3072)
            c.barrier()

            v_deq = dequant(wv_c, wv_s)
            A_vsb = A_kt
            bvr = brow[0:1, 0:D]
            vdma = {}
            proj_v = {}
            xlv = {}

            def vload(t):
                hb = A_h0 + (t % 2) * 8192
                c.w("gpsimd", "sv", v_deq if t < 2 else 0)
                c.w("gpsimd", "sp", proj_v.get(t - 2, 0))
                xlv[t] = c.dma("gpsimd", lambda E, t=t, hb=hb: E.dma_start(
                    out=AB[:, hb:hb + 8192],
                    in_=bass.AP(hbuf, t * TT,
                                [[S, 128], [128 * S, 16], [1, TT]])), "sw")

            def vcomp(t):
                hb = A_h0 + (t % 2) * 8192
                for ts4 in range(4):
                    vpar = ts4 % 2
                    for fg in range(4):
                        par = (ts4 * 4 + fg) % 2
                        c.w("tensor", "sw", xlv[t])
                        c.w("tensor", "sv", v_deq)
                        c.w("tensor", "sc",
                            n["sc"] if (ts4 * 4 + fg) >= 2 else 0)
                        for dc in range(NC_D):
                            c.op("tensor", lambda E, ts4=ts4, fg=fg, dc=dc, par=par, hb=hb: E.matmul(
                                PF[par][:, 0:512],
                                AB[:, hb + dc * 512 + ts4 * 128:
                                   hb + dc * 512 + ts4 * 128 + 128],
                                AB[:, dc * D + fg * 512:dc * D + (fg + 1) * 512],
                                start=(dc == 0), stop=False,
                                skip_group_check=True))
                        c.op("tensor", lambda E, fg=fg, par=par: E.matmul(
                            PF[par][:, 0:512], ones_rb[:],
                            bvr[0:1, fg * 512:(fg + 1) * 512],
                            start=False, stop=True, skip_group_check=True), "sp")
                        vmm = n["sp"]
                        c.w("scalar", "sp", vmm)
                        c.w("scalar", "sx",
                            vdma.get((t, ts4 - 2), vdma.get((t - 1, ts4 + 2), 0))
                            if fg == 0 else 0)
                        c.op("scalar", lambda E, fg=fg, par=par, vpar=vpar: E.activation(
                            out=AB[:, A_vsb + vpar * 2048 + fg * 512:
                                   A_vsb + vpar * 2048 + (fg + 1) * 512],
                            in_=PF[par][:, 0:512], func=AF.Identity), "sc")
                    c.op("scalar", lambda E: E.drain(), "sc")
                    ev = n["sc"]
                    c.w("sync", "sc", ev)
                    vdma[(t, ts4)] = c.dma(
                        "sync", lambda E, t=t, ts4=ts4, vpar=vpar: E.dma_start(
                            out=bass.AP(vbuf, (t * TT + ts4 * 128) * 128,
                                        [[128, 128], [S * 128, 16], [1, 128]]),
                            in_=bass.AP(AB, A_vsb + vpar * 2048,
                                        [[57344, 128], [512, 4], [1, 512]])),
                        "sx")
                proj_v[t] = n["sp"]

            vload(0)
            vload(1)
            for t in range(NTT):
                vcomp(t)
                if t + 2 < NTT:
                    vload(t + 2)
            c.barrier()

            vd()
            c.op("vector", lambda E: E.tensor_tensor(
                rows[0:1, 3104:3120], rows[0:1, 3088:3104],
                rows[0:1, 3072:3088], op=ALU.mult), "sv")
            vd()
            v_c1 = n["sv"]
            c.w("scalar", "sv", v_c1)
            c.op("scalar", lambda E: E.activation(
                out=rows[0:1, 3104:3120], in_=rows[0:1, 3104:3120],
                func=AF.Sqrt), "sc")
            sc_c = n["sc"]
            c.w("vector", "sc", sc_c)
            c.op("vector", lambda E: E.tensor_scalar(
                rows[0:1, 3104:3120], rows[0:1, 3104:3120],
                -INV_SQRT_DH, None, op0=ALU.mult), "sv")
            vd()
            v_c2 = n["sv"]
            c.w("tensor", "sv", v_c2)
            c.op("tensor", lambda E: E.matmul(
                p4[:, 0:16], ones_rf[:], rows[0:1, 3104:3120],
                start=True, stop=True, skip_group_check=True), "sp")
            pc = n["sp"]
            c.w("vector", "sp", pc)
            c.op("vector", lambda E: E.tensor_copy(cols[:, 64:80], p4[:, 0:16]),
                 "sv")
            v_negc = n["sv"]

            A_vaug = 0
            A_qt = 16512
            A_pt = 18560
            A_kth = 20608
            A_oT = 36992
            A_obf = 53376

            c.w("scalar", "sv", v_negc)
            c.w("vector", "sv", v_negc)
            c.op("vector", lambda E: E.memset(
                bass.AP(AB, A_vaug + 128, [[57344, 128], [129, 128]]), 1.0),
                "sv")
            v_ones = n["sv"]
            c.w("sync", "sv", v_ones)

            head_mm_done = {}
            head_evac_done = 0
            for h in range(NH):
                par = h % 2
                c.w("sync", "sp", head_mm_done.get(h - 2, 0))
                c.dma("sync", lambda E, h=h, par=par: E.dma_start(
                    out=AB[:, A_kth + par * 8192:A_kth + par * 8192 + 8192],
                    in_=bass.AP(ktbuf, h * 128 * S, [[S, 128], [1, S]])), "sx")
                c.dma("sync", lambda E, h=h, par=par: E.dma_start(
                    out=bass.AP(AB, A_vaug + par * 8256,
                                [[57344, 128], [129, 64], [1, 128]]),
                    in_=bass.AP(vbuf, h * S * 128,
                                [[128, 128], [128 * 128, 64], [1, 128]])), "sx")
                lx = c.dma("sync", lambda E, h=h, par=par: E.dma_start(
                    out=AB[:, A_qt + par * 1024:A_qt + par * 1024 + 1024],
                    in_=bass.AP(qtbuf, h * 128 * OWN, [[OWN, 128], [1, OWN]])),
                    "sx")
                exp_base = n["sc"]
                sc_mm = {}
                av = {}

                def scores_mm(kt, par=par, lx=lx, exp_base=exp_base):
                    spar = kt % 2
                    c.w("tensor", "sx", lx)
                    c.w("tensor", "sc", max(0, exp_base + kt - 1))
                    for qh in range(2):
                        c.op("tensor", lambda E, kt=kt, qh=qh, par=par, spar=spar: E.matmul(
                            PSC[spar][:, qh * 512:(qh + 1) * 512],
                            AB[:, A_kth + par * 8192 + kt * 128:
                               A_kth + par * 8192 + (kt + 1) * 128],
                            AB[:, A_qt + par * 1024 + qh * 512:
                               A_qt + par * 1024 + (qh + 1) * 512],
                            start=True, stop=True, skip_group_check=True),
                            "sp" if qh == 1 else None)
                    return n["sp"]

                sc_mm[0] = scores_mm(0)
                for kt in range(64):
                    spar = kt % 2
                    if kt < 63:
                        sc_mm[kt + 1] = scores_mm(kt + 1)
                    c.w("scalar", "sp", sc_mm[kt])
                    c.op("scalar", lambda E, kt=kt, h=h, spar=spar: E.activation(
                        out=AB[:, A_pt + spar * 1024:A_pt + spar * 1024 + 1024],
                        in_=PSC[spar][:, 0:1024], func=AF.Exp,
                        bias=cols[:, 64 + h:65 + h], scale=INV_SQRT_DH), "sc")
                    ex = n["sc"]
                    c.w("tensor", "sc", ex)
                    if kt == 0:
                        c.w("tensor", "sv", head_evac_done)
                    for qs in range(8):
                        ob = (qs // 3) * 512 + (qs % 3) * 132
                        c.op("tensor", lambda E, kt=kt, qs=qs, ob=ob, par=par, spar=spar: E.matmul(
                            pOb[:, ob:ob + 129],
                            AB[:, A_pt + spar * 1024 + qs * 128:
                               A_pt + spar * 1024 + (qs + 1) * 128],
                            bass.AP(AB, A_vaug + par * 8256 + kt * 129,
                                    [[57344, 128], [1, 129]]),
                            start=(kt == 0), stop=(kt == 63),
                            skip_group_check=True),
                            "sp" if qs == 7 else None)
                    av[kt] = n["sp"]
                head_mm_done[h] = n["sp"]
                mmall = n["sp"]
                for qs_g in range(8):
                    ob = (qs_g // 3) * 512 + (qs_g % 3) * 132
                    c.w("vector", "sp", mmall)
                    c.op("vector", lambda E, ob=ob: E.reciprocal(
                        cols[:, 144:145], pOb[:, ob + 128:ob + 129]))
                    vd()
                    c.op("vector", lambda E, ob=ob, qs_g=qs_g: E.tensor_scalar(
                        AB[:, A_obf + (qs_g % 2) * 128:
                           A_obf + (qs_g % 2) * 128 + 128],
                        pOb[:, ob:ob + 128], cols[:, 144:145], None,
                        op0=ALU.mult), "sv")
                    v_ev = n["sv"]
                    c.w("tensor", "sv", v_ev)
                    c.op("tensor", lambda E, qs_g=qs_g: E.transpose(
                        PTT[qs_g % 2][:, 0:128],
                        AB[:, A_obf + (qs_g % 2) * 128:
                           A_obf + (qs_g % 2) * 128 + 128], idn[:]), "sp")
                    tp = n["sp"]
                    c.w("vector", "sp", tp)
                    c.op("vector", lambda E, h=h, qs_g=qs_g: E.tensor_copy(
                        AB[:, A_oT + h * 1024 + qs_g * 128:
                           A_oT + h * 1024 + (qs_g + 1) * 128],
                        PTT[qs_g % 2][:, 0:128]), "sv")
                head_evac_done = n["sv"]
            c.barrier()

            v_deq = dequant(wo_c, wo_s, stage=32768, dbl=False)
            F_ga = 0
            F_xo = 2048
            F_t1 = 6144
            F_xp = 6656
            c.dma("sync", lambda E: E.dma_start(
                out=rows[0:1, 0:D], in_=bass.AP(modbuf, 0, [[9 * D, 1], [1, D]])),
                "sx")
            x_ga = n["sx"]
            for i in range(4):
                c.w("tensor", "sx", x_ga)
                c.w("tensor", "sv", n["sv"] if i >= 1 else 0)
                c.op("tensor", lambda E, i=i: E.matmul(
                    p4[:, 0:512], ones_rf[:], rows[0:1, i * 512:(i + 1) * 512],
                    start=True, stop=True, skip_group_check=True), "sp")
                bm = n["sp"]
                c.w("vector", "sp", bm)
                c.op("vector", lambda E, i=i: E.tensor_copy(
                    FA[:, F_ga + i * 512:F_ga + (i + 1) * 512], p4[:, 0:512]),
                    "sv")
            vd()
            v_ga = n["sv"]

            bor = brow[0:1, D:2 * D]
            xp_dmas = {}
            evdone = {}
            for qs in range(8):
                c.w("sync", "sv", evdone.get(qs - 2, v_ga))
                xo = c.dma("sync", lambda E, qs=qs: E.dma_start(
                    out=FA[:, F_xo + (qs % 2) * 2048:
                           F_xo + (qs % 2) * 2048 + 2048],
                    in_=bass.AP(x_own, qs * 128 * D, [[D, 128], [1, D]])), "sx")
                for dg in range(4):
                    par = (qs * 4 + dg) % 2
                    c.w("tensor", "sv", v_deq if qs == 0 and dg == 0 else 0)
                    c.w("tensor", "sv",
                        n["sv"] if (qs * 4 + dg) >= 2 else 0)
                    for fc in range(NC_D):
                        c.op("tensor", lambda E, qs=qs, dg=dg, fc=fc, par=par: E.matmul(
                            PF[par][:, 0:512],
                            AB[:, A_oT + fc * 1024 + qs * 128:
                               A_oT + fc * 1024 + (qs + 1) * 128],
                            AB[:, fc * D + dg * 512:fc * D + (dg + 1) * 512],
                            start=(fc == 0), stop=False,
                            skip_group_check=True))
                    c.op("tensor", lambda E, dg=dg, par=par: E.matmul(
                        PF[par][:, 0:512], ones_rb[:],
                        bor[0:1, dg * 512:(dg + 1) * 512],
                        start=False, stop=True, skip_group_check=True), "sp")
                    omm = n["sp"]
                    c.w("vector", "sp", omm)
                    c.w("vector", "sx", xo)
                    c.w("vector", "sx", xp_dmas.get(qs - 2, 0))
                    c.op("vector", lambda E, dg=dg, par=par: E.tensor_tensor(
                        FA[:, F_t1:F_t1 + 512], PF[par][:, 0:512],
                        FA[:, F_ga + dg * 512:F_ga + (dg + 1) * 512],
                        op=ALU.mult))
                    c.op("vector", lambda E, qs=qs, dg=dg: E.tensor_tensor(
                        FA[:, F_xp + (qs % 2) * 2048 + dg * 512:
                           F_xp + (qs % 2) * 2048 + (dg + 1) * 512],
                        FA[:, F_t1:F_t1 + 512],
                        FA[:, F_xo + (qs % 2) * 2048 + dg * 512:
                           F_xo + (qs % 2) * 2048 + (dg + 1) * 512],
                        op=ALU.add), "sv")
                evdone[qs] = n["sv"]
                c.w("sync", "sv", evdone[qs])
                xp_dmas[qs] = c.dma("sync", lambda E, qs=qs: E.dma_start(
                    out=bass.AP(xpbuf, qs * 128 * D, [[D, 128], [1, D]]),
                    in_=FA[:, F_xp + (qs % 2) * 2048:
                           F_xp + (qs % 2) * 2048 + 2048]), "sx")
            c.barrier()

            F_C2 = 0
            F_sb = 2048
            F_xps = 4096
            F_xn = 8192
            F_q1 = 10240
            F_q2 = 12288
            A_xq = 32768
            A_xqT = 40960
            for slot, foff in ((1, F_C2), (2, F_sb)):
                c.dma("sync", lambda E, slot=slot: E.dma_start(
                    out=rows[0:1, 0:D],
                    in_=bass.AP(modbuf, slot * D, [[9 * D, 1], [1, D]])), "sx")
                xg = n["sx"]
                for j in range(4):
                    c.w("tensor", "sx", xg)
                    c.w("tensor", "sv", n["sv"] if j >= 1 or slot == 2 else 0)
                    c.op("tensor", lambda E, j=j: E.matmul(
                        p4[:, 0:512], ones_rf[:],
                        rows[0:1, j * 512:(j + 1) * 512],
                        start=True, stop=True, skip_group_check=True), "sp")
                    bm = n["sp"]
                    c.w("vector", "sp", bm)
                    c.op("vector", lambda E, j=j, foff=foff: E.tensor_copy(
                        FA[:, foff + j * 512:foff + (j + 1) * 512],
                        p4[:, 0:512]), "sv")
                c.w("sync", "sv", n["sv"])
            vd()
            v_bc = n["sv"]

            xn_dones = {}
            for qs in range(8):
                c.w("sync", "sc", xn_dones.get(qs - 2, 0))
                c.w("sync", "sv", v_bc if qs == 0 else 0)
                xl = c.dma("sync", lambda E, qs=qs: E.dma_start(
                    out=FA[:, F_xps + (qs % 2) * 2048:
                           F_xps + (qs % 2) * 2048 + 2048],
                    in_=bass.AP(xpbuf, qs * 128 * D, [[D, 128], [1, D]])), "sx")
                xb = F_xps + (qs % 2) * 2048
                c.w("scalar", "sx", xl)
                c.w("scalar", "sp", n["sp"])
                c.op("scalar", lambda E, xb=xb: E.activation(
                    out=AB[:, A_xq:A_xq + 2048], in_=FA[:, xb:xb + 2048],
                    func=AF.Identity, accum_out=cols[:, 96:97]))
                c.op("scalar", lambda E, xb=xb: E.activation(
                    out=AB[:, A_xq:A_xq + 2048], in_=FA[:, xb:xb + 2048],
                    func=AF.Square, accum_out=cols[:, 104:105]))
                c.op("scalar", lambda E: E.drain(), "sc")
                st = n["sc"]
                c.w("vector", "sc", st)
                c.op("vector", lambda E: E.tensor_scalar(
                    cols[:, 96:97], cols[:, 96:97], 1.0 / D, None,
                    op0=ALU.mult))
                c.op("vector", lambda E: E.tensor_scalar(
                    cols[:, 104:105], cols[:, 104:105], 1.0 / D, None,
                    op0=ALU.mult))
                vd()
                c.op("vector", lambda E: E.tensor_tensor(
                    cols[:, 145:146], cols[:, 96:97], cols[:, 96:97],
                    op=ALU.mult))
                vd()
                c.op("vector", lambda E: E.tensor_tensor(
                    cols[:, 104:105], cols[:, 104:105], cols[:, 145:146],
                    op=ALU.subtract))
                vd()
                c.op("vector", lambda E: E.tensor_scalar(
                    cols[:, 104:105], cols[:, 104:105], EPS, None,
                    op0=ALU.add), "sv")
                vd()
                vv = n["sv"]
                c.w("scalar", "sv", vv)
                c.op("scalar", lambda E: E.activation(
                    out=cols[:, 104:105], in_=cols[:, 104:105],
                    func=AF.Sqrt), "sc")
                sq = n["sc"]
                c.w("vector", "sc", sq)
                c.op("vector", lambda E: E.reciprocal(
                    cols[:, 104:105], cols[:, 104:105]))
                vd()
                c.op("vector", lambda E: E.scalar_tensor_tensor(
                    cols[:, 96:97], cols[:, 96:97], -1.0, cols[:, 104:105],
                    op0=ALU.mult, op1=ALU.mult), "sv")
                vd()
                ab2 = n["sv"]
                c.w("scalar", "sv", ab2)
                c.op("scalar", lambda E, xb=xb: E.activation(
                    out=FA[:, F_xn:F_xn + 2048], in_=FA[:, xb:xb + 2048],
                    func=AF.Identity, bias=cols[:, 96:97],
                    scale=cols[:, 104:105]), "sc")
                xn_done = n["sc"]
                xn_dones[qs] = xn_done
                c.w("vector", "sc", xn_done)
                c.op("vector", lambda E: E.tensor_tensor(
                    FA[:, F_xn:F_xn + 2048], FA[:, F_xn:F_xn + 2048],
                    FA[:, F_C2:F_C2 + 2048], op=ALU.mult))
                c.op("vector", lambda E: E.tensor_tensor(
                    FA[:, F_xn:F_xn + 2048], FA[:, F_xn:F_xn + 2048],
                    FA[:, F_sb:F_sb + 2048], op=ALU.add),
                    "sv" if DEBUG_BUFS else None)
                if DEBUG_BUFS:
                    c.w("sync", "sv", n["sv"])
                    hd_dma = c.dma("sync", lambda E, qs=qs: E.dma_start(
                        out=bass.AP(h2dbg, qs * 128 * D, [[D, 128], [1, D]]),
                        in_=FA[:, F_xn:F_xn + 2048]), "sx")
                    c.w("vector", "sx", hd_dma)
                c.op("vector", lambda E: E.tensor_reduce(
                    cols[:, 145:146], FA[:, F_xn:F_xn + 2048], axis=AX.X,
                    op=ALU.max, apply_absolute_value=True))
                vd()
                c.op("vector", lambda E: E.tensor_scalar(
                    cols[:, 145:146], cols[:, 145:146], 1e-5, None,
                    op0=ALU.max))
                vd()
                c.op("vector", lambda E: E.tensor_scalar(
                    cols[:, 146:147], cols[:, 145:146], 1.0 / 127.0, None,
                    op0=ALU.mult))
                c.op("vector", lambda E: E.reciprocal(
                    cols[:, 145:146], cols[:, 145:146]))
                vd()
                c.op("vector", lambda E: E.tensor_scalar(
                    cols[:, 145:146], cols[:, 145:146], 127.0, None,
                    op0=ALU.mult))
                vd()
                c.op("vector", lambda E: E.tensor_scalar(
                    FA[:, F_xn:F_xn + 2048], FA[:, F_xn:F_xn + 2048],
                    cols[:, 145:146], None, op0=ALU.mult))
                c.op("vector", lambda E: E.tensor_scalar(
                    FA[:, F_q1:F_q1 + 2048], FA[:, F_xn:F_xn + 2048],
                    M32, -M32, op0=ALU.add, op1=ALU.add))
                c.op("vector", lambda E: E.tensor_scalar(
                    FA[:, F_q2:F_q2 + 2048], FA[:, F_xn:F_xn + 2048],
                    M64, -M64, op0=ALU.add, op1=ALU.add))
                c.op("vector", lambda E: E.tensor_tensor(
                    FA[:, F_q2:F_q2 + 2048], FA[:, F_q2:F_q2 + 2048],
                    FA[:, F_q1:F_q1 + 2048], op=ALU.subtract))
                c.op("vector", lambda E: E.scalar_tensor_tensor(
                    FA[:, F_q1:F_q1 + 2048], FA[:, F_q2:F_q2 + 2048],
                    w_col[:], FA[:, F_q1:F_q1 + 2048],
                    op0=ALU.mult, op1=ALU.add))
                c.op("vector", lambda E: E.tensor_scalar(
                    FA[:, F_q1:F_q1 + 2048], FA[:, F_q1:F_q1 + 2048],
                    -128.0, 127.0, op0=ALU.max, op1=ALU.min))
                c.op("vector", lambda E: E.tensor_scalar(
                    AB[:, A_xq:A_xq + 2048], FA[:, F_q1:F_q1 + 2048],
                    cols[:, 146:147], None, op0=ALU.mult), "sv")
                xq_done = n["sv"]
                if DEBUG_BUFS:
                    c.w("sync", "sv", xq_done)
                    xq_dma = c.dma("sync", lambda E, qs=qs: E.dma_start(
                        out=bass.AP(xqdbg, qs * 128 * D, [[D, 128], [1, D]]),
                        in_=AB[:, A_xq:A_xq + 2048]), "sx")
                for dc in range(NC_D):
                    c.w("tensor", "sv", xq_done if dc == 0 else 0)
                    c.w("tensor", "sv",
                        n["sv"] if (qs * 16 + dc) >= 2 else 0)
                    c.op("tensor", lambda E, dc=dc: E.transpose(
                        PTT[dc % 2][:, 0:128],
                        AB[:, A_xq + dc * 128:A_xq + (dc + 1) * 128],
                        idn[:]), "sp")
                    tp = n["sp"]
                    c.w("vector", "sp", tp)
                    c.op("vector", lambda E, dc=dc, qs=qs: E.tensor_copy(
                        AB[:, A_xqT + dc * 1024 + qs * 128:
                           A_xqT + dc * 1024 + (qs + 1) * 128],
                        PTT[dc % 2][:, 0:128]), "sv")
            c.barrier()

            F_nw = 0
            F_gb = 5632
            F_g = 7680
            F_nwrow = 8192
            F_gbrow = 13824
            A_w13 = 0
            A_r = 32768
            A_t = 34816
            A_gsq = 35840

            c.dma("sync", lambda E: E.dma_start(
                out=FA[0:1, F_gbrow:F_gbrow + D],
                in_=bass.AP(modbuf, 3 * D, [[9 * D, 1], [1, D]])), "sx")
            xg1 = n["sx"]
            c.dma("sync", lambda E: E.dma_start(
                out=FA[0:1, F_nwrow:F_nwrow + HID],
                in_=bass.AP(normw, 0, [[HID, 1], [1, HID]])), "sx")
            xg2 = n["sx"]
            for j in range(4):
                c.w("tensor", "sx", xg1)
                c.w("tensor", "sv", n["sv"] if j >= 1 else 0)
                c.op("tensor", lambda E, j=j: E.matmul(
                    p4[:, 0:512], ones_rf[:],
                    FA[0:1, F_gbrow + j * 512:F_gbrow + (j + 1) * 512],
                    start=True, stop=True, skip_group_check=True), "sp")
                bm = n["sp"]
                c.w("vector", "sp", bm)
                c.op("vector", lambda E, j=j: E.tensor_copy(
                    FA[:, F_gb + j * 512:F_gb + (j + 1) * 512], p4[:, 0:512]),
                    "sv")
            for j in range(11):
                c.w("tensor", "sx", xg2)
                c.w("tensor", "sv", n["sv"])
                c.op("tensor", lambda E, j=j: E.matmul(
                    p4[:, 0:512], ones_rf[:],
                    FA[0:1, F_nwrow + j * 512:F_nwrow + (j + 1) * 512],
                    start=True, stop=True, skip_group_check=True), "sp")
                bm = n["sp"]
                c.w("vector", "sp", bm)
                c.op("vector", lambda E, j=j: E.tensor_copy(
                    FA[:, F_nw + j * 512:F_nw + (j + 1) * 512], p4[:, 0:512]),
                    "sv")
            vd()
            c.op("vector", lambda E: E.memset(cols[:, 112:128], 0.0), "sv")
            vd()
            v_z2 = n["sv"]

            w13_loads = {}
            g_dones = {}
            t_dmas = {}
            for hp in range(11):
                wpar = hp % 2
                c.w("gpsimd", "sp", w13_loads.get(hp - 2, (0, 0))[1])
                c.dma("gpsimd", lambda E, hp=hp, wpar=wpar: E.dma_start(
                    out=AB[:, A_w13 + wpar * 16384:A_w13 + wpar * 16384 + 8192],
                    in_=bass.AP(w13T, hp * 512,
                                [[2 * HID, 128], [128 * 2 * HID, 16], [1, 512]])),
                    "sw")
                wl = c.dma("gpsimd", lambda E, hp=hp, wpar=wpar: E.dma_start(
                    out=AB[:, A_w13 + wpar * 16384 + 8192:
                           A_w13 + wpar * 16384 + 16384],
                    in_=bass.AP(w13T, HID + hp * 512,
                                [[2 * HID, 128], [128 * 2 * HID, 16], [1, 512]])),
                    "sw")
                for qs in range(8):
                    it = hp * 8 + qs
                    c.w("tensor", "sw", wl)
                    c.w("tensor", "sv", v_z2 if it == 0 else 0)
                    c.w("tensor", "sv", n["sv"] if it >= 2 else 0)
                    c.w("tensor", "sc", n["sc"] if it >= 2 else 0)
                    for dc in range(NC_D):
                        c.op("tensor", lambda E, qs=qs, dc=dc, wpar=wpar: E.matmul(
                            PF[qs % 2][:, 0:512],
                            AB[:, A_xqT + dc * 1024 + qs * 128:
                               A_xqT + dc * 1024 + (qs + 1) * 128],
                            AB[:, A_w13 + wpar * 16384 + dc * 512:
                               A_w13 + wpar * 16384 + (dc + 1) * 512],
                            start=(dc == 0), stop=(dc == NC_D - 1),
                            skip_group_check=True))
                        c.op("tensor", lambda E, qs=qs, dc=dc, wpar=wpar: E.matmul(
                            PS[qs % 2][:, 0:512],
                            AB[:, A_xqT + dc * 1024 + qs * 128:
                               A_xqT + dc * 1024 + (qs + 1) * 128],
                            AB[:, A_w13 + wpar * 16384 + 8192 + dc * 512:
                               A_w13 + wpar * 16384 + 8192 + (dc + 1) * 512],
                            start=(dc == 0), stop=(dc == NC_D - 1),
                            skip_group_check=True),
                            "sp" if dc == NC_D - 1 else None)
                    x13 = n["sp"]
                    c.w("scalar", "sp", x13)
                    c.w("scalar", "sv", g_dones.get(it - 2, 0))
                    c.op("scalar", lambda E, qs=qs: E.activation(
                        out=AB[:, A_r + (qs % 2) * 1024:
                               A_r + (qs % 2) * 1024 + 512],
                        in_=PF[qs % 2][:, 0:512], func=AF.Relu))
                    c.op("scalar", lambda E: E.drain())
                    c.op("scalar", lambda E, qs=qs: E.activation(
                        out=AB[:, A_r + (qs % 2) * 1024 + 512:
                               A_r + (qs % 2) * 1024 + 1024],
                        in_=AB[:, A_r + (qs % 2) * 1024:
                               A_r + (qs % 2) * 1024 + 512],
                        func=AF.Square), "sc")
                    r2 = n["sc"]
                    c.w("vector", "sc", r2)
                    c.op("vector", lambda E, qs=qs: E.tensor_tensor(
                        FA[:, F_g:F_g + 512],
                        AB[:, A_r + (qs % 2) * 1024 + 512:
                           A_r + (qs % 2) * 1024 + 1024],
                        PS[qs % 2][:, 0:512], op=ALU.mult), "sv")
                    g_done = n["sv"]
                    g_dones[it] = g_done
                    c.w("scalar", "sv", g_done)
                    c.op("scalar", lambda E: E.activation(
                        out=AB[:, A_gsq:A_gsq + 512], in_=FA[:, F_g:F_g + 512],
                        func=AF.Square, accum_out=cols[:, 147:148]))
                    c.op("scalar", lambda E: E.drain(), "sc")
                    gss = n["sc"]
                    c.w("vector", "sc", gss)
                    c.op("vector", lambda E, qs=qs: E.tensor_tensor(
                        cols[:, 112 + qs:113 + qs], cols[:, 112 + qs:113 + qs],
                        cols[:, 147:148], op=ALU.add))
                    c.w("vector", "sx", t_dmas.get(it - 2, 0))
                    c.op("vector", lambda E, hp=hp, qs=qs: E.tensor_tensor(
                        AB[:, A_t + (qs % 2) * 512:A_t + (qs % 2) * 512 + 512],
                        FA[:, F_g:F_g + 512],
                        FA[:, F_nw + hp * 512:F_nw + (hp + 1) * 512],
                        op=ALU.mult), "sv")
                    t_done = n["sv"]
                    c.op("vector", lambda E, qs=qs: E.tensor_reduce(
                        cols[:, 148:149],
                        AB[:, A_t + (qs % 2) * 512:A_t + (qs % 2) * 512 + 512],
                        axis=AX.X, op=ALU.max, apply_absolute_value=True))
                    vd()
                    c.op("vector", lambda E, qs=qs: E.tensor_tensor(
                        cols[:, 120 + qs:121 + qs], cols[:, 120 + qs:121 + qs],
                        cols[:, 148:149], op=ALU.max), "sv")
                    c.w("sync", "sv", t_done)
                    t_dmas[it] = c.dma("sync", lambda E, hp=hp, qs=qs: E.dma_start(
                        out=bass.AP(gbuf, qs * 128 * HID + hp * 512,
                                    [[HID, 128], [1, 512]]),
                        in_=AB[:, A_t + (qs % 2) * 512:
                               A_t + (qs % 2) * 512 + 512]), "sx")
                w13_loads[hp] = (wl, n["sp"])
            c.barrier()

            vd()
            c.op("vector", lambda E: E.tensor_scalar(
                cols[:, 112:120], cols[:, 112:120], 1.0 / HID, EPS,
                op0=ALU.mult, op1=ALU.add), "sv")
            vd()
            vq = n["sv"]
            c.w("scalar", "sv", vq)
            c.op("scalar", lambda E: E.activation(
                out=cols[:, 112:120], in_=cols[:, 112:120], func=AF.Sqrt), "sc")
            sq2 = n["sc"]
            c.w("vector", "sc", sq2)
            c.op("vector", lambda E: E.reciprocal(
                cols[:, 112:120], cols[:, 112:120]))
            vd()
            c.op("vector", lambda E: E.tensor_tensor(
                cols[:, 120:128], cols[:, 112:120], cols[:, 120:128],
                op=ALU.mult))
            vd()
            c.op("vector", lambda E: E.tensor_scalar(
                cols[:, 120:128], cols[:, 120:128], 1e-5, None, op0=ALU.max))
            vd()
            c.op("vector", lambda E: E.tensor_scalar(
                cols[:, 136:144], cols[:, 120:128], 1.0 / 127.0, None,
                op0=ALU.mult))
            c.op("vector", lambda E: E.reciprocal(
                cols[:, 120:128], cols[:, 120:128]))
            vd()
            c.op("vector", lambda E: E.tensor_scalar(
                cols[:, 120:128], cols[:, 120:128], 127.0, None,
                op0=ALU.mult))
            vd()
            c.op("vector", lambda E: E.tensor_tensor(
                cols[:, 128:136], cols[:, 112:120], cols[:, 120:128],
                op=ALU.mult), "sv")
            vd()
            v_q2c = n["sv"]

            A_iqT = 16384
            A_tin = 22528
            A_iq = 23552
            A_w2 = 24576
            F_y = 0
            F_xps2 = 2048
            F_o1 = 4096

            iq_store = {"v": 0}

            def quant2_and_iqT(qs):
                c.w("vector", "sx", iq_store["v"])
                for hp in range(11):
                    c.w("sync", "sv", n["sv"] if hp >= 2 else v_q2c)
                    tl = c.dma("sync", lambda E, qs=qs, hp=hp: E.dma_start(
                        out=AB[:, A_tin + (hp % 2) * 512:
                               A_tin + (hp % 2) * 512 + 512],
                        in_=bass.AP(gbuf, qs * 128 * HID + hp * 512,
                                    [[HID, 128], [1, 512]])), "sx")
                    c.w("vector", "sx", tl)
                    c.op("vector", lambda E, qs=qs, hp=hp: E.tensor_scalar(
                        FA[:, F_y:F_y + 512],
                        AB[:, A_tin + (hp % 2) * 512:
                           A_tin + (hp % 2) * 512 + 512],
                        cols[:, 128 + qs:129 + qs], None, op0=ALU.mult))
                    c.op("vector", lambda E: E.tensor_scalar(
                        FA[:, F_y + 512:F_y + 1024], FA[:, F_y:F_y + 512],
                        M32, -M32, op0=ALU.add, op1=ALU.add))
                    c.op("vector", lambda E: E.tensor_scalar(
                        FA[:, F_y + 1024:F_y + 1536], FA[:, F_y:F_y + 512],
                        M64, -M64, op0=ALU.add, op1=ALU.add))
                    c.op("vector", lambda E: E.tensor_tensor(
                        FA[:, F_y + 1024:F_y + 1536],
                        FA[:, F_y + 1024:F_y + 1536],
                        FA[:, F_y + 512:F_y + 1024], op=ALU.subtract))
                    c.op("vector", lambda E: E.scalar_tensor_tensor(
                        FA[:, F_y + 512:F_y + 1024],
                        FA[:, F_y + 1024:F_y + 1536], w_col[:],
                        FA[:, F_y + 512:F_y + 1024],
                        op0=ALU.mult, op1=ALU.add))
                    c.op("vector", lambda E: E.tensor_scalar(
                        FA[:, F_y + 512:F_y + 1024], FA[:, F_y + 512:F_y + 1024],
                        -128.0, 127.0, op0=ALU.max, op1=ALU.min))
                    c.op("vector", lambda E, qs=qs, hp=hp: E.tensor_scalar(
                        AB[:, A_iq + (hp % 2) * 512:A_iq + (hp % 2) * 512 + 512],
                        FA[:, F_y + 512:F_y + 1024],
                        cols[:, 136 + qs:137 + qs], None, op0=ALU.mult), "sv")
                    iq_done = n["sv"]
                    for j in range(4):
                        hc = hp * 4 + j
                        c.w("tensor", "sv", iq_done if j == 0 else 0)
                        c.w("tensor", "sv", n["sv"] if hc >= 2 else 0)
                        c.op("tensor", lambda E, hp=hp, j=j: E.transpose(
                            PTT[j % 2][:, 0:128],
                            AB[:, A_iq + (hp % 2) * 512 + j * 128:
                               A_iq + (hp % 2) * 512 + (j + 1) * 128],
                            idn[:]), "sp")
                        tp = n["sp"]
                        c.w("vector", "sp", tp)
                        c.op("vector", lambda E, hc=hc: E.tensor_copy(
                            AB[:, A_iqT + hc * 128:A_iqT + (hc + 1) * 128],
                            PTT[hc % 2][:, 0:128]), "sv")
                return n["sv"]

            for dg in range(4):
                c.w("gpsimd", "sp", n["sp"] if dg > 0 else 0)
                wl = c.dma("gpsimd", lambda E, dg=dg: E.dma_start(
                    out=AB[:, A_w2:A_w2 + 22528],
                    in_=bass.AP(w2T, dg * 512,
                                [[D, 128], [128 * D, 44], [1, 512]])), "sw")
                for qs in range(8):
                    par = qs % 2
                    if dg == 0:
                        iqT_done = quant2_and_iqT(qs)
                        c.w("sync", "sv", iqT_done)
                        iq_store["v"] = c.dma("sync", lambda E, qs=qs: E.dma_start(
                            out=bass.AP(iqbuf, qs * 128,
                                        [[OWN, 128], [128 * OWN, 44], [1, 128]]),
                            in_=AB[:, A_iqT:A_iqT + 5632]), "sx")
                        c.w("tensor", "sv", iqT_done)
                    else:
                        c.w("sync", "sp", n["sp"])
                        c.dma("sync", lambda E, qs=qs: E.dma_start(
                            out=AB[:, A_iqT:A_iqT + 5632],
                            in_=bass.AP(iqbuf, qs * 128,
                                        [[OWN, 128], [128 * OWN, 44], [1, 128]])),
                            "sx")
                        c.w("tensor", "sx", n["sx"])
                    c.w("tensor", "sw", wl)
                    c.w("tensor", "sv",
                        n["sv"] if (dg * 8 + qs) >= 2 or dg > 0 else 0)
                    for hc in range(NHC):
                        c.op("tensor", lambda E, hc=hc, par=par: E.matmul(
                            PF[par][:, 0:512],
                            AB[:, A_iqT + hc * 128:A_iqT + (hc + 1) * 128],
                            AB[:, A_w2 + hc * 512:A_w2 + (hc + 1) * 512],
                            start=(hc == 0), stop=(hc == NHC - 1),
                            skip_group_check=True),
                            "sp" if hc == NHC - 1 else None)
                    w2mm = n["sp"]
                    xl = c.dma("sync", lambda E, qs=qs, dg=dg: E.dma_start(
                        out=FA[:, F_xps2 + (qs % 2) * 512:
                               F_xps2 + (qs % 2) * 512 + 512],
                        in_=bass.AP(xpbuf, qs * 128 * D + dg * 512,
                                    [[D, 128], [1, 512]])), "sx")
                    c.w("vector", "sp", w2mm)
                    c.w("vector", "sx", xl)
                    c.w("vector", "so", max(0, n["so"] - 32))
                    c.op("vector", lambda E, dg=dg, par=par, qs=qs: E.tensor_tensor(
                        FA[:, F_o1 + (qs % 2) * 512:F_o1 + (qs % 2) * 512 + 512],
                        PF[par][:, 0:512],
                        FA[:, F_gb + dg * 512:F_gb + (dg + 1) * 512],
                        op=ALU.mult))
                    c.op("vector", lambda E, qs=qs: E.tensor_tensor(
                        FA[:, F_o1 + (qs % 2) * 512:F_o1 + (qs % 2) * 512 + 512],
                        FA[:, F_o1 + (qs % 2) * 512:F_o1 + (qs % 2) * 512 + 512],
                        FA[:, F_xps2 + (qs % 2) * 512:
                           F_xps2 + (qs % 2) * 512 + 512],
                        op=ALU.add), "sv")
                    o_done = n["sv"]
                    c.w("sync", "sv", o_done)
                    c.dma("sync", lambda E, qs=qs, dg=dg: E.dma_start(
                        out=bass.AP(out, qs * 128 * D + dg * 512,
                                    [[D, 128], [1, 512]]),
                        in_=FA[:, F_o1 + (qs % 2) * 512:
                               F_o1 + (qs % 2) * 512 + 512]), "so")

            for e in ENGS:
                c.w(e, "so", n["so"])

        @block.sync
        def _(E):
            emit(Ctx("sync", E, sems))

        @block.gpsimd
        def _(E):
            emit(Ctx("gpsimd", E, sems))

        @block.tensor
        def _(E):
            emit(Ctx("tensor", E, sems))

        @block.vector
        def _(E):
            emit(Ctx("vector", E, sems))

        @block.scalar
        def _(E):
            emit(Ctx("scalar", E, sems))

    return nc


_NC_CACHE = None
_LAST_RES = None


def kernel(L0_attn_x, L0_attn_wq_codes, L0_attn_wq_scales, L0_attn_bq,
           L0_attn_wk_codes, L0_attn_wk_scales, L0_attn_bk,
           L0_attn_wv_codes, L0_attn_wv_scales, L0_attn_bv,
           L0_attn_wo_codes, L0_attn_wo_scales, L0_attn_bo,
           L0_ffn_w13, L0_ffn_w2, L0_ffn_norm_w, L0_c, L0_wmod):
    global _NC_CACHE
    f = np.float32
    x = np.ascontiguousarray(np.asarray(L0_attn_x, f).reshape(S, D))
    rep = lambda s: np.ascontiguousarray(np.repeat(np.asarray(s, f), 32, axis=0))
    shared = {
        "wq_c": np.ascontiguousarray(np.asarray(L0_attn_wq_codes, f)),
        "wk_c": np.ascontiguousarray(np.asarray(L0_attn_wk_codes, f)),
        "wv_c": np.ascontiguousarray(np.asarray(L0_attn_wv_codes, f)),
        "wo_c": np.ascontiguousarray(np.asarray(L0_attn_wo_codes, f)),
        "wq_s": rep(L0_attn_wq_scales),
        "wk_s": rep(L0_attn_wk_scales),
        "wv_s": rep(L0_attn_wv_scales),
        "wo_s": rep(L0_attn_wo_scales),
        "bq": np.ascontiguousarray(np.asarray(L0_attn_bq, f)),
        "bk": np.ascontiguousarray(np.asarray(L0_attn_bk, f)),
        "bv": np.ascontiguousarray(np.asarray(L0_attn_bv, f)),
        "bo": np.ascontiguousarray(np.asarray(L0_attn_bo, f)),
        "w13T": np.ascontiguousarray(np.asarray(L0_ffn_w13, f).T),
        "w2T": np.ascontiguousarray(np.asarray(L0_ffn_w2, f).T),
        "normw": np.ascontiguousarray(np.asarray(L0_ffn_norm_w, f)),
        "cvec": np.ascontiguousarray(np.asarray(L0_c, f).reshape(1, D)),
        "wmod": np.ascontiguousarray(np.asarray(L0_wmod, f)),
    }
    in_maps = []
    for cix in range(8):
        order = np.r_[cix * OWN:S, 0:cix * OWN]
        m = dict(shared)
        m["xT"] = np.ascontiguousarray(x[order].T)
        m["x_own"] = np.ascontiguousarray(x[cix * OWN:(cix + 1) * OWN])
        in_maps.append(m)

    if _NC_CACHE is None:
        _NC_CACHE = build_nc()
    res = run_bass_kernel_spmd(_NC_CACHE, in_maps, core_ids=list(range(8)))
    global _LAST_RES
    _LAST_RES = res
    o = np.empty((S, D), f)
    for cix in range(8):
        o[cix * OWN:(cix + 1) * OWN] = res.results[cix]["out"]
    return o.reshape(1, S, D)


# version 10 (final)
# speedup vs baseline: 1.5421x; 1.1039x over previous
import numpy as np
from contextlib import ExitStack

import concourse.bass as bass
import concourse.mybir as mybir
from concourse.bass_utils import run_bass_kernel_spmd

F32 = mybir.dt.float32
BF16 = mybir.dt.bfloat16
AF = mybir.ActivationFunctionType
ALU = mybir.AluOpType
AX = mybir.AxisListType

M32 = float(np.float32(2.0 ** 23 + 2.0 ** 22))
M64 = float(2.0 ** 52 + 2.0 ** 51)
INV_SQRT_DH = 1.0 / float(np.sqrt(np.float32(128.0)))
DEBUG_BUFS = False

D = 2048
S = 8192
OWN = 1024
NC_D = D // 128
NH = 16
HID = 5632
NHC = HID // 128
TT = 512
NTT = S // TT
EPS = 1e-5

ENGS = ("sync", "gpsimd", "tensor", "vector", "scalar")
SEMS = ("sx", "sw", "sp", "sv", "sc", "sg", "so", "sb")


class Ctx:

    def __init__(self, eng, E, sems):
        self.eng, self.E, self.sems = eng, E, sems
        self.n = {k: 0 for k in SEMS}

    def w(self, owner, sem, val):
        if val > 0 and self.eng == owner:
            self.E.wait_ge(self.sems[sem], val)

    def op(self, owner, f, sem=None, inc=1):
        if sem is not None:
            self.n[sem] += inc
        if self.eng == owner:
            ins = f(self.E)
            if sem is not None:
                ins.then_inc(self.sems[sem], inc)
        return self.n[sem] if sem else None

    def dma(self, owner, f, sem):
        self.n[sem] += 16
        if self.eng == owner:
            f(self.E).then_inc(self.sems[sem], 16)
        return self.n[sem]

    def barrier(self):
        self.n["sb"] += 5
        tgt = self.n["sb"]
        if self.eng == "sync":
            self.E.wait_ge(self.sems["sx"], self.n["sx"])
            self.E.wait_ge(self.sems["so"], self.n["so"])
            self.E.nop().then_inc(self.sems["sb"], 1)
        elif self.eng == "gpsimd":
            self.E.wait_ge(self.sems["sw"], self.n["sw"])
            self.E.drain()
            self.E.nop().then_inc(self.sems["sb"], 1)
        else:
            self.E.drain()
            self.E.nop().then_inc(self.sems["sb"], 1)
        self.E.wait_ge(self.sems["sb"], tgt)


def build_nc():
    nc = bass.Bass("TRN2", target_bir_lowering=False,
                   detect_race_conditions=False)

    xT = nc.dram_tensor("xT", [D, S], F32, kind="ExternalInput")
    x_own = nc.dram_tensor("x_own", [OWN, D], F32, kind="ExternalInput")
    wq_c = nc.dram_tensor("wq_c", [D, D], F32, kind="ExternalInput")
    wk_c = nc.dram_tensor("wk_c", [D, D], F32, kind="ExternalInput")
    wv_c = nc.dram_tensor("wv_c", [D, D], F32, kind="ExternalInput")
    wo_c = nc.dram_tensor("wo_c", [D, D], F32, kind="ExternalInput")
    wq_s = nc.dram_tensor("wq_s", [D, D], F32, kind="ExternalInput")
    wk_s = nc.dram_tensor("wk_s", [D, D], F32, kind="ExternalInput")
    wv_s = nc.dram_tensor("wv_s", [D, D], F32, kind="ExternalInput")
    wo_s = nc.dram_tensor("wo_s", [D, D], F32, kind="ExternalInput")
    bq = nc.dram_tensor("bq", [D], F32, kind="ExternalInput")
    bk = nc.dram_tensor("bk", [D], F32, kind="ExternalInput")
    bv = nc.dram_tensor("bv", [D], F32, kind="ExternalInput")
    bo = nc.dram_tensor("bo", [D], F32, kind="ExternalInput")
    w13T = nc.dram_tensor("w13T", [D, 2 * HID], F32, kind="ExternalInput")
    w2T = nc.dram_tensor("w2T", [HID, D], F32, kind="ExternalInput")
    normw = nc.dram_tensor("normw", [HID], F32, kind="ExternalInput")
    cvec = nc.dram_tensor("cvec", [1, D], F32, kind="ExternalInput")
    wmod = nc.dram_tensor("wmod", [D, 6 * D], F32, kind="ExternalInput")

    dbg = "ExternalOutput" if DEBUG_BUFS else "Internal"
    out = nc.dram_tensor("out", [OWN, D], F32, kind="ExternalOutput")
    hbuf = nc.dram_tensor("hbuf", [D, S], BF16, kind=dbg)
    ktbuf = nc.dram_tensor("ktbuf", [D, S], BF16, kind=dbg)
    vbuf = nc.dram_tensor("vbuf", [NH, S, 128], BF16, kind=dbg)
    qtbuf = nc.dram_tensor("qtbuf", [D, OWN], BF16, kind=dbg)
    modbuf = nc.dram_tensor("modbuf", [1, 9 * D], F32, kind=dbg)
    xpbuf = nc.dram_tensor("xpbuf", [OWN, D], F32, kind=dbg)
    gbuf = nc.dram_tensor("gbuf", [OWN, HID], BF16, kind=dbg)
    if DEBUG_BUFS:
        h2dbg = nc.dram_tensor("h2dbg", [OWN, D], F32, kind="ExternalOutput")
        xqdbg = nc.dram_tensor("xqdbg", [OWN, D], BF16, kind="ExternalOutput")
    iqbuf = nc.dram_tensor("iqbuf", [HID, OWN], BF16, kind=dbg)

    with ExitStack() as ctx:
        E_ = ctx.enter_context
        idn = E_(nc.sbuf_tensor("idn", [128, 128], BF16))
        ones_cb = E_(nc.sbuf_tensor("ones_cb", [128, 1], BF16))
        ones_rf = E_(nc.sbuf_tensor("ones_rf", [1, 128], F32))
        ones_rb = E_(nc.sbuf_tensor("ones_rb", [1, 128], BF16))
        w_col = E_(nc.sbuf_tensor("w_col", [128, 1], F32))
        cols = E_(nc.sbuf_tensor("cols", [128, 160], F32))
        silu_cb = E_(nc.sbuf_tensor("silu_cb", [128, 16], BF16))
        rows = E_(nc.sbuf_tensor("rows", [1, 4096], F32))
        brow = E_(nc.sbuf_tensor("brow", [1, 2 * D], BF16))
        AB = E_(nc.sbuf_tensor("AB", [128, 57344], BF16))
        FA = E_(nc.sbuf_tensor("FA", [128, 17920], F32))
        p01 = E_(nc.psum_tensor("p01", [128, 1024], F32))
        p23 = E_(nc.psum_tensor("p23", [128, 1024], F32))
        pOb = E_(nc.psum_tensor("pOb", [128, 1536], F32))
        pXb = E_(nc.psum_tensor("pXb", [128, 512], BF16))
        p0 = p01[:, 0:512]
        p1 = p01[:, 512:1024]
        p2 = p23[:, 0:512]
        p3 = p23[:, 512:1024]
        p4 = pOb[:, 0:512]
        pt0 = pXb[:, 0:128]
        pt1 = pXb[:, 128:256]

        sems = {k: E_(nc.semaphore(k)) for k in SEMS}
        block = E_(nc.Block())

        PF = [p0, p1]
        PS = [p2, p3]
        PSC = [p01, p23]
        PTT = [pt0, pt1]

        def emit(c: Ctx):
            n = c.n

            def vd():
                c.op("vector", lambda E: E.drain())

            c.op("gpsimd", lambda E: E.memset(idn[:], 0.0), "sg")
            c.w("gpsimd", "sg", n["sg"])
            c.op("gpsimd", lambda E: E.affine_select(
                out=idn[:], in_=idn[:], compare_op=ALU.not_equal, fill=1.0,
                base=0, pattern=[[-1, 128]], channel_multiplier=1), "sg")
            c.op("gpsimd", lambda E: E.memset(ones_cb[:], 1.0), "sg")
            c.op("gpsimd", lambda E: E.memset(ones_rf[:], 1.0), "sg")
            c.op("gpsimd", lambda E: E.memset(ones_rb[:], 1.0), "sg")
            c.op("gpsimd", lambda E: E.memset(w_col[:], 1.0), "sg")
            g_consts = n["sg"]
            c.w("vector", "sg", g_consts)
            c.op("vector", lambda E: E.tensor_scalar(
                w_col[:], w_col[:], M64, -M64, op0=ALU.add, op1=ALU.add), "sv")
            vd()

            c.dma("gpsimd", lambda E: E.dma_start(
                out=brow[0:1, 0:D], in_=bass.AP(bv, 0, [[D, 1], [1, D]])), "sw")
            c.dma("gpsimd", lambda E: E.dma_start(
                out=brow[0:1, D:2 * D], in_=bass.AP(bo, 0, [[D, 1], [1, D]])), "sw")
            c.dma("sync", lambda E: E.dma_start(
                out=bass.AP(modbuf, 7 * D, [[9 * D, 1], [1, D]]),
                in_=bass.AP(bq, 0, [[D, 1], [1, D]])), "sx")
            c.dma("sync", lambda E: E.dma_start(
                out=bass.AP(modbuf, 8 * D, [[9 * D, 1], [1, D]]),
                in_=bass.AP(bk, 0, [[D, 1], [1, D]])), "sx")
            c.w("sync", "sx", n["sx"])
            c.dma("sync", lambda E: E.dma_start(
                out=cols[:, 32:48],
                in_=bass.AP(modbuf, 7 * D, [[1, 128], [128, 16]]),
                allow_slow_non_contiguous=True), "sx")
            c.dma("sync", lambda E: E.dma_start(
                out=cols[:, 48:64],
                in_=bass.AP(modbuf, 8 * D, [[1, 128], [128, 16]]),
                allow_slow_non_contiguous=True), "sx")

            c.dma("sync", lambda E: E.dma_start(
                out=rows[0:1, 0:D], in_=cvec.ap()), "sx")
            x_c = n["sx"]
            c.w("scalar", "sx", x_c)
            c.op("scalar", lambda E: E.activation(
                out=rows[0:1, D:2 * D], in_=rows[0:1, 0:D], func=AF.Silu), "sc")
            sc_silu = n["sc"]
            c.w("sync", "sc", sc_silu)
            c.dma("sync", lambda E: E.dma_start(
                out=bass.AP(modbuf, 6 * D, [[9 * D, 1], [1, D]]),
                in_=rows[0:1, D:2 * D]), "sx")
            x_silu_st = n["sx"]
            c.w("sync", "sx", x_silu_st)
            c.dma("sync", lambda E: E.dma_start(
                out=cols[:, 80:96], in_=bass.AP(modbuf, 6 * D, [[1, 128], [128, 16]]),
                allow_slow_non_contiguous=True), "sx")
            x_silu_col = n["sx"]


            W_F = 12288
            mod_row = FA[0:1, 0:6 * D]
            PMOD = [p0, p1, p2, p3]
            mod_evac = {}
            wm_loads = {}
            for cg in range(6):
                for dc in range(NC_D):
                    ch = cg * 16 + dc
                    par = ch % 2
                    qeng, qsem = ("sync", "sx") if par == 0 else ("gpsimd", "sw")
                    c.w(qeng, "sp", wm_loads.get(ch - 2, 0))
                    wl = c.dma(qeng, lambda E, cg=cg, dc=dc, par=par: E.dma_start(
                        out=FA[:, W_F + par * 2048:W_F + par * 2048 + 2048],
                        in_=bass.AP(wmod, dc * 128 * 6 * D + cg * 2048,
                                    [[6 * D, 128], [1, 2048]])), qsem)
                    c.w("tensor", qsem, wl)
                    c.w("tensor", "sx", x_silu_col)
                    if dc == 0:
                        c.w("tensor", "sv", mod_evac.get(cg - 1, 0))
                    for i in range(4):
                        c.op("tensor", lambda E, dc=dc, par=par, i=i: E.matmul(
                            PMOD[i][0:1, 0:512], cols[:, 80 + dc:81 + dc],
                            FA[:, W_F + par * 2048 + i * 512:
                               W_F + par * 2048 + (i + 1) * 512],
                            start=(dc == 0), stop=(dc == NC_D - 1),
                            skip_group_check=True),
                            "sp" if i == 3 else None)
                    wm_loads[ch] = n["sp"]
                mm = n["sp"]
                c.w("vector", "sp", mm)
                for i in range(4):
                    c.op("vector", lambda E, cg=cg, i=i: E.tensor_copy(
                        mod_row[0:1, cg * 2048 + i * 512:
                                cg * 2048 + (i + 1) * 512],
                        PMOD[i][0:1, 0:512]), "sv")
                mod_evac[cg] = n["sv"]
            v_mod = n["sv"]

            sa_o, ca_o, ga_o, sb_o, cb_o, gb_o = [i * D for i in range(6)]
            tr = FA[0:1, 6 * D:7 * D]
            tr2 = FA[0:1, 7 * D:8 * D]
            c.w("vector", "sv", v_mod)
            vd()
            for g_o in (ga_o, gb_o):
                c.op("vector", lambda E, g_o=g_o: E.tensor_scalar(
                    tr, mod_row[0:1, g_o:g_o + D], 0.0, None, op0=ALU.is_ge))
                vd()
                c.op("vector", lambda E: E.tensor_scalar(
                    tr, tr, 2.0, -1.0, op0=ALU.mult, op1=ALU.add))
                vd()
                c.op("vector", lambda E, g_o=g_o: E.tensor_scalar(
                    tr2, mod_row[0:1, g_o:g_o + D], -1.0, None, op0=ALU.mult))
                vd()
                c.op("vector", lambda E, g_o=g_o: E.tensor_tensor(
                    mod_row[0:1, g_o:g_o + D], mod_row[0:1, g_o:g_o + D],
                    tr2, op=ALU.max))
                vd()
                c.op("vector", lambda E, g_o=g_o: E.tensor_scalar(
                    mod_row[0:1, g_o:g_o + D], mod_row[0:1, g_o:g_o + D],
                    0.25, None, op0=ALU.max))
                vd()
                c.op("vector", lambda E, g_o=g_o: E.tensor_tensor(
                    mod_row[0:1, g_o:g_o + D], mod_row[0:1, g_o:g_o + D],
                    tr, op=ALU.mult))
                vd()
            c.op("vector", lambda E: E.tensor_scalar(
                mod_row[0:1, cb_o:cb_o + D], mod_row[0:1, cb_o:cb_o + D],
                1.0, None, op0=ALU.add))
            vd()
            c.op("vector", lambda E: E.tensor_scalar(
                mod_row[0:1, ca_o:ca_o + D], mod_row[0:1, ca_o:ca_o + D],
                1.0, None, op0=ALU.add), "sv")
            vd()
            v_modt = n["sv"]
            c.w("sync", "sv", v_modt)
            for i, g_o in enumerate((ga_o, cb_o, sb_o, gb_o, ca_o, sa_o)):
                c.dma("sync", lambda E, i=i, g_o=g_o: E.dma_start(
                    out=bass.AP(modbuf, i * D, [[9 * D, 1], [1, D]]),
                    in_=mod_row[0:1, g_o:g_o + D]), "sx")
            x_modst = n["sx"]
            c.w("sync", "sx", x_modst)
            c.dma("sync", lambda E: E.dma_start(
                out=cols[:, 0:16], in_=bass.AP(modbuf, 4 * D, [[1, 128], [128, 16]]),
                allow_slow_non_contiguous=True), "sx")
            c.dma("sync", lambda E: E.dma_start(
                out=cols[:, 16:32], in_=bass.AP(modbuf, 5 * D, [[1, 128], [128, 16]]),
                allow_slow_non_contiguous=True), "sx")
            c.op("vector", lambda E: E.memset(rows[0:1, 3072:3104], 0.0), "sv")
            c.barrier()

            def dequant(codes, scales, stage=32768, dbl=True):
                for dc in range(NC_D):
                    so_ = stage + ((dc % 2) * 4096 if dbl else 0)
                    c.w("gpsimd", "sv", n["sv"] if dc >= (2 if dbl else 1) else 0)
                    c.dma("gpsimd", lambda E, dc=dc, so_=so_: E.dma_start(
                        out=AB[:, so_:so_ + 2048],
                        in_=bass.AP(codes, dc * 128 * D, [[D, 128], [1, D]])),
                        "sw")
                    sw2 = c.dma("gpsimd", lambda E, dc=dc, so_=so_: E.dma_start(
                        out=AB[:, so_ + 2048:so_ + 4096],
                        in_=bass.AP(scales, dc * 128 * D, [[D, 128], [1, D]])),
                        "sw")
                    c.w("vector", "sw", sw2)
                    c.op("vector", lambda E, dc=dc, so_=so_: E.tensor_tensor(
                        AB[:, dc * D:(dc + 1) * D],
                        AB[:, so_:so_ + 2048],
                        AB[:, so_ + 2048:so_ + 4096],
                        op=ALU.mult), "sv")
                return n["sv"]

            A_xbf = 32768
            A_k2 = 34816
            A_bb = 35840
            A_h0 = 36864
            A_kt = 53248
            F_xT = 0
            F_h1 = 16896
            R_mean, R_ss = 0, 512

            def h_and_proj(w_codes, w_scales, tiles, kt_is_q, norm_row_off):
                v_deq = dequant(w_codes, w_scales)
                nt = len(tiles)
                xl = {}
                stats_mm = {}
                comb_rd = {}
                vab = {}
                bc_mm = {}
                bcopy = {}
                h_done = {}
                h_dma = {}
                proj_mm = {}
                kt_dma = {}

                def pre(j):
                    t = tiles[j]
                    if j == 0:
                        c.w("vector", "sv", v_deq)
                        c.w("scalar", "sv", v_deq)
                        c.w("gpsimd", "sv", v_deq)
                    c.w("gpsimd", "sv", h_done.get(j - 2, 0))
                    xl[j] = c.dma("gpsimd", lambda E, t=t, j=j: E.dma_start(
                        out=FA[:, F_xT + (j % 2) * 8192:
                               F_xT + (j % 2) * 8192 + 8192],
                        in_=bass.AP(xT, t * TT,
                                    [[S, 128], [128 * S, 16], [1, TT]])), "sw")
                    xb = F_xT + (j % 2) * 8192
                    pgate = stats_mm.get(j - 1, 0)
                    sbase = n["sp"]
                    for dc in range(NC_D):
                        par = dc % 2
                        c.w("vector", "sw", xl[j])
                        c.w("vector", "sp",
                            sbase + dc - 1 if dc >= 2 else pgate)
                        c.op("vector", lambda E, dc=dc, par=par, xb=xb: E.tensor_copy(
                            AB[:, A_xbf + par * 1024:A_xbf + par * 1024 + 512],
                            FA[:, xb + dc * 512:xb + (dc + 1) * 512]), "sv")
                        cast_v = n["sv"]
                        c.w("scalar", "sw", xl[j])
                        c.w("scalar", "sp",
                            sbase + dc - 1 if dc >= 2 else pgate)
                        c.op("scalar", lambda E, dc=dc, par=par, xb=xb: E.activation(
                            out=AB[:, A_xbf + par * 1024 + 512:
                                   A_xbf + par * 1024 + 1024],
                            in_=FA[:, xb + dc * 512:xb + (dc + 1) * 512],
                            func=AF.Square), "sc")
                        sq_v = n["sc"]
                        c.w("tensor", "sv", cast_v)
                        c.w("tensor", "sc", sq_v)
                        if dc == 0:
                            c.w("tensor", "sv", comb_rd.get(j - 1, 0))
                        c.op("tensor", lambda E, dc=dc, par=par: E.matmul(
                            p2[0:1, 0:512], ones_cb[:],
                            AB[:, A_xbf + par * 1024:A_xbf + par * 1024 + 512],
                            start=(dc == 0), stop=(dc == NC_D - 1),
                            skip_group_check=True))
                        c.op("tensor", lambda E, dc=dc, par=par: E.matmul(
                            p3[0:1, 0:512], ones_cb[:],
                            AB[:, A_xbf + par * 1024 + 512:
                               A_xbf + par * 1024 + 1024],
                            start=(dc == 0), stop=(dc == NC_D - 1),
                            skip_group_check=True), "sp")
                    stats_mm[j] = n["sp"]
                    RA = 1024 + (j % 2) * 1024
                    RB = RA + 512
                    c.w("vector", "sp", stats_mm[j])
                    c.w("vector", "sp", bc_mm.get(j - 2, 0))
                    c.op("vector", lambda E: E.tensor_scalar(
                        rows[0:1, R_mean:R_mean + 512], p2[0:1, 0:512],
                        1.0 / D, None, op0=ALU.mult))
                    c.op("vector", lambda E: E.tensor_scalar(
                        rows[0:1, R_ss:R_ss + 512], p3[0:1, 0:512],
                        1.0 / D, None, op0=ALU.mult), "sv")
                    comb_rd[j] = n["sv"]
                    vd()
                    c.op("vector", lambda E, RA=RA: E.tensor_tensor(
                        rows[0:1, RA:RA + 512], rows[0:1, R_mean:R_mean + 512],
                        rows[0:1, R_mean:R_mean + 512], op=ALU.mult))
                    vd()
                    c.op("vector", lambda E, RA=RA: E.tensor_tensor(
                        rows[0:1, RA:RA + 512], rows[0:1, R_ss:R_ss + 512],
                        rows[0:1, RA:RA + 512], op=ALU.subtract))
                    vd()
                    c.op("vector", lambda E, RA=RA: E.tensor_scalar(
                        rows[0:1, RA:RA + 512], rows[0:1, RA:RA + 512],
                        EPS, None, op0=ALU.add), "sv")
                    vd()
                    v_var = n["sv"]
                    c.w("scalar", "sv", v_var)
                    c.op("scalar", lambda E, RA=RA: E.activation(
                        out=rows[0:1, RA:RA + 512],
                        in_=rows[0:1, RA:RA + 512], func=AF.Sqrt), "sc")
                    sc_sq = n["sc"]
                    c.w("vector", "sc", sc_sq)
                    c.op("vector", lambda E, RA=RA: E.reciprocal(
                        rows[0:1, RA:RA + 512], rows[0:1, RA:RA + 512]))
                    vd()
                    c.op("vector", lambda E, RA=RA, RB=RB: E.scalar_tensor_tensor(
                        rows[0:1, RB:RB + 512], rows[0:1, R_mean:R_mean + 512],
                        -1.0, rows[0:1, RA:RA + 512],
                        op0=ALU.mult, op1=ALU.mult), "sv")
                    vd()
                    vab[j] = n["sv"]

                def hstage(j):
                    xb = F_xT + (j % 2) * 8192
                    RA = 1024 + (j % 2) * 1024
                    RB = RA + 512
                    bAo = (j % 2) * 512
                    bbs = A_bb + (j % 2) * 512
                    hb = A_h0 + (j % 2) * 8192
                    c.w("tensor", "sv", vab[j])
                    c.w("tensor", "sv", h_done.get(j - 2, 0))
                    c.w("tensor", "sv", bcopy.get(j - 1, 0))
                    c.op("tensor", lambda E, RA=RA, bAo=bAo: E.matmul(
                        pOb[:, bAo:bAo + 512], ones_rf[:],
                        rows[0:1, RA:RA + 512],
                        start=True, stop=True, skip_group_check=True))
                    c.op("tensor", lambda E, RB=RB: E.matmul(
                        pOb[:, 1024:1536], ones_rf[:], rows[0:1, RB:RB + 512],
                        start=True, stop=True, skip_group_check=True), "sp")
                    bc_mm[j] = n["sp"]
                    c.w("vector", "sp", bc_mm[j])
                    c.op("vector", lambda E, bbs=bbs: E.tensor_copy(
                        AB[:, bbs:bbs + 512], pOb[:, 1024:1536]), "sv")
                    bcopy[j] = n["sv"]
                    c.w("vector", "sx", h_dma.get(j - 2, 0))
                    c.w("vector", "sp", proj_mm.get(j - 2, 0))
                    for dc in range(NC_D):
                        c.op("vector", lambda E, dc=dc, xb=xb, bAo=bAo: E.tensor_tensor(
                            FA[:, F_h1:F_h1 + 512],
                            FA[:, xb + dc * 512:xb + (dc + 1) * 512],
                            pOb[:, bAo:bAo + 512], op=ALU.mult))
                        c.op("vector", lambda E, bbs=bbs: E.tensor_tensor(
                            FA[:, F_h1:F_h1 + 512], FA[:, F_h1:F_h1 + 512],
                            AB[:, bbs:bbs + 512], op=ALU.add))
                        c.op("vector", lambda E, dc=dc, hb=hb: E.tensor_scalar(
                            AB[:, hb + dc * 512:hb + (dc + 1) * 512],
                            FA[:, F_h1:F_h1 + 512],
                            cols[:, dc:dc + 1], cols[:, 16 + dc:17 + dc],
                            op0=ALU.mult, op1=ALU.add),
                            "sv" if dc == NC_D - 1 else None)
                    h_done[j] = n["sv"]
                    if not kt_is_q:
                        c.w("sync", "sv", h_done[j])
                        h_dma[j] = c.dma("sync", lambda E, j=j, hb=hb: E.dma_start(
                            out=bass.AP(hbuf, tiles[j] * TT,
                                        [[S, 128], [128 * S, 16], [1, TT]]),
                            in_=AB[:, hb:hb + 8192]), "sx")

                ev_ct = {}
                k2v = {}
                nmm = {}
                rmax_done = {}

                def proj(idx):
                    t = tiles[idx]
                    hb = A_h0 + (idx % 2) * 8192

                    def norm_mm(fc):
                        c.w("tensor", "sv", k2v[(idx, fc)])
                        c.w("tensor", "sv",
                            rmax_done.get((idx, fc - 2),
                                          rmax_done.get((idx - 1, fc + 14), 0)))
                        po = 32 + (fc % 2) * 32
                        c.op("tensor", lambda E, fc=fc, po=po: E.matmul(
                            p3[po:po + 1, 0:512], ones_cb[:],
                            AB[:, A_k2 + (fc % 2) * 512:
                               A_k2 + (fc % 2) * 512 + 512],
                            start=True, stop=True, skip_group_check=True), "sp")
                        nmm[(idx, fc)] = n["sp"]
                        c.w("vector", "sp", nmm[(idx, fc)])
                        c.op("vector", lambda E, po=po: E.reduce_max(
                            rows[0:1, 3120:3121], p3[po:po + 1, 0:512],
                            axis=AX.X))
                        vd()
                        c.op("vector", lambda E, fc=fc: E.tensor_tensor(
                            rows[0:1, norm_row_off + fc:norm_row_off + fc + 1],
                            rows[0:1, norm_row_off + fc:norm_row_off + fc + 1],
                            rows[0:1, 3120:3121], op=ALU.max), "sv")
                        rmax_done[(idx, fc)] = n["sv"]

                    for fc in range(NC_D):
                        par = fc % 2
                        sub = fc // 4
                        subg = idx * 4 + sub
                        c.w("tensor", "sv", h_done[idx] if fc == 0 else 0)
                        c.w("tensor", "sv",
                            v_deq if fc == 0 and idx == 0 else 0)
                        c.w("tensor", "sc",
                            ev_ct.get((idx, fc - 2),
                                      ev_ct.get((idx - 1, fc + 14), 0)))
                        for dc in range(NC_D):
                            c.op("tensor", lambda E, fc=fc, dc=dc, par=par, hb=hb: E.matmul(
                                PF[par][:, 0:512],
                                AB[:, dc * D + fc * 128:dc * D + fc * 128 + 128],
                                AB[:, hb + dc * 512:hb + (dc + 1) * 512],
                                start=(dc == 0), stop=(dc == NC_D - 1),
                                skip_group_check=True),
                                "sp" if dc == NC_D - 1 else None)
                        pj = n["sp"]
                        if fc >= 1:
                            norm_mm(fc - 1)
                        bias = (cols[:, 32 + fc:33 + fc] if kt_is_q
                                else cols[:, 48 + fc:49 + fc])
                        ktslot = A_kt + (sub % 2) * 2048 + (fc % 4) * 512
                        c.w("scalar", "sp", pj)
                        c.w("scalar", "sv",
                            k2v.get((idx, fc - 8),
                                    k2v.get((idx - 1, fc + 8), 0)))
                        if fc % 4 == 0:
                            c.w("scalar", "sx", kt_dma.get(subg - 2, 0))
                        c.op("scalar", lambda E, par=par, bias=bias, ktslot=ktslot: E.activation(
                            out=AB[:, ktslot:ktslot + 512],
                            in_=PF[par][:, 0:512], func=AF.Identity,
                            bias=bias), "sc")
                        ev_ct[(idx, fc)] = n["sc"]
                        c.w("vector", "sc", ev_ct[(idx, fc)])
                        c.w("vector", "sp",
                            nmm.get((idx, fc - 2),
                                    nmm.get((idx - 1, fc + 14), 0)))
                        c.op("vector", lambda E, fc=fc, ktslot=ktslot: E.tensor_tensor(
                            AB[:, A_k2 + (fc % 2) * 512:
                               A_k2 + (fc % 2) * 512 + 512],
                            AB[:, ktslot:ktslot + 512],
                            AB[:, ktslot:ktslot + 512],
                            op=ALU.mult), "sv")
                        k2v[(idx, fc)] = n["sv"]
                        if fc % 4 == 3:
                            c.op("scalar", lambda E: E.drain(), "sc")
                            c.w("sync", "sc", n["sc"])
                            if kt_is_q:
                                dst = bass.AP(
                                    qtbuf, t * TT + sub * 4 * 128 * OWN,
                                    [[OWN, 128], [128 * OWN, 4], [1, TT]])
                            else:
                                dst = bass.AP(
                                    ktbuf, t * TT + sub * 4 * 128 * S,
                                    [[S, 128], [128 * S, 4], [1, TT]])
                            kt_dma[subg] = c.dma(
                                "sync", lambda E, dst=dst, sub=sub: E.dma_start(
                                    out=dst,
                                    in_=AB[:, A_kt + (sub % 2) * 2048:
                                           A_kt + (sub % 2) * 2048 + 2048]),
                                "sx")
                    norm_mm(NC_D - 1)
                    proj_mm[idx] = n["sp"]

                pre(0)
                if nt > 1:
                    pre(1)
                hstage(0)
                for idx in range(nt):
                    if idx + 1 < nt:
                        hstage(idx + 1)
                    proj(idx)
                    if idx + 2 < nt:
                        pre(idx + 2)

            h_and_proj(wq_c, wq_s, [0, 1], True, 3088)
            c.barrier()
            h_and_proj(wk_c, wk_s, list(range(NTT)), False, 3072)
            c.barrier()

            v_deq = dequant(wv_c, wv_s)
            A_vsb = A_kt
            bvr = brow[0:1, 0:D]
            vdma = {}
            proj_v = {}
            xlv = {}

            def vload(t):
                hb = A_h0 + (t % 2) * 8192
                c.w("gpsimd", "sv", v_deq if t < 2 else 0)
                c.w("gpsimd", "sp", proj_v.get(t - 2, 0))
                xlv[t] = c.dma("gpsimd", lambda E, t=t, hb=hb: E.dma_start(
                    out=AB[:, hb:hb + 8192],
                    in_=bass.AP(hbuf, t * TT,
                                [[S, 128], [128 * S, 16], [1, TT]])), "sw")

            def vcomp(t):
                hb = A_h0 + (t % 2) * 8192
                for ts4 in range(4):
                    vpar = ts4 % 2
                    for fg in range(4):
                        par = (ts4 * 4 + fg) % 2
                        c.w("tensor", "sw", xlv[t])
                        c.w("tensor", "sv", v_deq)
                        c.w("tensor", "sc",
                            n["sc"] if (ts4 * 4 + fg) >= 2 else 0)
                        for dc in range(NC_D):
                            c.op("tensor", lambda E, ts4=ts4, fg=fg, dc=dc, par=par, hb=hb: E.matmul(
                                PF[par][:, 0:512],
                                AB[:, hb + dc * 512 + ts4 * 128:
                                   hb + dc * 512 + ts4 * 128 + 128],
                                AB[:, dc * D + fg * 512:dc * D + (fg + 1) * 512],
                                start=(dc == 0), stop=False,
                                skip_group_check=True))
                        c.op("tensor", lambda E, fg=fg, par=par: E.matmul(
                            PF[par][:, 0:512], ones_rb[:],
                            bvr[0:1, fg * 512:(fg + 1) * 512],
                            start=False, stop=True, skip_group_check=True), "sp")
                        vmm = n["sp"]
                        c.w("scalar", "sp", vmm)
                        c.w("scalar", "sx",
                            vdma.get((t, ts4 - 2), vdma.get((t - 1, ts4 + 2), 0))
                            if fg == 0 else 0)
                        c.op("scalar", lambda E, fg=fg, par=par, vpar=vpar: E.activation(
                            out=AB[:, A_vsb + vpar * 2048 + fg * 512:
                                   A_vsb + vpar * 2048 + (fg + 1) * 512],
                            in_=PF[par][:, 0:512], func=AF.Identity), "sc")
                    c.op("scalar", lambda E: E.drain(), "sc")
                    ev = n["sc"]
                    c.w("sync", "sc", ev)
                    vdma[(t, ts4)] = c.dma(
                        "sync", lambda E, t=t, ts4=ts4, vpar=vpar: E.dma_start(
                            out=bass.AP(vbuf, (t * TT + ts4 * 128) * 128,
                                        [[128, 128], [S * 128, 16], [1, 128]]),
                            in_=bass.AP(AB, A_vsb + vpar * 2048,
                                        [[57344, 128], [512, 4], [1, 512]])),
                        "sx")
                proj_v[t] = n["sp"]

            vload(0)
            vload(1)
            for t in range(NTT):
                vcomp(t)
                if t + 2 < NTT:
                    vload(t + 2)
            c.barrier()

            vd()
            c.op("vector", lambda E: E.tensor_tensor(
                rows[0:1, 3104:3120], rows[0:1, 3088:3104],
                rows[0:1, 3072:3088], op=ALU.mult), "sv")
            vd()
            v_c1 = n["sv"]
            c.w("scalar", "sv", v_c1)
            c.op("scalar", lambda E: E.activation(
                out=rows[0:1, 3104:3120], in_=rows[0:1, 3104:3120],
                func=AF.Sqrt), "sc")
            sc_c = n["sc"]
            c.w("vector", "sc", sc_c)
            c.op("vector", lambda E: E.tensor_scalar(
                rows[0:1, 3104:3120], rows[0:1, 3104:3120],
                -INV_SQRT_DH, None, op0=ALU.mult), "sv")
            vd()
            v_c2 = n["sv"]
            c.w("tensor", "sv", v_c2)
            c.op("tensor", lambda E: E.matmul(
                p4[:, 0:16], ones_rf[:], rows[0:1, 3104:3120],
                start=True, stop=True, skip_group_check=True), "sp")
            pc = n["sp"]
            c.w("vector", "sp", pc)
            c.op("vector", lambda E: E.tensor_copy(cols[:, 64:80], p4[:, 0:16]),
                 "sv")
            v_negc = n["sv"]

            A_vaug = 0
            A_qt = 16512
            A_pt = 18560
            A_kth = 20608
            A_oT = 36992
            A_obf = 53376

            c.w("scalar", "sv", v_negc)
            c.w("vector", "sv", v_negc)
            c.op("vector", lambda E: E.memset(
                bass.AP(AB, A_vaug + 128, [[57344, 128], [129, 128]]), 1.0),
                "sv")
            v_ones = n["sv"]
            c.w("sync", "sv", v_ones)

            head_mm_done = {}
            head_evac_done = 0
            for h in range(NH):
                par = h % 2
                c.w("sync", "sp", head_mm_done.get(h - 2, 0))
                c.dma("sync", lambda E, h=h, par=par: E.dma_start(
                    out=AB[:, A_kth + par * 8192:A_kth + par * 8192 + 8192],
                    in_=bass.AP(ktbuf, h * 128 * S, [[S, 128], [1, S]])), "sx")
                c.dma("sync", lambda E, h=h, par=par: E.dma_start(
                    out=bass.AP(AB, A_vaug + par * 8256,
                                [[57344, 128], [129, 64], [1, 128]]),
                    in_=bass.AP(vbuf, h * S * 128,
                                [[128, 128], [128 * 128, 64], [1, 128]])), "sx")
                lx = c.dma("sync", lambda E, h=h, par=par: E.dma_start(
                    out=AB[:, A_qt + par * 1024:A_qt + par * 1024 + 1024],
                    in_=bass.AP(qtbuf, h * 128 * OWN, [[OWN, 128], [1, OWN]])),
                    "sx")
                exp_base = n["sc"]
                sc_mm = {}
                av = {}

                def scores_mm(kt, par=par, lx=lx, exp_base=exp_base):
                    spar = kt % 2
                    c.w("tensor", "sx", lx)
                    c.w("tensor", "sc", max(0, exp_base + kt - 1))
                    for qh in range(2):
                        c.op("tensor", lambda E, kt=kt, qh=qh, par=par, spar=spar: E.matmul(
                            PSC[spar][:, qh * 512:(qh + 1) * 512],
                            AB[:, A_kth + par * 8192 + kt * 128:
                               A_kth + par * 8192 + (kt + 1) * 128],
                            AB[:, A_qt + par * 1024 + qh * 512:
                               A_qt + par * 1024 + (qh + 1) * 512],
                            start=True, stop=True, skip_group_check=True),
                            "sp" if qh == 1 else None)
                    return n["sp"]

                sc_mm[0] = scores_mm(0)
                for kt in range(64):
                    spar = kt % 2
                    if kt < 63:
                        sc_mm[kt + 1] = scores_mm(kt + 1)
                    c.w("scalar", "sp", sc_mm[kt])
                    c.op("scalar", lambda E, kt=kt, h=h, spar=spar: E.activation(
                        out=AB[:, A_pt + spar * 1024:A_pt + spar * 1024 + 1024],
                        in_=PSC[spar][:, 0:1024], func=AF.Exp,
                        bias=cols[:, 64 + h:65 + h], scale=INV_SQRT_DH), "sc")
                    ex = n["sc"]
                    c.w("tensor", "sc", ex)
                    if kt == 0:
                        c.w("tensor", "sv", head_evac_done)
                    for qs in range(8):
                        ob = (qs // 3) * 512 + (qs % 3) * 132
                        c.op("tensor", lambda E, kt=kt, qs=qs, ob=ob, par=par, spar=spar: E.matmul(
                            pOb[:, ob:ob + 129],
                            AB[:, A_pt + spar * 1024 + qs * 128:
                               A_pt + spar * 1024 + (qs + 1) * 128],
                            bass.AP(AB, A_vaug + par * 8256 + kt * 129,
                                    [[57344, 128], [1, 129]]),
                            start=(kt == 0), stop=(kt == 63),
                            skip_group_check=True),
                            "sp" if qs == 7 else None)
                    av[kt] = n["sp"]
                head_mm_done[h] = n["sp"]
                mmall = n["sp"]
                for qs_g in range(8):
                    ob = (qs_g // 3) * 512 + (qs_g % 3) * 132
                    c.w("vector", "sp", mmall)
                    c.op("vector", lambda E, ob=ob: E.reciprocal(
                        cols[:, 144:145], pOb[:, ob + 128:ob + 129]))
                    vd()
                    c.op("vector", lambda E, ob=ob, qs_g=qs_g: E.tensor_scalar(
                        AB[:, A_obf + (qs_g % 2) * 128:
                           A_obf + (qs_g % 2) * 128 + 128],
                        pOb[:, ob:ob + 128], cols[:, 144:145], None,
                        op0=ALU.mult), "sv")
                    v_ev = n["sv"]
                    c.w("tensor", "sv", v_ev)
                    c.op("tensor", lambda E, qs_g=qs_g: E.transpose(
                        PTT[qs_g % 2][:, 0:128],
                        AB[:, A_obf + (qs_g % 2) * 128:
                           A_obf + (qs_g % 2) * 128 + 128], idn[:]), "sp")
                    tp = n["sp"]
                    c.w("vector", "sp", tp)
                    c.op("vector", lambda E, h=h, qs_g=qs_g: E.tensor_copy(
                        AB[:, A_oT + h * 1024 + qs_g * 128:
                           A_oT + h * 1024 + (qs_g + 1) * 128],
                        PTT[qs_g % 2][:, 0:128]), "sv")
                head_evac_done = n["sv"]
            c.barrier()

            v_deq = dequant(wo_c, wo_s, stage=32768, dbl=False)
            F_ga = 0
            F_xo = 2048
            F_t1 = 6144
            F_xp = 6656
            c.dma("sync", lambda E: E.dma_start(
                out=rows[0:1, 0:D], in_=bass.AP(modbuf, 0, [[9 * D, 1], [1, D]])),
                "sx")
            x_ga = n["sx"]
            for i in range(4):
                c.w("tensor", "sx", x_ga)
                c.w("tensor", "sv", n["sv"] if i >= 1 else 0)
                c.op("tensor", lambda E, i=i: E.matmul(
                    p4[:, 0:512], ones_rf[:], rows[0:1, i * 512:(i + 1) * 512],
                    start=True, stop=True, skip_group_check=True), "sp")
                bm = n["sp"]
                c.w("vector", "sp", bm)
                c.op("vector", lambda E, i=i: E.tensor_copy(
                    FA[:, F_ga + i * 512:F_ga + (i + 1) * 512], p4[:, 0:512]),
                    "sv")
            vd()
            v_ga = n["sv"]

            bor = brow[0:1, D:2 * D]
            xp_dmas = {}
            evdone = {}
            for qs in range(8):
                c.w("sync", "sv", evdone.get(qs - 2, v_ga))
                xo = c.dma("sync", lambda E, qs=qs: E.dma_start(
                    out=FA[:, F_xo + (qs % 2) * 2048:
                           F_xo + (qs % 2) * 2048 + 2048],
                    in_=bass.AP(x_own, qs * 128 * D, [[D, 128], [1, D]])), "sx")
                for dg in range(4):
                    par = (qs * 4 + dg) % 2
                    c.w("tensor", "sv", v_deq if qs == 0 and dg == 0 else 0)
                    c.w("tensor", "sv",
                        n["sv"] if (qs * 4 + dg) >= 2 else 0)
                    for fc in range(NC_D):
                        c.op("tensor", lambda E, qs=qs, dg=dg, fc=fc, par=par: E.matmul(
                            PF[par][:, 0:512],
                            AB[:, A_oT + fc * 1024 + qs * 128:
                               A_oT + fc * 1024 + (qs + 1) * 128],
                            AB[:, fc * D + dg * 512:fc * D + (dg + 1) * 512],
                            start=(fc == 0), stop=False,
                            skip_group_check=True))
                    c.op("tensor", lambda E, dg=dg, par=par: E.matmul(
                        PF[par][:, 0:512], ones_rb[:],
                        bor[0:1, dg * 512:(dg + 1) * 512],
                        start=False, stop=True, skip_group_check=True), "sp")
                    omm = n["sp"]
                    c.w("vector", "sp", omm)
                    c.w("vector", "sx", xo)
                    c.w("vector", "sx", xp_dmas.get(qs - 2, 0))
                    c.op("vector", lambda E, dg=dg, par=par: E.tensor_tensor(
                        FA[:, F_t1:F_t1 + 512], PF[par][:, 0:512],
                        FA[:, F_ga + dg * 512:F_ga + (dg + 1) * 512],
                        op=ALU.mult))
                    c.op("vector", lambda E, qs=qs, dg=dg: E.tensor_tensor(
                        FA[:, F_xp + (qs % 2) * 2048 + dg * 512:
                           F_xp + (qs % 2) * 2048 + (dg + 1) * 512],
                        FA[:, F_t1:F_t1 + 512],
                        FA[:, F_xo + (qs % 2) * 2048 + dg * 512:
                           F_xo + (qs % 2) * 2048 + (dg + 1) * 512],
                        op=ALU.add), "sv")
                evdone[qs] = n["sv"]
                c.w("sync", "sv", evdone[qs])
                xp_dmas[qs] = c.dma("sync", lambda E, qs=qs: E.dma_start(
                    out=bass.AP(xpbuf, qs * 128 * D, [[D, 128], [1, D]]),
                    in_=FA[:, F_xp + (qs % 2) * 2048:
                           F_xp + (qs % 2) * 2048 + 2048]), "sx")
            c.barrier()

            F_C2 = 0
            F_sb = 2048
            F_xps = 4096
            F_xn = 8192
            F_q1 = 10240
            F_q2 = 12288
            A_xq = 32768
            A_xqT = 40960
            for slot, foff in ((1, F_C2), (2, F_sb)):
                c.dma("sync", lambda E, slot=slot: E.dma_start(
                    out=rows[0:1, 0:D],
                    in_=bass.AP(modbuf, slot * D, [[9 * D, 1], [1, D]])), "sx")
                xg = n["sx"]
                for j in range(4):
                    c.w("tensor", "sx", xg)
                    c.w("tensor", "sv", n["sv"] if j >= 1 or slot == 2 else 0)
                    c.op("tensor", lambda E, j=j: E.matmul(
                        p4[:, 0:512], ones_rf[:],
                        rows[0:1, j * 512:(j + 1) * 512],
                        start=True, stop=True, skip_group_check=True), "sp")
                    bm = n["sp"]
                    c.w("vector", "sp", bm)
                    c.op("vector", lambda E, j=j, foff=foff: E.tensor_copy(
                        FA[:, foff + j * 512:foff + (j + 1) * 512],
                        p4[:, 0:512]), "sv")
                c.w("sync", "sv", n["sv"])
            vd()
            v_bc = n["sv"]

            xn_dones = {}
            for qs in range(8):
                c.w("sync", "sc", xn_dones.get(qs - 2, 0))
                c.w("sync", "sv", v_bc if qs == 0 else 0)
                xl = c.dma("sync", lambda E, qs=qs: E.dma_start(
                    out=FA[:, F_xps + (qs % 2) * 2048:
                           F_xps + (qs % 2) * 2048 + 2048],
                    in_=bass.AP(xpbuf, qs * 128 * D, [[D, 128], [1, D]])), "sx")
                xb = F_xps + (qs % 2) * 2048
                c.w("scalar", "sx", xl)
                c.w("scalar", "sp", n["sp"])
                c.op("scalar", lambda E, xb=xb: E.activation(
                    out=AB[:, A_xq:A_xq + 2048], in_=FA[:, xb:xb + 2048],
                    func=AF.Identity, accum_out=cols[:, 96:97]))
                c.op("scalar", lambda E, xb=xb: E.activation(
                    out=AB[:, A_xq:A_xq + 2048], in_=FA[:, xb:xb + 2048],
                    func=AF.Square, accum_out=cols[:, 104:105]))
                c.op("scalar", lambda E: E.drain(), "sc")
                st = n["sc"]
                c.w("vector", "sc", st)
                c.op("vector", lambda E: E.tensor_scalar(
                    cols[:, 96:97], cols[:, 96:97], 1.0 / D, None,
                    op0=ALU.mult))
                c.op("vector", lambda E: E.tensor_scalar(
                    cols[:, 104:105], cols[:, 104:105], 1.0 / D, None,
                    op0=ALU.mult))
                vd()
                c.op("vector", lambda E: E.tensor_tensor(
                    cols[:, 145:146], cols[:, 96:97], cols[:, 96:97],
                    op=ALU.mult))
                vd()
                c.op("vector", lambda E: E.tensor_tensor(
                    cols[:, 104:105], cols[:, 104:105], cols[:, 145:146],
                    op=ALU.subtract))
                vd()
                c.op("vector", lambda E: E.tensor_scalar(
                    cols[:, 104:105], cols[:, 104:105], EPS, None,
                    op0=ALU.add), "sv")
                vd()
                vv = n["sv"]
                c.w("scalar", "sv", vv)
                c.op("scalar", lambda E: E.activation(
                    out=cols[:, 104:105], in_=cols[:, 104:105],
                    func=AF.Sqrt), "sc")
                sq = n["sc"]
                c.w("vector", "sc", sq)
                c.op("vector", lambda E: E.reciprocal(
                    cols[:, 104:105], cols[:, 104:105]))
                vd()
                c.op("vector", lambda E: E.scalar_tensor_tensor(
                    cols[:, 96:97], cols[:, 96:97], -1.0, cols[:, 104:105],
                    op0=ALU.mult, op1=ALU.mult), "sv")
                vd()
                ab2 = n["sv"]
                c.w("scalar", "sv", ab2)
                c.op("scalar", lambda E, xb=xb: E.activation(
                    out=FA[:, F_xn:F_xn + 2048], in_=FA[:, xb:xb + 2048],
                    func=AF.Identity, bias=cols[:, 96:97],
                    scale=cols[:, 104:105]), "sc")
                xn_done = n["sc"]
                xn_dones[qs] = xn_done
                c.w("vector", "sc", xn_done)
                c.op("vector", lambda E: E.tensor_tensor(
                    FA[:, F_xn:F_xn + 2048], FA[:, F_xn:F_xn + 2048],
                    FA[:, F_C2:F_C2 + 2048], op=ALU.mult))
                c.op("vector", lambda E: E.tensor_tensor(
                    FA[:, F_xn:F_xn + 2048], FA[:, F_xn:F_xn + 2048],
                    FA[:, F_sb:F_sb + 2048], op=ALU.add),
                    "sv" if DEBUG_BUFS else None)
                if DEBUG_BUFS:
                    c.w("sync", "sv", n["sv"])
                    hd_dma = c.dma("sync", lambda E, qs=qs: E.dma_start(
                        out=bass.AP(h2dbg, qs * 128 * D, [[D, 128], [1, D]]),
                        in_=FA[:, F_xn:F_xn + 2048]), "sx")
                    c.w("vector", "sx", hd_dma)
                c.op("vector", lambda E: E.tensor_reduce(
                    cols[:, 145:146], FA[:, F_xn:F_xn + 2048], axis=AX.X,
                    op=ALU.max, apply_absolute_value=True))
                vd()
                c.op("vector", lambda E: E.tensor_scalar(
                    cols[:, 145:146], cols[:, 145:146], 1e-5, None,
                    op0=ALU.max))
                vd()
                c.op("vector", lambda E: E.tensor_scalar(
                    cols[:, 146:147], cols[:, 145:146], 1.0 / 127.0, None,
                    op0=ALU.mult))
                c.op("vector", lambda E: E.reciprocal(
                    cols[:, 145:146], cols[:, 145:146]))
                vd()
                c.op("vector", lambda E: E.tensor_scalar(
                    cols[:, 145:146], cols[:, 145:146], 127.0, None,
                    op0=ALU.mult))
                vd()
                c.op("vector", lambda E: E.tensor_scalar(
                    FA[:, F_xn:F_xn + 2048], FA[:, F_xn:F_xn + 2048],
                    cols[:, 145:146], None, op0=ALU.mult))
                c.op("vector", lambda E: E.tensor_scalar(
                    FA[:, F_q1:F_q1 + 2048], FA[:, F_xn:F_xn + 2048],
                    M32, -M32, op0=ALU.add, op1=ALU.add))
                c.op("vector", lambda E: E.tensor_scalar(
                    FA[:, F_q2:F_q2 + 2048], FA[:, F_xn:F_xn + 2048],
                    M64, -M64, op0=ALU.add, op1=ALU.add))
                c.op("vector", lambda E: E.tensor_tensor(
                    FA[:, F_q2:F_q2 + 2048], FA[:, F_q2:F_q2 + 2048],
                    FA[:, F_q1:F_q1 + 2048], op=ALU.subtract))
                c.op("vector", lambda E: E.scalar_tensor_tensor(
                    FA[:, F_q1:F_q1 + 2048], FA[:, F_q2:F_q2 + 2048],
                    w_col[:], FA[:, F_q1:F_q1 + 2048],
                    op0=ALU.mult, op1=ALU.add))
                c.op("vector", lambda E: E.tensor_scalar(
                    FA[:, F_q1:F_q1 + 2048], FA[:, F_q1:F_q1 + 2048],
                    -128.0, 127.0, op0=ALU.max, op1=ALU.min))
                c.op("vector", lambda E: E.tensor_scalar(
                    AB[:, A_xq:A_xq + 2048], FA[:, F_q1:F_q1 + 2048],
                    cols[:, 146:147], None, op0=ALU.mult), "sv")
                xq_done = n["sv"]
                if DEBUG_BUFS:
                    c.w("sync", "sv", xq_done)
                    xq_dma = c.dma("sync", lambda E, qs=qs: E.dma_start(
                        out=bass.AP(xqdbg, qs * 128 * D, [[D, 128], [1, D]]),
                        in_=AB[:, A_xq:A_xq + 2048]), "sx")
                for dc in range(NC_D):
                    c.w("tensor", "sv", xq_done if dc == 0 else 0)
                    c.w("tensor", "sv",
                        n["sv"] if (qs * 16 + dc) >= 2 else 0)
                    c.op("tensor", lambda E, dc=dc: E.transpose(
                        PTT[dc % 2][:, 0:128],
                        AB[:, A_xq + dc * 128:A_xq + (dc + 1) * 128],
                        idn[:]), "sp")
                    tp = n["sp"]
                    c.w("vector", "sp", tp)
                    c.op("vector", lambda E, dc=dc, qs=qs: E.tensor_copy(
                        AB[:, A_xqT + dc * 1024 + qs * 128:
                           A_xqT + dc * 1024 + (qs + 1) * 128],
                        PTT[dc % 2][:, 0:128]), "sv")
            c.barrier()

            F_nw = 0
            F_gb = 5632
            F_g = 7680
            F_nwrow = 8192
            F_gbrow = 13824
            A_w13 = 0
            A_r = 32768
            A_t = 34816
            A_gsq = 35840

            c.dma("sync", lambda E: E.dma_start(
                out=FA[0:1, F_gbrow:F_gbrow + D],
                in_=bass.AP(modbuf, 3 * D, [[9 * D, 1], [1, D]])), "sx")
            xg1 = n["sx"]
            c.dma("sync", lambda E: E.dma_start(
                out=FA[0:1, F_nwrow:F_nwrow + HID],
                in_=bass.AP(normw, 0, [[HID, 1], [1, HID]])), "sx")
            xg2 = n["sx"]
            for j in range(4):
                c.w("tensor", "sx", xg1)
                c.w("tensor", "sv", n["sv"] if j >= 1 else 0)
                c.op("tensor", lambda E, j=j: E.matmul(
                    p4[:, 0:512], ones_rf[:],
                    FA[0:1, F_gbrow + j * 512:F_gbrow + (j + 1) * 512],
                    start=True, stop=True, skip_group_check=True), "sp")
                bm = n["sp"]
                c.w("vector", "sp", bm)
                c.op("vector", lambda E, j=j: E.tensor_copy(
                    FA[:, F_gb + j * 512:F_gb + (j + 1) * 512], p4[:, 0:512]),
                    "sv")
            for j in range(11):
                c.w("tensor", "sx", xg2)
                c.w("tensor", "sv", n["sv"])
                c.op("tensor", lambda E, j=j: E.matmul(
                    p4[:, 0:512], ones_rf[:],
                    FA[0:1, F_nwrow + j * 512:F_nwrow + (j + 1) * 512],
                    start=True, stop=True, skip_group_check=True), "sp")
                bm = n["sp"]
                c.w("vector", "sp", bm)
                c.op("vector", lambda E, j=j: E.tensor_copy(
                    FA[:, F_nw + j * 512:F_nw + (j + 1) * 512], p4[:, 0:512]),
                    "sv")
            vd()
            c.op("vector", lambda E: E.memset(cols[:, 112:128], 0.0), "sv")
            vd()
            v_z2 = n["sv"]

            PF3 = [p0, p1, p4]
            PS3 = [p2, p3, pOb[:, 512:1024]]
            w13_loads = {}
            g_dones = {}
            r2s = {}
            t_dmas = {}
            for hp in range(11):
                wpar = hp % 2
                c.w("gpsimd", "sp", w13_loads.get(hp - 2, 0))
                wl1 = c.dma("gpsimd", lambda E, hp=hp, wpar=wpar: E.dma_start(
                    out=AB[:, A_w13 + wpar * 16384:A_w13 + wpar * 16384 + 8192],
                    in_=bass.AP(w13T, hp * 512,
                                [[2 * HID, 128], [128 * 2 * HID, 16], [1, 512]])),
                    "sw")
                wl3 = c.dma("gpsimd", lambda E, hp=hp, wpar=wpar: E.dma_start(
                    out=AB[:, A_w13 + wpar * 16384 + 8192:
                           A_w13 + wpar * 16384 + 16384],
                    in_=bass.AP(w13T, HID + hp * 512,
                                [[2 * HID, 128], [128 * 2 * HID, 16], [1, 512]])),
                    "sw")
                for qs in range(8):
                    it = hp * 8 + qs
                    pi = it % 3
                    c.w("tensor", "sw", wl3)
                    c.w("tensor", "sv", v_z2 if it < 3 else 0)
                    c.w("tensor", "sv", g_dones.get(it - 3, 0))
                    c.w("tensor", "sc", r2s.get(it - 3, 0))
                    for dc in range(NC_D):
                        c.op("tensor", lambda E, qs=qs, dc=dc, wpar=wpar, pi=pi: E.matmul(
                            PF3[pi][:, 0:512],
                            AB[:, A_xqT + dc * 1024 + qs * 128:
                               A_xqT + dc * 1024 + (qs + 1) * 128],
                            AB[:, A_w13 + wpar * 16384 + dc * 512:
                               A_w13 + wpar * 16384 + (dc + 1) * 512],
                            start=(dc == 0), stop=(dc == NC_D - 1),
                            skip_group_check=True))
                        c.op("tensor", lambda E, qs=qs, dc=dc, wpar=wpar, pi=pi: E.matmul(
                            PS3[pi][:, 0:512],
                            AB[:, A_xqT + dc * 1024 + qs * 128:
                               A_xqT + dc * 1024 + (qs + 1) * 128],
                            AB[:, A_w13 + wpar * 16384 + 8192 + dc * 512:
                               A_w13 + wpar * 16384 + 8192 + (dc + 1) * 512],
                            start=(dc == 0), stop=(dc == NC_D - 1),
                            skip_group_check=True),
                            "sp" if dc == NC_D - 1 else None)
                    x13 = n["sp"]
                    c.w("scalar", "sp", x13)
                    c.w("scalar", "sv", g_dones.get(it - 2, 0))
                    c.op("scalar", lambda E, qs=qs, pi=pi: E.activation(
                        out=AB[:, A_r + (qs % 2) * 1024:
                               A_r + (qs % 2) * 1024 + 512],
                        in_=PF3[pi][:, 0:512], func=AF.Relu))
                    c.op("scalar", lambda E: E.drain())
                    c.op("scalar", lambda E, qs=qs: E.activation(
                        out=AB[:, A_r + (qs % 2) * 1024 + 512:
                               A_r + (qs % 2) * 1024 + 1024],
                        in_=AB[:, A_r + (qs % 2) * 1024:
                               A_r + (qs % 2) * 1024 + 512],
                        func=AF.Square), "sc")
                    r2 = n["sc"]
                    r2s[it] = r2
                    c.w("vector", "sc", r2)
                    c.op("vector", lambda E, qs=qs, pi=pi: E.tensor_tensor(
                        FA[:, F_g:F_g + 512],
                        AB[:, A_r + (qs % 2) * 1024 + 512:
                           A_r + (qs % 2) * 1024 + 1024],
                        PS3[pi][:, 0:512], op=ALU.mult), "sv")
                    g_done = n["sv"]
                    g_dones[it] = g_done
                    c.w("scalar", "sv", g_done)
                    c.op("scalar", lambda E: E.activation(
                        out=AB[:, A_gsq:A_gsq + 512], in_=FA[:, F_g:F_g + 512],
                        func=AF.Square, accum_out=cols[:, 147:148]))
                    c.op("scalar", lambda E: E.drain(), "sc")
                    gss = n["sc"]
                    c.w("vector", "sc", gss)
                    c.op("vector", lambda E, qs=qs: E.tensor_tensor(
                        cols[:, 112 + qs:113 + qs], cols[:, 112 + qs:113 + qs],
                        cols[:, 147:148], op=ALU.add))
                    c.w("vector", "sx", t_dmas.get(it - 2, 0))
                    c.op("vector", lambda E, hp=hp, qs=qs: E.tensor_tensor(
                        AB[:, A_t + (qs % 2) * 512:A_t + (qs % 2) * 512 + 512],
                        FA[:, F_g:F_g + 512],
                        FA[:, F_nw + hp * 512:F_nw + (hp + 1) * 512],
                        op=ALU.mult), "sv")
                    t_done = n["sv"]
                    c.op("vector", lambda E, qs=qs: E.tensor_reduce(
                        cols[:, 148:149],
                        AB[:, A_t + (qs % 2) * 512:A_t + (qs % 2) * 512 + 512],
                        axis=AX.X, op=ALU.max, apply_absolute_value=True))
                    vd()
                    c.op("vector", lambda E, qs=qs: E.tensor_tensor(
                        cols[:, 120 + qs:121 + qs], cols[:, 120 + qs:121 + qs],
                        cols[:, 148:149], op=ALU.max), "sv")
                    c.w("sync", "sv", t_done)
                    t_dmas[it] = c.dma("sync", lambda E, hp=hp, qs=qs: E.dma_start(
                        out=bass.AP(gbuf, qs * 128 * HID + hp * 512,
                                    [[HID, 128], [1, 512]]),
                        in_=AB[:, A_t + (qs % 2) * 512:
                               A_t + (qs % 2) * 512 + 512]), "sx")
                w13_loads[hp] = n["sp"]
            c.barrier()

            vd()
            c.op("vector", lambda E: E.tensor_scalar(
                cols[:, 112:120], cols[:, 112:120], 1.0 / HID, EPS,
                op0=ALU.mult, op1=ALU.add), "sv")
            vd()
            vq = n["sv"]
            c.w("scalar", "sv", vq)
            c.op("scalar", lambda E: E.activation(
                out=cols[:, 112:120], in_=cols[:, 112:120], func=AF.Sqrt), "sc")
            sq2 = n["sc"]
            c.w("vector", "sc", sq2)
            c.op("vector", lambda E: E.reciprocal(
                cols[:, 112:120], cols[:, 112:120]))
            vd()
            c.op("vector", lambda E: E.tensor_tensor(
                cols[:, 120:128], cols[:, 112:120], cols[:, 120:128],
                op=ALU.mult))
            vd()
            c.op("vector", lambda E: E.tensor_scalar(
                cols[:, 120:128], cols[:, 120:128], 1e-5, None, op0=ALU.max))
            vd()
            c.op("vector", lambda E: E.tensor_scalar(
                cols[:, 136:144], cols[:, 120:128], 1.0 / 127.0, None,
                op0=ALU.mult))
            c.op("vector", lambda E: E.reciprocal(
                cols[:, 120:128], cols[:, 120:128]))
            vd()
            c.op("vector", lambda E: E.tensor_scalar(
                cols[:, 120:128], cols[:, 120:128], 127.0, None,
                op0=ALU.mult))
            vd()
            c.op("vector", lambda E: E.tensor_tensor(
                cols[:, 128:136], cols[:, 112:120], cols[:, 120:128],
                op=ALU.mult), "sv")
            vd()
            v_q2c = n["sv"]

            A_iqT = 16384
            A_tin = 22528
            A_iq = 23552
            A_w2 = 24576
            F_y = 0
            F_xps2 = 2048
            F_o1 = 4096

            iq_st = {}
            w2mms = {}

            def quant2_and_iqT(qs):
                iqb = A_iqT if qs % 2 == 0 else 0
                c.w("vector", "sx", iq_st.get(qs - 2, 0))
                c.w("vector", "sp", w2mms.get((0, qs - 2), 0))
                for hp in range(11):
                    c.w("sync", "sv", n["sv"] if hp >= 2 else v_q2c)
                    tl = c.dma("sync", lambda E, qs=qs, hp=hp: E.dma_start(
                        out=AB[:, A_tin + (hp % 2) * 512:
                               A_tin + (hp % 2) * 512 + 512],
                        in_=bass.AP(gbuf, qs * 128 * HID + hp * 512,
                                    [[HID, 128], [1, 512]])), "sx")
                    c.w("vector", "sx", tl)
                    c.op("vector", lambda E, qs=qs, hp=hp: E.tensor_scalar(
                        FA[:, F_y:F_y + 512],
                        AB[:, A_tin + (hp % 2) * 512:
                           A_tin + (hp % 2) * 512 + 512],
                        cols[:, 128 + qs:129 + qs], None, op0=ALU.mult))
                    c.op("vector", lambda E: E.tensor_scalar(
                        FA[:, F_y + 512:F_y + 1024], FA[:, F_y:F_y + 512],
                        M32, -M32, op0=ALU.add, op1=ALU.add))
                    c.op("vector", lambda E: E.tensor_scalar(
                        FA[:, F_y + 1024:F_y + 1536], FA[:, F_y:F_y + 512],
                        M64, -M64, op0=ALU.add, op1=ALU.add))
                    c.op("vector", lambda E: E.tensor_tensor(
                        FA[:, F_y + 1024:F_y + 1536],
                        FA[:, F_y + 1024:F_y + 1536],
                        FA[:, F_y + 512:F_y + 1024], op=ALU.subtract))
                    c.op("vector", lambda E: E.scalar_tensor_tensor(
                        FA[:, F_y + 512:F_y + 1024],
                        FA[:, F_y + 1024:F_y + 1536], w_col[:],
                        FA[:, F_y + 512:F_y + 1024],
                        op0=ALU.mult, op1=ALU.add))
                    c.op("vector", lambda E: E.tensor_scalar(
                        FA[:, F_y + 512:F_y + 1024], FA[:, F_y + 512:F_y + 1024],
                        -128.0, 127.0, op0=ALU.max, op1=ALU.min))
                    c.op("vector", lambda E, qs=qs, hp=hp: E.tensor_scalar(
                        AB[:, A_iq + (hp % 2) * 512:A_iq + (hp % 2) * 512 + 512],
                        FA[:, F_y + 512:F_y + 1024],
                        cols[:, 136 + qs:137 + qs], None, op0=ALU.mult), "sv")
                    iq_done = n["sv"]
                    for j in range(4):
                        hc = hp * 4 + j
                        c.w("tensor", "sv", iq_done if j == 0 else 0)
                        c.w("tensor", "sv", n["sv"] if hc >= 2 else 0)
                        c.op("tensor", lambda E, hp=hp, j=j: E.transpose(
                            PTT[j % 2][:, 0:128],
                            AB[:, A_iq + (hp % 2) * 512 + j * 128:
                               A_iq + (hp % 2) * 512 + (j + 1) * 128],
                            idn[:]), "sp")
                        tp = n["sp"]
                        c.w("vector", "sp", tp)
                        c.op("vector", lambda E, hc=hc, iqb=iqb: E.tensor_copy(
                            AB[:, iqb + hc * 128:iqb + (hc + 1) * 128],
                            PTT[hc % 2][:, 0:128]), "sv")
                return n["sv"]

            for dg in range(4):
                c.w("gpsimd", "sp", n["sp"] if dg > 0 else 0)
                c.dma("gpsimd", lambda E, dg=dg: E.dma_start(
                    out=AB[:, A_w2:A_w2 + 11264],
                    in_=bass.AP(w2T, dg * 512,
                                [[D, 128], [128 * D, 22], [1, 512]])), "sw")
                wl = c.dma("gpsimd", lambda E, dg=dg: E.dma_start(
                    out=AB[:, A_w2 + 11264:A_w2 + 22528],
                    in_=bass.AP(w2T, 22 * 128 * D + dg * 512,
                                [[D, 128], [128 * D, 22], [1, 512]])), "sw")
                for qs in range(8):
                    par = qs % 2
                    iqb = A_iqT if qs % 2 == 0 else 0
                    if dg == 0:
                        iqT_done = quant2_and_iqT(qs)
                        c.w("sync", "sv", iqT_done)
                        iq_st[qs] = c.dma("sync", lambda E, qs=qs, iqb=iqb: E.dma_start(
                            out=bass.AP(iqbuf, qs * 128,
                                        [[OWN, 128], [128 * OWN, 44], [1, 128]]),
                            in_=AB[:, iqb:iqb + 5632]), "sx")
                        c.w("tensor", "sv", iqT_done)
                    else:
                        c.w("sync", "sp",
                            w2mms.get((dg, qs - 2),
                                      w2mms.get((dg - 1, qs + 6), 0)))
                        c.dma("sync", lambda E, qs=qs, iqb=iqb: E.dma_start(
                            out=AB[:, iqb:iqb + 5632],
                            in_=bass.AP(iqbuf, qs * 128,
                                        [[OWN, 128], [128 * OWN, 44], [1, 128]])),
                            "sx")
                        c.w("tensor", "sx", n["sx"])
                    c.w("tensor", "sw", wl)
                    c.w("tensor", "sv",
                        n["sv"] if (dg * 8 + qs) >= 2 or dg > 0 else 0)
                    for hc in range(NHC):
                        c.op("tensor", lambda E, hc=hc, par=par, iqb=iqb: E.matmul(
                            PF[par][:, 0:512],
                            AB[:, iqb + hc * 128:iqb + (hc + 1) * 128],
                            AB[:, A_w2 + hc * 512:A_w2 + (hc + 1) * 512],
                            start=(hc == 0), stop=(hc == NHC - 1),
                            skip_group_check=True),
                            "sp" if hc == NHC - 1 else None)
                    w2mm = n["sp"]
                    w2mms[(dg, qs)] = w2mm
                    xl = c.dma("sync", lambda E, qs=qs, dg=dg: E.dma_start(
                        out=FA[:, F_xps2 + (qs % 2) * 512:
                               F_xps2 + (qs % 2) * 512 + 512],
                        in_=bass.AP(xpbuf, qs * 128 * D + dg * 512,
                                    [[D, 128], [1, 512]])), "sx")
                    c.w("vector", "sp", w2mm)
                    c.w("vector", "sx", xl)
                    c.w("vector", "so", max(0, n["so"] - 32))
                    c.op("vector", lambda E, dg=dg, par=par, qs=qs: E.tensor_tensor(
                        FA[:, F_o1 + (qs % 2) * 512:F_o1 + (qs % 2) * 512 + 512],
                        PF[par][:, 0:512],
                        FA[:, F_gb + dg * 512:F_gb + (dg + 1) * 512],
                        op=ALU.mult))
                    c.op("vector", lambda E, qs=qs: E.tensor_tensor(
                        FA[:, F_o1 + (qs % 2) * 512:F_o1 + (qs % 2) * 512 + 512],
                        FA[:, F_o1 + (qs % 2) * 512:F_o1 + (qs % 2) * 512 + 512],
                        FA[:, F_xps2 + (qs % 2) * 512:
                           F_xps2 + (qs % 2) * 512 + 512],
                        op=ALU.add), "sv")
                    o_done = n["sv"]
                    c.w("sync", "sv", o_done)
                    c.dma("sync", lambda E, qs=qs, dg=dg: E.dma_start(
                        out=bass.AP(out, qs * 128 * D + dg * 512,
                                    [[D, 128], [1, 512]]),
                        in_=FA[:, F_o1 + (qs % 2) * 512:
                               F_o1 + (qs % 2) * 512 + 512]), "so")

            for e in ENGS:
                c.w(e, "so", n["so"])

        @block.sync
        def _(E):
            emit(Ctx("sync", E, sems))

        @block.gpsimd
        def _(E):
            emit(Ctx("gpsimd", E, sems))

        @block.tensor
        def _(E):
            emit(Ctx("tensor", E, sems))

        @block.vector
        def _(E):
            emit(Ctx("vector", E, sems))

        @block.scalar
        def _(E):
            emit(Ctx("scalar", E, sems))

    return nc


_NC_CACHE = None
_LAST_RES = None


def kernel(L0_attn_x, L0_attn_wq_codes, L0_attn_wq_scales, L0_attn_bq,
           L0_attn_wk_codes, L0_attn_wk_scales, L0_attn_bk,
           L0_attn_wv_codes, L0_attn_wv_scales, L0_attn_bv,
           L0_attn_wo_codes, L0_attn_wo_scales, L0_attn_bo,
           L0_ffn_w13, L0_ffn_w2, L0_ffn_norm_w, L0_c, L0_wmod):
    global _NC_CACHE
    f = np.float32
    x = np.ascontiguousarray(np.asarray(L0_attn_x, f).reshape(S, D))
    rep = lambda s: np.ascontiguousarray(np.repeat(np.asarray(s, f), 32, axis=0))
    shared = {
        "wq_c": np.ascontiguousarray(np.asarray(L0_attn_wq_codes, f)),
        "wk_c": np.ascontiguousarray(np.asarray(L0_attn_wk_codes, f)),
        "wv_c": np.ascontiguousarray(np.asarray(L0_attn_wv_codes, f)),
        "wo_c": np.ascontiguousarray(np.asarray(L0_attn_wo_codes, f)),
        "wq_s": rep(L0_attn_wq_scales),
        "wk_s": rep(L0_attn_wk_scales),
        "wv_s": rep(L0_attn_wv_scales),
        "wo_s": rep(L0_attn_wo_scales),
        "bq": np.ascontiguousarray(np.asarray(L0_attn_bq, f)),
        "bk": np.ascontiguousarray(np.asarray(L0_attn_bk, f)),
        "bv": np.ascontiguousarray(np.asarray(L0_attn_bv, f)),
        "bo": np.ascontiguousarray(np.asarray(L0_attn_bo, f)),
        "w13T": np.ascontiguousarray(np.asarray(L0_ffn_w13, f).T),
        "w2T": np.ascontiguousarray(np.asarray(L0_ffn_w2, f).T),
        "normw": np.ascontiguousarray(np.asarray(L0_ffn_norm_w, f)),
        "cvec": np.ascontiguousarray(np.asarray(L0_c, f).reshape(1, D)),
        "wmod": np.ascontiguousarray(np.asarray(L0_wmod, f)),
    }
    in_maps = []
    for cix in range(8):
        order = np.r_[cix * OWN:S, 0:cix * OWN]
        m = dict(shared)
        m["xT"] = np.ascontiguousarray(x[order].T)
        m["x_own"] = np.ascontiguousarray(x[cix * OWN:(cix + 1) * OWN])
        in_maps.append(m)

    if _NC_CACHE is None:
        _NC_CACHE = build_nc()
    res = run_bass_kernel_spmd(_NC_CACHE, in_maps, core_ids=list(range(8)))
    global _LAST_RES
    _LAST_RES = res
    o = np.empty((S, D), f)
    for cix in range(8):
        o[cix * OWN:(cix + 1) * OWN] = res.results[cix]["out"]
    return o.reshape(1, S, D)
